# version 04
# baseline (speedup 1.0000x reference)
import numpy as np
from contextlib import ExitStack
import concourse.bass as bass
import concourse.mybir as mybir
from concourse.bass_utils import run_bass_kernel_spmd

F32 = mybir.dt.float32
BF16 = mybir.dt.bfloat16
AF = mybir.ActivationFunctionType
ALU = mybir.AluOpType
AX = mybir.AxisListType

S = 4096
D = 1024
NH = 16
HD = 64
DFF = 4096
LN_EPS = 1e-5
ATT_EPS = 1e-6

ENGINES = ("sync", "tensor", "vector", "scalar", "gpsimd")

_FUSED = {"on": False, "nc": None, "prog": None, "ns": "", "dram": {}, "C": None}


def _mknc():
    if _FUSED["on"]:
        return _FUSED["nc"]
    return bass.Bass("TRN2", target_bir_lowering=False)


def _mkprog():
    if _FUSED["on"]:
        return _FUSED["prog"]
    return Prog()


def _dram(nc, name, shape, dt, kind):
    if _FUSED["on"]:
        if name in _FUSED["dram"]:
            return _FUSED["dram"][name]
        return nc.dram_tensor(_FUSED["ns"] + name, shape, dt, kind="Internal").ap()
    return nc.dram_tensor(name, shape, dt, kind=kind).ap()


def _nm(name):
    return _FUSED["ns"] + name if _FUSED["on"] else name


def _fin(nc, prog, stack):
    if _FUSED["on"]:
        prog.barrier()
    else:
        realize(nc, prog, stack)


class Prog:
    def __init__(self):
        self.steps = []
        self.counts = {}

    def step(self, eng, fn, waits=(), inc=None, n=1):
        tok = None
        if inc is not None:
            self.counts[inc] = self.counts.get(inc, 0) + n
            tok = (inc, self.counts[inc])
        self.steps.append((eng, fn, [w for w in waits if w is not None], inc, n))
        return tok

    def sync(self, fn, waits=(), inc=None, n=16):
        return self.step("sync", fn, waits, inc, n)

    def all_tok(self, name):
        return (name, self.counts.get(name, 0))

    def cur_tok(self, name):
        return (name, self.counts.get(name, 0))

    def barrier(self):
        waits = [(n, c) for (n, c) in self.counts.items()]
        for eng in ENGINES:
            self.step(eng, lambda e: e.nop(), waits=list(waits))

    def gp(self, fn, waits=(), inc=None, n=1):
        return self.step("gpsimd", fn, waits, inc, n)

    def pe(self, fn, waits=(), inc=None, n=1):
        return self.step("tensor", fn, waits, inc, n)

    def ve(self, fn, waits=(), inc=None, n=1):
        return self.step("vector", fn, waits, inc, n)

    def se(self, fn, waits=(), inc=None, n=1):
        return self.step("scalar", fn, waits, inc, n)


def realize(nc, prog, stack):
    def tail(sy):
        return sy.nop()
    prog.step("sync", tail, waits=[(n, prog.counts[n]) for n in list(prog.counts)])
    sems = {name: stack.enter_context(nc.semaphore(name)) for name in prog.counts}
    block = stack.enter_context(nc.Block())
    for eng_name in ENGINES:
        steps = [s for s in prog.steps if s[0] == eng_name]
        if not steps:
            continue

        def make_body(steps):
            def body(eng):
                seen = {}
                for (_, fn, waits, inc, n) in steps:
                    for (sname, val) in waits:
                        if val == "ALL":
                            val = prog.counts[sname]
                        if seen.get(sname, 0) < val:
                            eng.wait_ge(sems[sname], val)
                            seen[sname] = val
                    instr = fn(eng)
                    if inc is not None:
                        instr.then_inc(sems[inc], n)
            return body

        getattr(block, eng_name)(make_body(steps))


def host_constants():
    ident = np.eye(128, dtype=np.float32)
    m = np.triu(np.ones((128, 128), np.float32))
    mask4 = np.tile(m, (1, 4))
    return {"c_ident": ident, "c_mask4": mask4}


def declare_constants(nc, stack, prog, ld, need=("id_bf", "id32", "mask4",
                                                  "ones_row", "ones_row32",
                                                  "ones_col", "eps_ln")):
    if _FUSED["on"] and _FUSED["C"] is not None:
        return _FUSED["C"]
    out = {"const_toks": []}
    if "id_bf" in need or "id32" in need:
        ident_d = _dram(nc, "c_ident", [128, 128], F32, "ExternalInput")
    if "id_bf" in need:
        id_bf = stack.enter_context(nc.sbuf_tensor("id_bf", [128, 128], BF16))
        out["const_toks"].append(prog.gp(lambda g: g.dma_start(out=id_bf[:], in_=ident_d), inc=ld, n=16))
        out["id_bf"] = id_bf
    if "id32" in need:
        id32 = stack.enter_context(nc.sbuf_tensor("id32", [128, 128], F32))
        out["const_toks"].append(prog.gp(lambda g: g.dma_start(out=id32[:], in_=ident_d), inc=ld, n=16))
        out["id32"] = id32
    if "mask4" in need:
        mask4_d = _dram(nc, "c_mask4", [128, 512], F32, "ExternalInput")
        mask4 = stack.enter_context(nc.sbuf_tensor("mask4", [128, 512], BF16))
        out["const_toks"].append(prog.gp(lambda g: g.dma_start(out=mask4[:], in_=mask4_d), inc=ld, n=16))
        out["mask4"] = mask4
    if "ones_row" in need:
        ones_row = stack.enter_context(nc.sbuf_tensor("ones_row", [1, 512], BF16))
        out["const_toks"].append(prog.gp(lambda g: g.memset(ones_row[:], 1.0), inc="gs", n=1))
        out["ones_row"] = ones_row
    if "ones_row32" in need:
        ones_row32 = stack.enter_context(nc.sbuf_tensor("ones_row32", [1, 128], F32))
        out["const_toks"].append(prog.gp(lambda g: g.memset(ones_row32[:], 1.0), inc="gs", n=1))
        out["ones_row32"] = ones_row32
    if "ones_col" in need:
        ones_col = stack.enter_context(nc.sbuf_tensor("ones_col", [128, 1], BF16))
        out["const_toks"].append(prog.gp(lambda g: g.memset(ones_col[:], 1.0), inc="gs", n=1))
        out["ones_col"] = ones_col
    if "eps_ln" in need:
        eps_ln = stack.enter_context(nc.sbuf_tensor("eps_ln", [128, 1], F32))
        out["const_toks"].append(prog.gp(lambda g: g.memset(eps_ln[:], LN_EPS), inc="gs", n=1))
        out["eps_ln"] = eps_ln
    return out


def emit_ln_tile(prog, x_tile, nchunks, bn6, mv, rs, eps_col, waits):
    tok_bn = None
    for c in range(nchunks):
        for i in range(2):
            last = (c == nchunks - 1 and i == 1)
            tok_bn = prog.ve(lambda v, c=c, i=i: v.bn_stats(
                bn6[:, c * 12 + i * 6: c * 12 + (i + 1) * 6],
                x_tile[:, c * 1024 + i * 512: c * 1024 + (i + 1) * 512]),
                waits=waits if (c == 0 and i == 0) else (),
                inc="vs" if last else None)
    tok_mv = None
    for c in range(nchunks):
        tok_mv = prog.ve(lambda v, c=c: v.bn_aggr(
            mv[:, c * 2:(c + 1) * 2], bn6[:, c * 12:(c + 1) * 12]),
            waits=[tok_bn] if c == 0 else (),
            inc="vs" if c == nchunks - 1 else None)
    tok_l = prog.se(lambda s: s.activation(
        rs[:], mv[:].rearrange("p (c two) -> p c two", two=2)[:, :, 1:2],
        AF.Ln, bias=eps_col[:]), waits=[tok_mv], inc="ss")
    tok_rs = prog.se(lambda s: s.activation(rs[:], rs[:], AF.Exp, scale=-0.5),
                     waits=[tok_l], inc="ss")
    return tok_mv, tok_rs


def load_weight_T(prog, dram_ap, sb_view, kdim, ndim, maxld=None):
    kb = kdim // 128
    nsplit = 1 if ndim <= 2048 else (ndim + 2047) // 2048
    step = ndim // nsplit
    tok = None
    for j in range(nsplit):
        tok = prog.gp(lambda g, j=j: g.dma_start(
            out=sb_view.rearrange("p (k n) -> p k n", k=kb)[:, :, j * step:(j + 1) * step],
            in_=dram_ap.rearrange("(k p) n -> p k n", p=128)[:, :, j * step:(j + 1) * step]),
            inc="ld", n=16)
    return tok


def x_tile_in(ap, i, nchunk=4):
    return ap.rearrange("(t c p) d -> t p c d", p=128, c=nchunk)[i]


def build_phase_init():
    nc = _mknc()
    x_in = _dram(nc, "x_in", [S, D], F32, "ExternalInput")
    x_out = _dram(nc, "x_out", [S, D], F32, "ExternalOutput")
    prog = _mkprog()
    with ExitStack() as stack:
        sb = lambda n, s, d: stack.enter_context(nc.sbuf_tensor(_nm(n), s, d))
        xt = [sb(f"xt{i}", [128, 4 * D], F32) for i in range(2)]
        ot = [sb(f"ot{i}", [128, 4 * D], F32) for i in range(2)]
        bn6 = sb("bn6", [128, 48], F32)
        mv = sb("mv", [128, 8], F32)
        rs = sb("rs", [128, 4], F32)
        eps_ln = sb("eps_ln", [128, 1], F32)
        eps_tok = prog.gp(lambda g: g.memset(eps_ln[:], LN_EPS), inc="gs", n=1)
        comp, stores, loads = [], [], []
        for i in range(8):
            b = i % 2
            loads.append(prog.sync(
                lambda sy, i=i, b=b: sy.dma_start(
                    out=xt[b][:].rearrange("p (c d) -> p c d", c=4),
                    in_=x_tile_in(x_in, i)),
                waits=[comp[i - 2]] if i >= 2 else [], inc=f"lx{b}"))
            tok_mv, tok_rs = emit_ln_tile(prog, xt[b], 4, bn6, mv, rs, eps_ln,
                                          waits=[loads[i], eps_tok])
            ctok = None
            for c in range(4):
                ctok = prog.ve(
                    lambda v, b=b, c=c: v.tensor_scalar(
                        ot[b][:, c * D:(c + 1) * D], xt[b][:, c * D:(c + 1) * D],
                        mv[:, 2 * c:2 * c + 1], rs[:, c:c + 1],
                        ALU.subtract, ALU.mult),
                    waits=[tok_mv, tok_rs] + ([stores[i - 2]] if i >= 2 else [])
                    if c == 0 else (), inc="vs" if c == 3 else None)
            comp.append(ctok)
            stores.append(prog.sync(
                lambda sy, i=i, b=b: sy.dma_start(
                    out=x_tile_in(x_out, i),
                    in_=ot[b][:].rearrange("p (c d) -> p c d", c=4)),
                waits=[comp[i]], inc=f"st{b}"))
        prog.sync(lambda sy: sy.nop(), waits=[stores[-1]])
        _fin(nc, prog, stack)
    return nc


def build_phase_l0_att(layer, NT=None, debug_stage=None):
    TT = 512
    if NT is None:
        NT = S // TT
    nc = _mknc()
    x_in = _dram(nc, "x_in", [S, D], F32, "ExternalInput")
    x_out = _dram(nc, "x_out", [S, D], F32, "ExternalOutput")
    Wq = _dram(nc, "Wq", [D, D], F32, "ExternalInput")
    Wk = _dram(nc, "Wk", [D, D], F32, "ExternalInput")
    Wv = _dram(nc, "Wv", [D, D], F32, "ExternalInput")
    Wo = _dram(nc, "Wo", [D, D], F32, "ExternalInput")
    bqd = _dram(nc, "bq", [1, D], F32, "ExternalInput")
    bkd = _dram(nc, "bk", [1, D], F32, "ExternalInput")
    bvd = _dram(nc, "bv", [1, D], F32, "ExternalInput")
    bod = _dram(nc, "bo", [1, D], F32, "ExternalInput")

    prog = _mkprog()
    with ExitStack() as stack:
        sb = lambda n, s, d: stack.enter_context(nc.sbuf_tensor(_nm(n), s, d))
        psal = lambda n, s, d: stack.enter_context(nc.psum_tensor(_nm(n), s, d))
        C = declare_constants(nc, stack, prog, "ld",
                              need=("id_bf", "mask4", "ones_row"))
        id_bf, mask4 = C["id_bf"], C["mask4"]
        ones_row = C["ones_row"]

        Wq_sb = sb("Wq_sb", [128, 8 * D], BF16)
        Wk_sb = sb("Wk_sb", [128, 8 * D], BF16)
        Wv_sb = sb("Wv_sb", [128, 8 * D], BF16)
        Wo_sb = sb("Wo_sb", [128, 8 * D], BF16)
        bq_sb = sb("bq_sb", [1, D], BF16)
        bk_sb = sb("bk_sb", [1, D], BF16)
        bv_sb = sb("bv_sb", [1, D], BF16)
        bo_sb = sb("bo_sb", [1, D], BF16)

        wq_t = load_weight_T(prog, Wq, Wq_sb[:], D, D)
        wk_t = load_weight_T(prog, Wk, Wk_sb[:], D, D)
        wv_t = load_weight_T(prog, Wv, Wv_sb[:], D, D)
        wo_t = load_weight_T(prog, Wo, Wo_sb[:], D, D)
        bq_t = prog.gp(lambda g: g.dma_start(out=bq_sb[:], in_=bqd), inc="ld", n=16)
        bk_t = prog.gp(lambda g: g.dma_start(out=bk_sb[:], in_=bkd), inc="ld", n=16)
        bv_t = prog.gp(lambda g: g.dma_start(out=bv_sb[:], in_=bvd), inc="ld", n=16)
        bo_t = prog.gp(lambda g: g.dma_start(out=bo_sb[:], in_=bod), inc="ld", n=16)

        xt32 = [sb(f"xt32_{i}", [128, 4 * D], F32) for i in range(2)]
        xo32 = [sb("xo32_0", [128, 4 * D], F32)] * 2
        xb = sb("xb", [128, 4 * D], BF16)
        xT = sb("xT", [128, 8 * TT], BF16)
        qT = sb("qT", [128, 8 * TT], BF16)
        kT = sb("kT", [128, 8 * TT], BF16)
        kphi = sb("kphi", [128, 4 * D], BF16)
        v_aug = sb("v_aug", [128, 4 * 1040], BF16)
        st_sb = sb("st_sb", [128, 520], BF16)
        st32 = [sb(f"st32_{j}", [128, 520], F32) for j in range(2)]
        sT_bf = sb("sT_bf", [128, 2048], BF16)
        att_sb = sb("att_sb", [128, 1024], BF16)
        attT_sb = sb("attT_sb", [128, 1024], BF16)
        dd = sb("dd", [128, 16], F32)
        dr = sb("dr", [128, 16], F32)
        phi1 = [sb(f"phi1_{j}", [128, 512], F32) for j in range(2)]
        phi2 = [sb(f"phi2_{j}", [128, 512], BF16) for j in range(2)]

        sp = [psal(f"sp{j}", [128, 512], F32) for j in range(2)]
        pp = [psal(f"pp{j}", [128, 512], F32) for j in range(2)]
        npp = [psal(f"np{j}", [128, 512], F32) for j in range(2)]
        tp = [psal(f"tp{j}", [128, 1024], BF16) for j in range(2)]

        ones_aug = prog.gp(lambda g: g.memset(
            v_aug[:].rearrange("p (c h x) -> p (c h) x", c=4, h=16)[:, :, 64:65], 1.0),
            inc="gs", n=1)
        st0_tok = prog.gp(lambda g: g.memset(st32[1][:], 0.0), inc="gs", n=1)

        consts = C["const_toks"]
        tp_tok = [None, None]
        tp_rr = [0]

        def transpose_block(src_view_fn, dst_view_fn, nblocks, waits, evac_fn=None):
            last = None
            for base in range(0, nblocks, 8):
                cnt = min(8, nblocks - base)
                tb = tp_rr[0] % 2
                tp_rr[0] += 1
                w = list(waits) + ([tp_tok[tb]] if tp_tok[tb] else [])
                for j in range(cnt):
                    tok_tr = prog.pe(lambda t, tb=tb, j=j, base=base: t.transpose(
                        tp[tb][:, j * 128:(j + 1) * 128], src_view_fn(base + j), id_bf[:]),
                        waits=w if j == 0 else (), inc="ps" if j == cnt - 1 else None)
                for j in range(cnt):
                    last = prog.se(lambda s, tb=tb, j=j, base=base: s.copy(
                        dst_view_fn(base + j), tp[tb][:, j * 128:(j + 1) * 128]),
                        waits=[tok_tr] if j == 0 else (), inc="ss" if j == cnt - 1 else None)
                tp_tok[tb] = last
                waits = ()
            return last

        stores = []
        xt_free = [None, None]
        xo_free = [None, None]
        last_m4 = None
        last_sadd = None
        attT_free = None
        prev_pp_user = None
        for i in range(NT):
            b = i % 2
            tok_x = prog.sync(
                lambda sy, i=i, b=b: sy.dma_start(
                    out=xt32[b][:].rearrange("p (c d) -> p c d", c=4),
                    in_=x_tile_in(x_in, i)),
                waits=[xt_free[b]], inc=f"lx{b}")
            tok_xb = prog.ve(lambda v, b=b: v.tensor_copy(xb[:], xt32[b][:]),
                             waits=[tok_x], inc="vs")
            if debug_stage == "load":
                dtok = prog.sync(lambda sy, i=i, b=b: sy.dma_start(
                    out=x_tile_in(x_out, i),
                    in_=xt32[b][:].rearrange("p (c d) -> p c d", c=4)),
                    waits=[tok_xb], inc=f"st{b}")
                stores.append(dtok)
                xt_free[b] = dtok
                xo_free[b] = dtok
                continue
            xT_done = transpose_block(
                lambda j: xb[:, (j % 4) * D + (j // 4) * 128:(j % 4) * D + (j // 4) * 128 + 128],
                lambda j: xT[:, (j // 4) * TT + (j % 4) * 128:(j // 4) * TT + (j % 4) * 128 + 128],
                32, [tok_xb, prog.all_tok('ld')])

            def emit_proj_T(W_sb, b_sb, out_T, wtoks, prev_done=None):
                outs = []
                for db in range(8):
                    pb = db % 2
                    w0 = [xT_done] + wtoks + ([outs[db - 2][0]] if db >= 2 else [prev_done])
                    for kb in range(8):
                        prog.pe(lambda t, pb=pb, db=db, kb=kb: t.matmul(
                            pp[pb][:], W_sb[:, kb * D + db * 128: kb * D + (db + 1) * 128],
                            xT[:, kb * TT:(kb + 1) * TT], start=(kb == 0), stop=False),
                            waits=w0 if kb == 0 else ())
                    tok_mm = prog.pe(lambda t, pb=pb, db=db: t.matmul(
                        pp[pb][:], b_sb[0:1, db * 128:(db + 1) * 128], ones_row[0:1, :],
                        start=False, stop=True), inc="ps")
                    fb = db % 2
                    w1 = [tok_mm] + ([outs[db - 2][1]] if db >= 2 else [])
                    tok_v1 = prog.ve(lambda v, pb=pb, fb=fb: v.tensor_scalar(
                        phi1[fb][:], pp[pb][:], 0.0, None, ALU.min), waits=w1, inc="vs")
                    tok_s1 = prog.se(lambda s, fb=fb: s.activation(
                        phi2[fb][:], phi1[fb][:], AF.Exp), waits=[tok_v1], inc="ss")
                    tok_v2 = prog.ve(lambda v, pb=pb, fb=fb, db=db: v.scalar_tensor_tensor(
                        out_T[:, db * TT:(db + 1) * TT], pp[pb][:], 0.0, phi2[fb][:],
                        ALU.max, ALU.add), waits=[tok_s1], inc="vs")
                    outs.append((tok_v2, tok_s1))
                return outs[-1][0], outs[-2][0]

            qT_done, _ = emit_proj_T(Wq_sb, bq_sb, qT, [prog.all_tok('ld')], prev_pp_user)
            kT_done, _ = emit_proj_T(Wk_sb, bk_sb, kT, [prog.all_tok('ld')], qT_done)

            v_last = []
            for c in range(4):
                for nb in range(2):
                    pb = (c * 2 + nb) % 2
                    w0 = [xT_done, prog.all_tok('ld'), kT_done] + ([v_last[-2]] if len(v_last) >= 2 else [])
                    for kb in range(8):
                        prog.pe(lambda t, pb=pb, c=c, kb=kb, nb=nb: t.matmul(
                            pp[pb][:], xT[:, kb * TT + c * 128: kb * TT + (c + 1) * 128],
                            Wv_sb[:, kb * D + nb * 512: kb * D + (nb + 1) * 512],
                            start=(kb == 0), stop=False), waits=w0 if kb == 0 else ())
                    tok_mm = prog.pe(lambda t, pb=pb, nb=nb: t.matmul(
                        pp[pb][:], ones_row[0:1, 0:128], bv_sb[0:1, nb * 512:(nb + 1) * 512],
                        start=False, stop=True), inc="ps")
                    v_last.append(prog.se(lambda s, pb=pb, c=c, nb=nb: s.copy(
                        v_aug[:].rearrange("p (c h x) -> p c h x", c=4, h=16)
                        [:, c, nb * 8:(nb + 1) * 8, 0:64],
                        pp[pb][:].rearrange("p (h x) -> p h x", x=64)),
                        waits=[tok_mm], inc="ss"))
            v_done = v_last[-1]

            if debug_stage == "proj":
                dtok = prog.gp(lambda g, i=i: g.dma_start(
                    out=x_tile_in(x_out, i),
                    in_=qT[:].rearrange("p (c x) -> p c x", c=4)),
                    waits=[kT_done], inc=f"st{b}", n=16)
                stores.append(dtok)
                xt_free[b] = dtok
                xo_free[b] = dtok
                continue

            kphi_done = transpose_block(
                lambda j: kT[:, (j // 4) * TT + (j % 4) * 128:(j // 4) * TT + (j % 4) * 128 + 128],
                lambda j: kphi[:, (j % 4) * D + (j // 4) * 128:(j % 4) * D + (j // 4) * 128 + 128],
                32, [kT_done, v_done])

            if debug_stage == "kphi":
                dtok = prog.gp(lambda g, i=i: g.dma_start(
                    out=x_tile_in(x_out, i),
                    in_=kphi[:].rearrange("p (c x) -> p c x", c=4)),
                    waits=[kphi_done], inc=f"st{b}", n=16)
                stores.append(dtok)
                xt_free[b] = dtok
                xo_free[b] = dtok
                continue

            for c in range(4):
                g = i * 4 + c
                pv, cur = (g + 1) % 2, g % 2
                e1 = prog.ve(lambda v, pv=pv: v.tensor_copy(st_sb[:], st32[pv][:]),
                             waits=[st0_tok, ones_aug, last_sadd], inc="vs")
                st_ready = e1
                def score_slot(h):
                    bank = (h % 2) + 2 * (h // 8)
                    lane = (h // 2) % 4
                    return bank, lane
                score_banks = [pp[0], pp[1], npp[0], npp[1]]
                bank_tok = [None] * 4
                for h in range(NH):
                    db, half = h // 2, (h % 2) * 64
                    bank, lane = score_slot(h)
                    tgt_b = score_banks[bank]
                    w0 = [st_ready, qT_done, kphi_done] if h == 0 else ()
                    bank_tok[bank] = prog.pe(lambda t, tgt_b=tgt_b, lane=lane, db=db,
                                             half=half, c=c: t.matmul(
                        tgt_b[:, lane * 128:(lane + 1) * 128],
                        kT[half:half + 64, db * TT + c * 128: db * TT + (c + 1) * 128],
                        qT[half:half + 64, db * TT + c * 128: db * TT + (c + 1) * 128],
                        start=True, stop=True),
                        waits=w0, inc="ps" if h in (6, 7, 14, 15) else None)
                mask_toks = []
                for bank in range(4):
                    mask_toks.append(prog.ve(lambda v, bank=bank: v.tensor_tensor(
                        sT_bf[:, bank * 512:(bank + 1) * 512], score_banks[bank][:],
                        mask4[:], ALU.mult),
                        waits=[bank_tok[bank]], inc="vs"))
                sT_done = mask_toks[-1]
                if debug_stage == "scores":
                    if c == 0:
                        dtok = prog.sync(lambda sy, i=i, b=b: sy.dma_start(
                            out=x_tile_in(x_out, i),
                            in_=xt32[b][:].rearrange("p (c d) -> p c d", c=4)),
                            waits=[sT_done], inc=f"st{b}", n=16)
                        stores.append(dtok)
                        xt_free[b] = dtok
                        xo_free[b] = dtok
                        last_sadd = sT_done
                    continue

                def npv(h):
                    if h < 7:
                        return npp[0], h * 65
                    if h < 14:
                        return npp[1], (h - 7) * 65
                    return pp[0], (h - 14) * 65

                for h in range(NH):
                    db, half = h // 2, (h % 2) * 64
                    jp = h // 8
                    scol = ((h % 8) // 2) * 65
                    prow = (h % 2) * 64
                    tgt, col = npv(h)
                    bank_s, lane_s = score_slot(h)
                    slot = bank_s * 4 + lane_s
                    w0 = [sT_done] if h == 0 else ()
                    prog.pe(lambda t, h=h, c=c, tgt=tgt, col=col, slot=slot: t.matmul(
                        tgt[:, col:col + 65],
                        sT_bf[:, slot * 128:(slot + 1) * 128],
                        v_aug[:, c * 1040 + h * 65: c * 1040 + (h + 1) * 65],
                        start=True, stop=False), waits=w0)
                    prog.pe(lambda t, h=h, c=c, db=db, half=half, jp=jp, prow=prow,
                            scol=scol, tgt=tgt, col=col: t.matmul(
                        tgt[:, col:col + 65],
                        qT[half:half + 64, db * TT + c * 128: db * TT + (c + 1) * 128],
                        st_sb[prow:prow + 64, jp * 260 + scol: jp * 260 + scol + 65],
                        start=False, stop=True))
                for h in range(NH):
                    jp = h // 8
                    scol = ((h % 8) // 2) * 65
                    prow = (h % 2) * 64
                    tok_m4 = prog.pe(lambda t, h=h, c=c, jp=jp, prow=prow, scol=scol: t.matmul(
                        sp[jp][prow:prow + 64, scol:scol + 65],
                        kphi[:, c * D + h * 64: c * D + (h + 1) * 64],
                        v_aug[:, c * 1040 + h * 65: c * 1040 + (h + 1) * 65],
                        start=True, stop=True), inc="ps" if h == NH - 1 else None)
                last_m4 = tok_m4
                sa1 = prog.ve(lambda v, pv=pv, cur=cur: v.tensor_tensor(
                    st32[cur][:, 0:260], sp[0][:, 0:260], st32[pv][:, 0:260], ALU.add),
                    waits=[tok_m4], inc="vs")
                last_sadd = prog.ve(lambda v, pv=pv, cur=cur: v.tensor_tensor(
                    st32[cur][:, 260:520], sp[1][:, 0:260], st32[pv][:, 260:520], ALU.add),
                    inc="vs")

                if debug_stage == "state":
                    if c == 3:
                        dtok = prog.gp(lambda g, i=i, cur=cur: g.dma_start(
                            out=x_tile_in(x_out, i)[:, 0:1, 0:520],
                            in_=st32[cur][:].rearrange("p x -> p 1 x")),
                            waits=[last_sadd], inc=f"st{b}", n=16)
                        stores.append(dtok)
                        xt_free[b] = dtok
                        xo_free[b] = dtok
                    continue
                d1a = prog.ve(lambda v: v.tensor_scalar(
                    dd[:, 0:7].unsqueeze(2),
                    npp[0][:, 0:455].rearrange("p (h x) -> p h x", x=65)[:, :, 64:65],
                    ATT_EPS, None, ALU.add), waits=[tok_m4], inc="vs")
                d1b = prog.ve(lambda v: v.tensor_scalar(
                    dd[:, 7:14].unsqueeze(2),
                    npp[1][:, 0:455].rearrange("p (h x) -> p h x", x=65)[:, :, 64:65],
                    ATT_EPS, None, ALU.add), inc="vs")
                d1c = prog.ve(lambda v: v.tensor_scalar(
                    dd[:, 14:16].unsqueeze(2),
                    pp[0][:, 0:130].rearrange("p (h x) -> p h x", x=65)[:, :, 64:65],
                    ATT_EPS, None, ALU.add), inc="vs")
                d2 = prog.ve(lambda v: v.reciprocal(dr[:], dd[:]), waits=[d1c], inc="vs")
                div_last = None
                for h in range(NH):
                    tgt, col = npv(h)
                    div_last = prog.ve(lambda v, h=h, tgt=tgt, col=col: v.tensor_scalar(
                        att_sb[:, h * 64:(h + 1) * 64], tgt[:, col:col + 64],
                        dr[:, h:h + 1], None, ALU.mult),
                        waits=[d2] if h == 0 else (), inc="vs" if h == NH - 1 else None)

                if debug_stage == "chunks":
                    dtok = prog.gp(lambda g, i=i, c=c: g.dma_start(
                        out=x_tile_in(x_out, i)[:, c, :],
                        in_=att_sb[:]),
                        waits=[div_last], inc=f"st{b}", n=16)
                    if c == 3:
                        stores.append(dtok)
                        xt_free[b] = dtok
                        xo_free[b] = dtok
                        tok_v3 = dtok
                        prev_pp_user = sT_done
                    continue
                attT_done = transpose_block(
                    lambda j: att_sb[:, j * 128:(j + 1) * 128],
                    lambda j: attT_sb[:, j * 128:(j + 1) * 128],
                    8, [div_last, attT_free])
                attT_free = None
                for nb in range(2):
                    pb = nb
                    for pr in range(8):
                        prog.pe(lambda t, pb=pb, pr=pr, nb=nb: t.matmul(
                            pp[pb][:], attT_sb[:, pr * 128:(pr + 1) * 128],
                            Wo_sb[:, pr * D + nb * 512: pr * D + (nb + 1) * 512],
                            start=(pr == 0), stop=False),
                            waits=[attT_done, prog.all_tok('ld')] if pr == 0 else ())
                    tok_o = prog.pe(lambda t, pb=pb, nb=nb: t.matmul(
                        pp[pb][:], ones_row[0:1, 0:128], bo_sb[0:1, nb * 512:(nb + 1) * 512],
                        start=False, stop=True), inc="ps")
                    tok_v3 = prog.ve(lambda v, pb=pb, c=c, nb=nb, b=b: v.tensor_tensor(
                        xo32[b][:, c * D + nb * 512: c * D + (nb + 1) * 512],
                        pp[pb][:], xt32[b][:, c * D + nb * 512: c * D + (nb + 1) * 512],
                        ALU.add),
                        waits=[tok_o] + ([xo_free[b]] if (c == 0 and nb == 0) else []),
                        inc="vs")
                    if nb == 1:
                        attT_free = tok_v3
            if debug_stage in ("chunks", "scores", "state"):
                continue
            xt_free[b] = tok_v3
            prev_pp_user = tok_v3
            stores.append(prog.sync(
                lambda sy, i=i, b=b: sy.dma_start(
                    out=x_tile_in(x_out, i),
                    in_=xo32[b][:].rearrange("p (c d) -> p c d", c=4)),
                waits=[tok_v3], inc=f"st{b}"))
            xo_free[b] = stores[-1]
        prog.sync(lambda sy: sy.nop(), waits=[stores[-1]])
        _fin(nc, prog, stack)
    return nc


def build_phase_l0_ffn(layer):
    TT = 256
    NT = S // TT
    NC = TT // 128
    nc = _mknc()
    x_in = _dram(nc, "x_in", [S, D], F32, "ExternalInput")
    x_out = _dram(nc, "x_out", [S, D], F32, "ExternalOutput")
    W1 = _dram(nc, "W1", [D, DFF], F32, "ExternalInput")
    b1d = _dram(nc, "b1", [1, DFF], F32, "ExternalInput")
    W2 = _dram(nc, "W2", [DFF, D], F32, "ExternalInput")
    b2d = _dram(nc, "b2", [1, D], F32, "ExternalInput")
    g1d = _dram(nc, "g1", [1, D], F32, "ExternalInput")
    be1d = _dram(nc, "be1", [1, D], F32, "ExternalInput")
    g2d = _dram(nc, "g2", [1, D], F32, "ExternalInput")
    be2d = _dram(nc, "be2", [1, D], F32, "ExternalInput")

    prog = _mkprog()
    with ExitStack() as stack:
        sb = lambda n, s, d: stack.enter_context(nc.sbuf_tensor(_nm(n), s, d))
        psal = lambda n, s, d: stack.enter_context(nc.psum_tensor(_nm(n), s, d))
        C = declare_constants(nc, stack, prog, "ld",
                              need=("id32", "ones_row", "ones_row32", "eps_ln"))
        id32, ones_row = C["id32"], C["ones_row"]
        consts = C["const_toks"]

        W1_sb = sb("W1_sb", [128, 8 * DFF], BF16)
        W2_sb = sb("W2_sb", [128, 32 * D], BF16)
        b1_sb = sb("b1_sb", [1, DFF], BF16)
        b2p_sb = sb("b2p_sb", [1, D], BF16)
        g1c = sb("g1c", [128, 8], F32)
        be1c = sb("be1c", [128, 8], F32)
        g1f = sb("g1f", [128, D], F32)
        g2f = sb("g2f", [128, D], F32)
        be2f = sb("be2f", [128, D], F32)

        w1_t = load_weight_T(prog, W1, W1_sb[:], D, DFF)
        w2_t = load_weight_T(prog, W2, W2_sb[:], DFF, D)
        b1_t = prog.gp(lambda g: g.dma_start(out=b1_sb[:], in_=b1d), inc="ld", n=16)
        g1c_t = prog.gp(lambda g: g.dma_start(
            out=g1c[:].unsqueeze(2),
            in_=g1d.rearrange("one (k p) -> p k one", p=128),
            allow_slow_non_contiguous=True), inc="ld", n=16)
        be1c_t = prog.gp(lambda g: g.dma_start(
            out=be1c[:].unsqueeze(2),
            in_=be1d.rearrange("one (k p) -> p k one", p=128),
            allow_slow_non_contiguous=True), inc="ld", n=16)
        b2_t = prog.gp(lambda g: g.dma_start(out=g1f[0:1, :], in_=b2d), inc="ld", n=16)
        be1_t = prog.gp(lambda g: g.dma_start(out=be2f[0:1, :], in_=be1d), inc="ld", n=16)
        g1r_t = prog.gp(lambda g: g.dma_start(out=g2f[0:1, :], in_=g1d), inc="ld", n=16)
        g2r_t = prog.gp(lambda g: g.dma_start(out=u_acc[0:1, 0:D], in_=g2d), inc="ld", n=16)
        be2r_t = prog.gp(lambda g: g.dma_start(out=u_acc[0:1, D:2 * D], in_=be2d), inc="ld", n=16)
        b2p_tok = prog.ve(lambda v: v.tensor_tensor(b2p_sb[:], g1f[0:1, :], be2f[0:1, :], ALU.add),
                          waits=[prog.all_tok('ld')], inc="vs")

        xt32 = [sb("xt32_s", [128, NC * D], F32)] * 2
        xc32 = [sb(f"xc32_{i}", [128, NC * D], F32) for i in range(2)]
        xl1T = sb("xl1T", [128, 8 * TT], BF16)
        hT = sb("hT", [128, 32 * TT], BF16)
        u_acc = sb("u_acc", [128, NC * D], F32)
        bn6 = sb("bn6", [128, 48], F32)
        mv = sb("mv", [128, 2 * NC], F32)
        rs = sb("rs", [128, NC], F32)

        pp = [psal(f"pp{j}", [128, 512], F32) for j in range(2)]
        npps = [psal(f"np{j}", [128, 512], F32) for j in range(2)]
        tpf = [psal(f"tp{j}", [128, 512], F32) for j in range(2)]
        gpp = psal("gpp", [128, 512], F32)

        ones_row32 = C["ones_row32"]

        def replicate(dst, src_row, wtok, prev):
            prev = prev if isinstance(prev, (list, tuple)) else ([prev] if prev else [])
            toks = []
            for jj in range(2):
                w = [wtok, prog.all_tok('gs')] + list(prev) + ([toks[-1]] if toks else [])
                tok_mm = prog.pe(lambda t, jj=jj: t.matmul(
                    gpp[:], ones_row32[0:1, 0:128], src_row[:, jj * 512:(jj + 1) * 512],
                    start=True, stop=True), waits=w, inc="ps")
                toks.append(prog.ve(lambda v, jj=jj: v.tensor_copy(
                    dst[:, jj * 512:(jj + 1) * 512], gpp[:]), waits=[tok_mm], inc="vs"))
            return toks[-1]

        rep0 = replicate(be2f, u_acc[0:1, D:2 * D], be2r_t, [b2p_tok])
        rep1 = replicate(g1f, g2f[0:1, :], g1r_t, [b2p_tok, rep0])
        rep_done = replicate(g2f, u_acc[0:1, 0:D], g2r_t, [rep1])

        tp_tok = [None, None]
        stores = []
        xt_free = [None, None]
        xc_free = [None, None]
        u_free = rep_done
        for i in range(NT):
            b = i % 2
            tok_x = prog.sync(
                lambda sy, i=i, b=b: sy.dma_start(
                    out=xt32[b][:].rearrange("p (c d) -> p c d", c=NC),
                    in_=x_tile_in(x_in, i, nchunk=NC)),
                waits=[xt_free[b]], inc=f"lx{b}")
            tok_mv, tok_rs = emit_ln_tile(prog, xt32[b], NC, bn6, mv, rs,
                                          C["eps_ln"], waits=[tok_x])
            tok_xc = None
            for cc in range(NC):
                tok_xc = prog.ve(lambda v, b=b, cc=cc: v.tensor_scalar(
                    xc32[b][:, cc * D:(cc + 1) * D], xt32[b][:, cc * D:(cc + 1) * D],
                    mv[:, 2 * cc:2 * cc + 1], rs[:, cc:cc + 1],
                    ALU.subtract, ALU.mult),
                    waits=[tok_mv, tok_rs, xc_free[b]] if cc == 0 else (),
                    inc="vs" if cc == NC - 1 else None)
            xt_free[b] = tok_xc
            u0_toks = []
            for cc in range(NC):
                u0_toks.append(prog.ve(lambda v, b=b, cc=cc: v.tensor_tensor(
                    u_acc[:, cc * D:(cc + 1) * D], xc32[b][:, cc * D:(cc + 1) * D],
                    g1f[:], ALU.mult),
                    waits=[tok_xc, rep_done, u_free] if cc == 0 else (),
                    inc="vs" if cc == NC - 1 else None))
            tok_u0 = u0_toks[-1]
            ev_last = None
            for rnd in range(4):
                tb = rnd % 2
                w = [tok_xc] + ([tp_tok[tb]] if tp_tok[tb] else [])
                for j in range(4):
                    blk = rnd * 4 + j
                    db, cc = blk // NC, blk % NC
                    tok_tr = prog.pe(lambda t, tb=tb, j=j, db=db, cc=cc, b=b: t.transpose(
                        tpf[tb][:, j * 128:(j + 1) * 128],
                        xc32[b][:, cc * D + db * 128: cc * D + (db + 1) * 128], id32[:]),
                        waits=w if j == 0 else (), inc="ps" if j == 3 else None)
                for j in range(4):
                    blk = rnd * 4 + j
                    db, cc = blk // NC, blk % NC
                    ev_last = prog.ve(lambda v, tb=tb, j=j, db=db, cc=cc: v.tensor_scalar(
                        xl1T[:, db * TT + cc * 128: db * TT + (cc + 1) * 128],
                        tpf[tb][:, j * 128:(j + 1) * 128],
                        g1c[:, db:db + 1], be1c[:, db:db + 1], ALU.mult, ALU.add),
                        waits=[tok_tr, prog.all_tok('ld')] if j == 0 else (),
                        inc="vs" if j == 3 else None)
                tp_tok[tb] = ev_last
            xl1T_done = ev_last

            h_evs = []
            for ob in range(32):
                pb = ob % 2
                w0 = [xl1T_done, prog.all_tok('ld')] + ([h_evs[ob - 2]] if ob >= 2 else [])
                for kb in range(8):
                    prog.pe(lambda t, pb=pb, ob=ob, kb=kb: t.matmul(
                        pp[pb][:, 0:TT], W1_sb[:, kb * DFF + ob * 128: kb * DFF + (ob + 1) * 128],
                        xl1T[:, kb * TT:(kb + 1) * TT], start=(kb == 0), stop=False),
                        waits=w0 if kb == 0 else ())
                tok_mm = prog.pe(lambda t, pb=pb, ob=ob: t.matmul(
                    pp[pb][:, 0:TT], b1_sb[0:1, ob * 128:(ob + 1) * 128],
                    ones_row[0:1, 0:TT], start=False, stop=True), inc="ps")
                h_evs.append(prog.se(lambda s, pb=pb, ob=ob: s.activation(
                    hT[:, ob * TT:(ob + 1) * TT], pp[pb][:, 0:TT], AF.Gelu),
                    waits=[tok_mm], inc="ss"))
            hT_done = h_evs[-1]

            u_toks = []
            for cc in range(NC):
                for nb in range(2):
                    qb = (cc * 2 + nb) % 2
                    w0 = [hT_done, prog.all_tok('ld'), b2p_tok] + ([u_toks[-2]] if len(u_toks) >= 2 else [])
                    for ob in range(32):
                        prog.pe(lambda t, qb=qb, cc=cc, nb=nb, ob=ob: t.matmul(
                            npps[qb][:], hT[:, ob * TT + cc * 128: ob * TT + (cc + 1) * 128],
                            W2_sb[:, ob * D + nb * 512: ob * D + (nb + 1) * 512],
                            start=(ob == 0), stop=False), waits=w0 if ob == 0 else ())
                    tok_mm = prog.pe(lambda t, qb=qb, nb=nb: t.matmul(
                        npps[qb][:], ones_row[0:1, 0:128], b2p_sb[0:1, nb * 512:(nb + 1) * 512],
                        start=False, stop=True), inc="ps")
                    u_toks.append(prog.ve(lambda v, qb=qb, cc=cc, nb=nb, b=b: v.tensor_tensor(
                        xc32[b][:, cc * D + nb * 512: cc * D + (nb + 1) * 512],
                        npps[qb][:],
                        u_acc[:, cc * D + nb * 512: cc * D + (nb + 1) * 512], ALU.add),
                        waits=[tok_mm, tok_u0], inc="vs"))
            u_done = u_toks[-1]

            tok_mv2, tok_rs2 = emit_ln_tile(prog, xc32[b], NC, bn6, mv, rs,
                                            C["eps_ln"], waits=[u_done])
            a1 = None
            for cc in range(NC):
                a1 = prog.ve(lambda v, b=b, cc=cc: v.tensor_scalar(
                    u_acc[:, cc * D:(cc + 1) * D], xc32[b][:, cc * D:(cc + 1) * D],
                    mv[:, 2 * cc:2 * cc + 1], rs[:, cc:cc + 1],
                    ALU.subtract, ALU.mult),
                    waits=[tok_mv2, tok_rs2] if cc == 0 else (),
                    inc="vs" if cc == NC - 1 else None)
            a2s = []
            for cc in range(NC):
                a2s.append(prog.ve(lambda v, cc=cc, b=b: v.tensor_tensor(
                    xc32[b][:, cc * D:(cc + 1) * D], u_acc[:, cc * D:(cc + 1) * D],
                    g2f[:], ALU.mult), waits=[a1] if cc == 0 else (),
                    inc="vs" if cc == NC - 1 else None))
            a3s = []
            for cc in range(NC):
                a3s.append(prog.ve(lambda v, cc=cc, b=b: v.tensor_tensor(
                    u_acc[:, cc * D:(cc + 1) * D], xc32[b][:, cc * D:(cc + 1) * D],
                    be2f[:], ALU.add), waits=[a2s[-1]] if cc == 0 else (),
                    inc="vs" if cc == NC - 1 else None))
            a3 = a3s[-1]
            xc_free[b] = a3
            stores.append(prog.sync(
                lambda sy, i=i: sy.dma_start(
                    out=x_tile_in(x_out, i, nchunk=NC),
                    in_=u_acc[:].rearrange("p (c d) -> p c d", c=NC)),
                waits=[a3], inc="st"))
            u_free = stores[-1]
        prog.sync(lambda sy: sy.nop(), waits=[stores[-1]])
        _fin(nc, prog, stack)
    return nc


def build_phase_res_ln():
    nc = _mknc()
    x_in = _dram(nc, "x_in", [S, D], F32, "ExternalInput")
    res_in = _dram(nc, "res_in", [S, D], F32, "ExternalInput")
    x_out = _dram(nc, "x_out", [S, D], F32, "ExternalOutput")
    gfd = _dram(nc, "gf", [1, D], F32, "ExternalInput")
    befd = _dram(nc, "bef", [1, D], F32, "ExternalInput")
    prog = _mkprog()
    with ExitStack() as stack:
        sb = lambda n, s, d: stack.enter_context(nc.sbuf_tensor(_nm(n), s, d))
        psal = lambda n, s, d: stack.enter_context(nc.psum_tensor(_nm(n), s, d))
        C = declare_constants(nc, stack, prog, "ld", need=("ones_row32", "eps_ln"))
        ones_row32 = C["ones_row32"]
        xt = [sb(f"xt{i}", [128, 4 * D], F32) for i in range(2)]
        rt = [sb(f"rt{i}", [128, 4 * D], F32) for i in range(2)]
        ot = [sb(f"ot{i}", [128, 4 * D], F32) for i in range(2)]
        gff = sb("gff", [128, D], F32)
        beff = sb("beff", [128, D], F32)
        bn6 = sb("bn6", [128, 48], F32)
        mv = sb("mv", [128, 8], F32)
        rs = sb("rs", [128, 4], F32)
        gpp = psal("gpp", [128, 512], F32)
        g_t = prog.gp(lambda g: g.dma_start(out=gff[0:1, :], in_=gfd), inc="ld", n=16)
        b_t = prog.gp(lambda g: g.dma_start(out=beff[0:1, :], in_=befd), inc="ld", n=16)
        rtoks = []
        for (dst, srcdst) in ((beff, beff), (gff, gff)):
            for jj in range(2):
                w = [prog.all_tok("ld"), prog.all_tok("gs")] + ([rtoks[-1]] if rtoks else [])
                tok_mm = prog.pe(lambda t, jj=jj, srcdst=srcdst: t.matmul(
                    gpp[:], ones_row32[0:1, 0:128], srcdst[0:1, jj * 512:(jj + 1) * 512],
                    start=True, stop=True), waits=w, inc="ps")
                rtoks.append(prog.ve(lambda v, jj=jj, dst=dst: v.tensor_copy(
                    dst[:, jj * 512:(jj + 1) * 512], gpp[:]), waits=[tok_mm], inc="vs"))
        rep_done = rtoks[-1]
        comp, stores = [], []
        for i in range(8):
            b = i % 2
            lx = prog.sync(lambda sy, i=i, b=b: sy.dma_start(
                out=xt[b][:].rearrange("p (c d) -> p c d", c=4), in_=x_tile_in(x_in, i)),
                waits=[comp[i - 2]] if i >= 2 else [], inc=f"lx{b}")
            lr = prog.sync(lambda sy, i=i, b=b: sy.dma_start(
                out=rt[b][:].rearrange("p (c d) -> p c d", c=4), in_=x_tile_in(res_in, i)),
                inc=f"lx{b}")
            tok_mv, tok_rs = emit_ln_tile(prog, xt[b], 4, bn6, mv, rs, C["eps_ln"],
                                          waits=[lx, lr])
            ctok = None
            for c in range(4):
                t1 = prog.ve(lambda v, b=b, c=c: v.tensor_scalar(
                    ot[b][:, c * D:(c + 1) * D], xt[b][:, c * D:(c + 1) * D],
                    mv[:, 2 * c:2 * c + 1], rs[:, c:c + 1], ALU.subtract, ALU.mult),
                    waits=([tok_mv, tok_rs, rep_done] +
                           ([stores[i - 2]] if i >= 2 else [])) if c == 0 else ())
                t2 = prog.ve(lambda v, b=b, c=c: v.tensor_tensor(
                    ot[b][:, c * D:(c + 1) * D], ot[b][:, c * D:(c + 1) * D], gff[:],
                    ALU.mult))
                t3 = prog.ve(lambda v, b=b, c=c: v.tensor_tensor(
                    ot[b][:, c * D:(c + 1) * D], ot[b][:, c * D:(c + 1) * D], beff[:],
                    ALU.add))
                ctok = prog.ve(lambda v, b=b, c=c: v.tensor_tensor(
                    ot[b][:, c * D:(c + 1) * D], ot[b][:, c * D:(c + 1) * D],
                    rt[b][:, c * D:(c + 1) * D], ALU.add), inc="vs" if c == 3 else None)
            comp.append(ctok)
            stores.append(prog.sync(lambda sy, i=i, b=b: sy.dma_start(
                out=x_tile_in(x_out, i), in_=ot[b][:].rearrange("p (c d) -> p c d", c=4)),
                waits=[ctok], inc=f"st{b}"))
        prog.sync(lambda sy: sy.nop(), waits=[stores[-1]])
        _fin(nc, prog, stack)
    return nc


def build_phase_l0ffn_big():
    nc = _mknc()
    x_in = _dram(nc, "x_in", [S, D], F32, "ExternalInput")
    x_out = _dram(nc, "x_out", [S, D], F32, "ExternalOutput")
    w1d = _dram(nc, "w1_w", [DFF, D], F32, "ExternalInput")
    w2d = _dram(nc, "w2_w", [DFF, D], F32, "ExternalInput")
    h1d = _dram(nc, "h1_w", [DFF, DFF], F32, "ExternalInput")
    h2d = _dram(nc, "h2_w", [DFF, DFF], F32, "ExternalInput")
    w3d = _dram(nc, "w3_w", [D, DFF], F32, "ExternalInput")
    s1d = _dram(nc, "w1_s", [DFF], F32, "ExternalInput")
    s2d = _dram(nc, "w2_s", [DFF], F32, "ExternalInput")
    h1sd = _dram(nc, "h1_s", [DFF], F32, "ExternalInput")
    h2sd = _dram(nc, "h2_s", [DFF], F32, "ExternalInput")
    s3d = _dram(nc, "w3_s", [D], F32, "ExternalInput")
    w1bf = _dram(nc, "z_w1bf", [DFF, D], BF16, "Internal")
    w2bf = _dram(nc, "z_w2bf", [DFF, D], BF16, "Internal")
    h1bf = _dram(nc, "z_h1bf", [DFF, DFF], BF16, "Internal")
    h2bf = _dram(nc, "z_h2bf", [DFF, DFF], BF16, "Internal")
    w3bf = _dram(nc, "z_w3bf", [D, DFF], BF16, "Internal")
    gT_d = _dram(nc, "z_gT", [8, 128, 32, 512], BF16, "Internal")
    uT_d = _dram(nc, "z_uT", [8, 128, 32, 512], BF16, "Internal")
    u2T_d = _dram(nc, "z_u2T", [8, 128, 32, 512], BF16, "Internal")

    prog = _mkprog()
    with ExitStack() as stack:
        sb = lambda n, s, d: stack.enter_context(nc.sbuf_tensor(_nm(n), s, d))
        psal = lambda n, s, d: stack.enter_context(nc.psum_tensor(_nm(n), s, d))
        C = declare_constants(nc, stack, prog, "ld", need=("id_bf", "id32", "eps_ln"))
        id32, id_bf = C["id32"], C["id_bf"]

        pool = sb("pool", [128, 32768], BF16)
        stg = [sb(f"stg{j}", [128, 2048], BF16) for j in range(2)]
        ev1 = [sb(f"ev1_{j}", [128, 512], BF16) for j in range(2)]
        ev2 = [sb(f"ev2_{j}", [128, 512], BF16) for j in range(2)]
        xt32 = sb("xt32", [128, 4 * D], F32)
        ot32 = sb("ot32", [128, 4 * D], F32)
        xc = sb("xc", [128, 4 * D], BF16)
        xl0T = sb("xl0T", [128, 8 * 512], BF16)
        gslab = sb("gslab", [128, 32 * 512], BF16)
        scol = sb("scol", [128, 136], F32)
        sstg = sb("sstg", [32, 128], F32)
        bn6 = sb("bn6", [128, 48], F32)
        mv = sb("mv", [128, 8], F32)
        rs = sb("rs", [128, 4], F32)

        pp = [psal(f"pp{j}", [128, 512], F32) for j in range(4)]
        tpb2 = [psal(f"tpb{j}", [128, 1024], BF16) for j in range(2)]
        gpp = psal("gpp", [128, 512], F32)

        prev_ev = None
        for (sd, cnt, coff) in [(s1d, 32, 0), (s2d, 32, 32), (h1sd, 32, 64),
                                (h2sd, 32, 96), (s3d, 8, 128)]:
            lt = prog.sync(lambda sy, sd=sd, cnt=cnt: sy.dma_start(
                out=sstg[0:cnt, :], in_=sd.rearrange("(k p) -> k p", p=128)),
                waits=[prev_ev], inc="sx")
            tt = prog.pe(lambda t, cnt=cnt: t.transpose(
                gpp[:, 0:cnt], sstg[0:cnt, :], id32[0:cnt, 0:cnt]),
                waits=[lt, prog.all_tok("ld")], inc="ps")
            prev_ev = prog.ve(lambda v, cnt=cnt, coff=coff: v.tensor_copy(
                scol[:, coff:coff + cnt], gpp[:, 0:cnt]), waits=[tt], inc="vs")
        scol_done = prev_ev

        pw_last = [None, None]

        def prep(dram_in, dram_out, rows, cols, scale_coff=None):
            ncol = max(1, cols // 2048)
            ccw = min(cols, 2048)
            for rb in range(rows // 128):
                for cb in range(ncol):
                    j = (rb * ncol + cb) % 2
                    lt = prog.gp(lambda g, rb=rb, cb=cb, j=j, ccw=ccw: g.dma_start(
                        out=stg[j][:, 0:ccw],
                        in_=dram_in[rb * 128:(rb + 1) * 128, cb * ccw:(cb + 1) * ccw]),
                        waits=[pw_last[j]], inc=f"pl{j}", n=16)
                    if scale_coff is not None:
                        lt = prog.ve(lambda v, rb=rb, j=j, ccw=ccw: v.tensor_scalar(
                            stg[j][:, 0:ccw], stg[j][:, 0:ccw],
                            scol[:, scale_coff + rb:scale_coff + rb + 1], None,
                            ALU.mult), waits=[lt, scol_done], inc="vs")
                    pw_last[j] = prog.sync(lambda sy, rb=rb, cb=cb, j=j, ccw=ccw: sy.dma_start(
                        out=dram_out[rb * 128:(rb + 1) * 128, cb * ccw:(cb + 1) * ccw],
                        in_=stg[j][:, 0:ccw]), waits=[lt], inc=f"pw{j}")
            return pw_last[0], pw_last[1]

        w1p = prep(w1d, w1bf, DFF, D)
        w2p = prep(w2d, w2bf, DFF, D)
        h1p = prep(h1d, h1bf, DFF, DFF)
        h2p = prep(h2d, h2bf, DFF, DFF)
        w3p = prep(w3d, w3bf, D, DFF, scale_coff=128)
        prep_done = [w3p[0], w3p[1]]

        lastg = {}
        c1_done = None
        for oh in range(2):
            for (bfd, base) in [(w1bf, 0), (w2bf, 16384)]:
                for ib in range(8):
                    wv = []
                    if oh == 0 and base == 0 and ib == 0:
                        wv = [w2p[0], w2p[1]]
                    if oh == 1 and base == 0 and ib == 0:
                        wv = [lastg.get(("gst", 0)), lastg.get(("gst", 1))]
                    prog.sync(lambda sy, bfd=bfd, base=base, ib=ib, oh=oh: sy.dma_start(
                        out=pool[:, base + ib * 2048: base + (ib + 1) * 2048],
                        in_=bfd[oh * 2048:(oh + 1) * 2048, ib * 128:(ib + 1) * 128],
                        transpose=True), waits=wv, inc="tx")
            w12T_done = prog.cur_tok("tx")
            for i in range(8):
                lx = prog.sync(lambda sy, i=i: sy.dma_start(
                    out=xt32[:].rearrange("p (c d) -> p c d", c=4),
                    in_=x_tile_in(x_in, i)), waits=[lastg.get(("xfree",))], inc="lx0")
                tok_mv, tok_rs = emit_ln_tile(prog, xt32, 4, bn6, mv, rs, C["eps_ln"],
                                              waits=[lx])
                xc_tok = None
                for c in range(4):
                    xc_tok = prog.ve(lambda v, c=c: v.tensor_scalar(
                        xc[:, c * D:(c + 1) * D], xt32[:, c * D:(c + 1) * D],
                        mv[:, 2 * c:2 * c + 1], rs[:, c:c + 1], ALU.subtract, ALU.mult),
                        waits=[tok_mv, tok_rs, lastg.get(("xcfree",))] if c == 0 else (),
                        inc="vs" if c == 3 else None)
                lastg[("xfree",)] = xc_tok
                ev_last = None
                for rnd in range(4):
                    w = [xc_tok] + ([lastg.get(("tp", rnd % 2))] if lastg.get(("tp", rnd % 2)) else [])
                    tb = rnd % 2
                    for j in range(8):
                        blk = rnd * 8 + j
                        db, cc = blk // 4, blk % 4
                        tok_tr = prog.pe(lambda t, tb=tb, j=j, db=db, cc=cc: t.transpose(
                            tpb2[tb][:, (j % 8) * 128:(j % 8 + 1) * 128],
                            xc[:, cc * D + db * 128: cc * D + (db + 1) * 128], id_bf[:]),
                            waits=w if j == 0 else (), inc="ps" if j == 7 else None)
                    for j in range(8):
                        blk = rnd * 8 + j
                        db, cc = blk // 4, blk % 4
                        ev_last = prog.se(lambda s, tb=tb, j=j, db=db, cc=cc: s.copy(
                            xl0T[:, db * 512 + cc * 128: db * 512 + (cc + 1) * 128],
                            tpb2[tb][:, (j % 8) * 128:(j % 8 + 1) * 128]),
                            waits=[tok_tr] if j == 0 else (), inc="ss" if j == 7 else None)
                    lastg[("tp", rnd % 2)] = ev_last
                xl0T_done = ev_last
                lastg[("xcfree",)] = xl0T_done

                for ogl in range(16):
                    og = oh * 16 + ogl
                    pa, pb_ = (0, 1) if ogl % 2 == 0 else (2, 3)
                    e = ogl % 2
                    w0 = [xl0T_done, w12T_done] + \
                         ([lastg.get(("gev", pa))] if lastg.get(("gev", pa)) else [])
                    for ib in range(8):
                        tok_a = prog.pe(lambda t, pa=pa, ogl=ogl, ib=ib: t.matmul(
                            pp[pa][:], pool[:, ib * 2048 + ogl * 128: ib * 2048 + (ogl + 1) * 128],
                            xl0T[:, ib * 512:(ib + 1) * 512], start=(ib == 0), stop=(ib == 7)),
                            waits=w0 if ib == 0 else (), inc="ps" if ib == 7 else None)
                    for ib in range(8):
                        tok_b = prog.pe(lambda t, pb_=pb_, ogl=ogl, ib=ib: t.matmul(
                            pp[pb_][:], pool[:, 16384 + ib * 2048 + ogl * 128: 16384 + ib * 2048 + (ogl + 1) * 128],
                            xl0T[:, ib * 512:(ib + 1) * 512], start=(ib == 0), stop=(ib == 7)),
                            inc="ps" if ib == 7 else None)
                    sil = prog.se(lambda s, pa=pa, og=og, e=e: s.activation(
                        ev1[e][:], pp[pa][:], AF.Silu, scale=scol[:, og:og + 1]),
                        waits=[tok_a, scol_done], inc="ss")
                    gev = prog.ve(lambda v, pb_=pb_, og=og, e=e: v.scalar_tensor_tensor(
                        ev2[e][:], pp[pb_][:], scol[:, 32 + og:32 + og + 1],
                        ev1[e][:], ALU.mult, ALU.mult),
                        waits=[tok_b, sil, lastg.get(("gst", e))], inc="vs")
                    gst = prog.gp(lambda g, og=og, i=i, e=e: g.dma_start(
                        out=gT_d[i, :, og, :], in_=ev2[e][:]),
                        waits=[gev], inc=f"gw{e}", n=16)
                    lastg[("gev", pa)] = gst
                    lastg[("gst", e)] = gst
        c1_pair = [lastg[("gst", 0)], lastg[("gst", 1)]]
        c1_done = c1_pair

        def hlayer(src_d, dst_d, bfd, sc_off, wait_w, wait_src, pool_free):
            done = None
            out_last = {}
            for oq in range(4):
                stoks = []
                for hb in range(32):
                    wv = ([wait_w[0], wait_w[1]] + list(pool_free)) if (oq == 0 and hb == 0) else (done if (hb == 0 and done) else [])
                    stoks.append(prog.sync(lambda sy, bfd=bfd, oq=oq, hb=hb: sy.dma_start(
                        out=pool[:, hb * 1024:(hb + 1) * 1024],
                        in_=bfd[oq * 1024:(oq + 1) * 1024, hb * 128:(hb + 1) * 128],
                        transpose=True), waits=wv, inc="tx"))
                strips_done = prog.cur_tok("tx")
                for t8 in range(8):
                    lslab = prog.sync(lambda sy, t8=t8: sy.dma_start(
                        out=gslab[:].rearrange("p (h t) -> p h t", h=32),
                        in_=src_d[t8]),
                        waits=list(wait_src) + [out_last.get(("slabfree", t8 - 1, oq))], inc="lx0")
                    for og in range(8):
                        o = oq * 8 + og
                        pq = og % 4
                        w0 = [lslab, strips_done] +                              ([out_last.get((t8, og - 4))] if og >= 4 else
                              [out_last.get((t8 - 1, og + 4))])
                        for kb in range(32):
                            tok_mm = prog.pe(lambda t, pq=pq, og=og, kb=kb: t.matmul(
                                pp[pq][:], pool[:, kb * 1024 + og * 128: kb * 1024 + (og + 1) * 128],
                                gslab[:, kb * 512:(kb + 1) * 512],
                                start=(kb == 0), stop=(kb == 31)),
                                waits=w0 if kb == 0 else (), inc="ps" if kb == 31 else None)
                        e = og % 2
                        uev = prog.se(lambda s, pq=pq, o=o, e=e: s.activation(
                            ev1[e][:], pp[pq][:], AF.Gelu, scale=scol[:, sc_off + o:sc_off + o + 1]),
                            waits=[tok_mm, out_last.get(("ust", e, t8, og - 2)) if og >= 2
                                   else out_last.get(("ust", e, t8 - 1, 6 + e))], inc="ss")
                        ust = prog.gp(lambda g, o=o, t8=t8, e=e: g.dma_start(
                            out=dst_d[t8, :, o, :], in_=ev1[e][:]),
                            waits=[uev], inc=f"uw{e}", n=16)
                        out_last[(t8, og)] = ust
                        out_last[("ust", e, t8, og)] = ust
                    out_last[("slabfree", t8, oq)] = out_last[(t8, 7)]
                done = [out_last[("ust", 0, 7, 6)], out_last[("ust", 1, 7, 7)]]
            return done

        c2_done = hlayer(gT_d, uT_d, h1bf, 64, h1p, c1_pair, c1_pair)
        c3_done = hlayer(uT_d, u2T_d, h2bf, 96, h2p, c2_done, c2_done)

        w3T_toks = []
        for hb in range(32):
            w = [w3p[0], w3p[1]] + list(c3_done) if hb == 0 else []
            w3T_toks.append(prog.sync(lambda sy, hb=hb: sy.dma_start(
                out=pool[:, hb * 1024:(hb + 1) * 1024],
                in_=w3bf[:, hb * 128:(hb + 1) * 128], transpose=True),
                waits=w, inc="tx"))
        w3T_done = prog.cur_tok("tx")
        ostores = []
        prev = {}
        for i in range(8):
            lslab = prog.sync(lambda sy, i=i: sy.dma_start(
                out=gslab[:].rearrange("p (h t) -> p h t", h=32),
                in_=u2T_d[i]),
                waits=list(c3_done) + [prev.get(("sfree", i - 1))], inc="lx0")
            lx = prog.sync(lambda sy, i=i: sy.dma_start(
                out=xt32[:].rearrange("p (c d) -> p c d", c=4),
                in_=x_tile_in(x_in, i)), waits=[prev.get(("xfree", i - 1))] + c1_pair,
                inc="lx1")
            otok = None
            for cc in range(4):
                for nb in range(2):
                    pq = (cc * 2 + nb) % 4
                    w0 = [lslab, lx, w3T_done] +                          ([prev.get(("ev", cc * 2 + nb - 4))] if cc * 2 + nb >= 4 else
                          [prev.get(("pev", cc * 2 + nb + 4))])
                    for kb in range(32):
                        tok_mm = prog.pe(lambda t, pq=pq, cc=cc, nb=nb, kb=kb: t.matmul(
                            pp[pq][:], gslab[:, kb * 512 + cc * 128: kb * 512 + (cc + 1) * 128],
                            pool[:, kb * 1024 + nb * 512: kb * 1024 + (nb + 1) * 512],
                            start=(kb == 0), stop=(kb == 31)),
                            waits=w0 if kb == 0 else (), inc="ps" if kb == 31 else None)
                    otok = prog.ve(lambda v, cc=cc, nb=nb, pq=pq: v.tensor_tensor(
                        ot32[:, cc * D + nb * 512: cc * D + (nb + 1) * 512], pp[pq][:],
                        xt32[:, cc * D + nb * 512: cc * D + (nb + 1) * 512], ALU.add),
                        waits=[tok_mm] + ([prev.get(("ofree", i - 1))] if cc == 0 and nb == 0 else []),
                        inc="vs")
                    prev[("ev", cc * 2 + nb)] = otok
            for k in range(8):
                prev[("pev", k)] = prev.get(("ev", k))
            prev[("xfree", i)] = otok
            prev[("sfree", i)] = otok
            ostores.append(prog.sync(lambda sy, i=i: sy.dma_start(
                out=x_tile_in(x_out, i), in_=ot32[:].rearrange("p (c d) -> p c d", c=4)),
                waits=[otok], inc="st0"))
            prev[("ofree", i)] = ostores[-1]
        prog.sync(lambda sy: sy.nop(), waits=[ostores[-1]])
        _fin(nc, prog, stack)
    return nc


def build_phase_l1_att():
    nc = _mknc()
    x_in = _dram(nc, "x_in", [S, D], F32, "ExternalInput")
    x_out = _dram(nc, "x_out", [S, D], F32, "ExternalOutput")
    wqd = _dram(nc, "w_q", [D, 128], F32, "ExternalInput")
    wkd = _dram(nc, "w_k", [D, 128], F32, "ExternalInput")
    wvd = _dram(nc, "w_v", [D, 128], F32, "ExternalInput")
    wod = _dram(nc, "w_o", [128, D], F32, "ExternalInput")
    hmd = _dram(nc, "head_dim_mask", [128, 1], mybir.dt.int32, "ExternalInput")

    prog = _mkprog()
    with ExitStack() as stack:
        sb = lambda n, s, d: stack.enter_context(nc.sbuf_tensor(_nm(n), s, d))
        psal = lambda n, s, d: stack.enter_context(nc.psum_tensor(_nm(n), s, d))
        C = declare_constants(nc, stack, prog, "ld",
                              need=("id_bf", "ones_row", "ones_col", "eps_ln"))
        id_bf, ones_row, ones_col = C["id_bf"], C["ones_row"], C["ones_col"]

        wq_sb = sb("wq_sb", [128, 8 * 128], BF16)
        wk_sb = sb("wk_sb", [128, 8 * 128], BF16)
        wv_sb = sb("wv_sb", [128, 8 * 128], BF16)
        wo_sb = sb("wo_sb", [128, D], BF16)
        hm_sb = sb("hm_sb", [128, 1], F32)
        qT = sb("qT", [128, S], BF16)
        kT = sb("kT", [128, S], BF16)
        v_aug = sb("v_aug", [128, 32 * 129], BF16)
        AT = sb("AT", [128, 32 * 512], BF16)
        xt32 = [sb(f"xt32_{j}", [128, 4 * D], F32) for j in range(2)]
        ot32 = sb("ot32", [128, 4 * D], F32)
        xc = sb("xc", [128, 4 * D], BF16)
        xT = sb("xT", [128, 8 * 512], BF16)
        sq = sb("sq", [128, 512], BF16)
        qn = sb("qn", [1, S], F32)
        kn = sb("kn", [1, S], F32)
        cb = sb("cb", [1, S], BF16)
        kmx = sb("kmx", [1, 1], F32)
        tmp_a = [sb(f"tmp_a{j}", [128, 512], F32) for j in range(2)]
        cbr_a = sb("cbr_a", [128, 512], F32)
        rp_a = sb("rp_a", [128, 512], F32)
        yTs_a = sb("yTs_a", [128, 512], BF16)
        drow_a = sb("drow_a", [1, 512], F32)
        drbf_a = sb("drbf_a", [1, 512], BF16)
        bn6 = sb("bn6", [128, 48], F32)
        mv = sb("mv", [128, 8], F32)
        rs = sb("rs", [128, 4], F32)

        pp = [psal(f"pp{j}", [128, 512], F32) for j in range(3)]
        tpb2 = [psal(f"tpb{j}", [128, 1024], BF16) for j in range(2)]
        npq = psal("npq", [128, 512], F32)
        yp_a = psal("ypa", [128, 512], F32)
        dp_a = psal("dpa", [128, 512], F32)

        for (dd_, sbuf) in ((wqd, wq_sb), (wkd, wk_sb), (wvd, wv_sb)):
            prog.gp(lambda g, dd_=dd_, sbuf=sbuf: g.dma_start(
                out=sbuf[:].rearrange("p (k n) -> p k n", k=8),
                in_=dd_.rearrange("(k p) n -> p k n", p=128)), inc="ld", n=16)
        prog.gp(lambda g: g.dma_start(out=wo_sb[:], in_=wod), inc="ld", n=16)
        prog.gp(lambda g: g.dma_start(out=hm_sb[:], in_=hmd), inc="ld", n=16)
        wo_m = prog.ve(lambda v: v.tensor_scalar(wo_sb[:], wo_sb[:], hm_sb[:], None,
                                                 ALU.mult),
                       waits=[prog.all_tok("ld")], inc="vs")

        tp_tok = [None, None]
        lastq = {}
        nprev = None
        for i in range(8):
            b = i % 2
            lx = prog.sync(lambda sy, i=i, b=b: sy.dma_start(
                out=xt32[b][:].rearrange("p (c d) -> p c d", c=4),
                in_=x_tile_in(x_in, i)), waits=[lastq.get(("xfree", i - 2))],
                inc=f"lx{b}")
            tok_mv, tok_rs = emit_ln_tile(prog, xt32[b], 4, bn6, mv, rs, C["eps_ln"],
                                          waits=[lx])
            xc_tok = None
            for c in range(4):
                xc_tok = prog.ve(lambda v, b=b, c=c: v.tensor_scalar(
                    xc[:, c * D:(c + 1) * D], xt32[b][:, c * D:(c + 1) * D],
                    mv[:, 2 * c:2 * c + 1], rs[:, c:c + 1], ALU.subtract, ALU.mult),
                    waits=[tok_mv, tok_rs, lastq.get(("xcfree", i - 1))] if c == 0 else (),
                    inc="vs" if c == 3 else None)
            lastq[("xfree", i)] = xc_tok
            ev_last = None
            for rnd in range(4):
                tb = rnd % 2
                w = [xc_tok] + ([tp_tok[tb]] if tp_tok[tb] else [])
                for j in range(8):
                    blk = rnd * 8 + j
                    db, cc = blk // 4, blk % 4
                    tok_tr = prog.pe(lambda t, tb=tb, j=j, db=db, cc=cc: t.transpose(
                        tpb2[tb][:, j * 128:(j + 1) * 128],
                        xc[:, cc * D + db * 128: cc * D + (db + 1) * 128], id_bf[:]),
                        waits=w if j == 0 else (), inc="ps" if j == 7 else None)
                for j in range(8):
                    blk = rnd * 8 + j
                    db, cc = blk // 4, blk % 4
                    ev_last = prog.se(lambda s, tb=tb, j=j, db=db, cc=cc: s.copy(
                        xT[:, db * 512 + cc * 128: db * 512 + (cc + 1) * 128],
                        tpb2[tb][:, j * 128:(j + 1) * 128]),
                        waits=[tok_tr] if j == 0 else (), inc="ss" if j == 7 else None)
                tp_tok[tb] = ev_last
            xT_done = ev_last
            lastq[("xcfree", i)] = xT_done
            for (wsb, outT, nrow, pq) in ((wq_sb, qT, qn, 0), (wk_sb, kT, kn, 1)):
                for kb in range(8):
                    tok_mm = prog.pe(lambda t, wsb=wsb, pq=pq, kb=kb: t.matmul(
                        pp[pq][:], wsb[:, kb * 128:(kb + 1) * 128],
                        xT[:, kb * 512:(kb + 1) * 512], start=(kb == 0), stop=(kb == 7)),
                        waits=[xT_done, prog.all_tok("ld"),
                               lastq.get(("pev", pq, i - 1))] if kb == 0 else (),
                        inc="ps" if kb == 7 else None)
                ev = prog.se(lambda s, outT=outT, pq=pq, i=i: s.copy(
                    outT[:, i * 512:(i + 1) * 512], pp[pq][:]), waits=[tok_mm], inc="ss")
                lastq[("pev", pq, i)] = ev
                sq_t = prog.se(lambda s, outT=outT, i=i: s.activation(
                    sq[:], outT[:, i * 512:(i + 1) * 512], AF.Square),
                    waits=[ev], inc="ss")
                nm = prog.pe(lambda t: t.matmul(
                    npq[0:1, 0:512], ones_col[:], sq[:], start=True, stop=True),
                    waits=[sq_t, nprev], inc="ps")
                nprev = prog.se(lambda s, nrow=nrow, i=i: s.activation(
                    nrow[0:1, i * 512:(i + 1) * 512], npq[0:1, 0:512], AF.Sqrt),
                    waits=[nm], inc="ss")
            for c in range(4):
                for kb in range(8):
                    tok_mm = prog.pe(lambda t, c=c, kb=kb: t.matmul(
                        pp[2][:, 0:128], xT[:, kb * 512 + c * 128: kb * 512 + (c + 1) * 128],
                        wv_sb[:, kb * 128:(kb + 1) * 128], start=(kb == 0), stop=(kb == 7)),
                        waits=[xT_done, lastq.get(("vev", i, c - 1)),
                               lastq.get(("vev", i - 1, 3))] if kb == 0 else (),
                        inc="ps" if kb == 7 else None)
                vev = prog.se(lambda s, i=i, c=c: s.copy(
                    v_aug[:, (i * 4 + c) * 129:(i * 4 + c) * 129 + 128], pp[2][:, 0:128]),
                    waits=[tok_mm], inc="ss")
                lastq[("vev", i, c)] = vev
        v_all = lastq[("vev", 7, 3)]
        SCL = 0.08838834764831845
        km1 = prog.ve(lambda v: v.reduce_max(kmx[:], kn[:], axis=AX.X),
                      waits=[nprev], inc="vs")
        cb_t = prog.ve(lambda v: v.tensor_scalar(cb[:], qn[:], kmx[0:1, 0:1],
                                                 -1.01 * SCL, ALU.mult, ALU.mult),
                       waits=[km1], inc="vs")

        prevo = {}
        stores = []
        for ab in range(8):
            b = ab % 2
            lx = prog.sync(lambda sy, ab=ab, b=b: sy.dma_start(
                out=xt32[b][:].rearrange("p (c d) -> p c d", c=4),
                in_=x_tile_in(x_in, ab)),
                waits=[prevo.get(("xfree", ab - 2), lastq.get(("xfree", 6 + (ab % 2)))),
                       prog.cur_tok("vs")], inc=f"lx{b}")
            rep_mm = prog.pe(lambda t, ab=ab: t.matmul(
                npq[:, 0:512], ones_row[0:1, 0:128], cb[0:1, ab * 512:(ab + 1) * 512],
                start=True, stop=True),
                waits=[cb_t, v_all, prevo.get("rp_cp"), prog.all_tok("gs")],
                inc="ps")
            cbr_cp = prog.ve(lambda v: v.tensor_copy(cbr_a[:], npq[:, 0:512]),
                             waits=[rep_mm], inc="vs")
            for kb in range(32):
                pq = kb % 2
                w0 = [prevo.get(("sttb", pq))] + ([cb_t] if (ab == 0 and kb == 0) else [])
                mm1 = prog.pe(lambda t, pq=pq, kb=kb, ab=ab: t.matmul(
                    pp[pq][:], kT[:, kb * 128:(kb + 1) * 128],
                    qT[:, ab * 512:(ab + 1) * 512], start=True, stop=True),
                    waits=w0, inc="ps")
                stt = prog.ve(lambda v, pq=pq, kb=kb: v.scalar_tensor_tensor(
                    tmp_a[kb % 2][:], pp[pq][:], SCL, cbr_a[:], ALU.mult, ALU.add),
                    waits=[mm1, prevo.get(("exp", kb % 2))] +
                          ([cbr_cp] if kb == 0 else []), inc="vs")
                ex = prog.se(lambda s, kb=kb: s.activation(
                    AT[:, kb * 512:(kb + 1) * 512], tmp_a[kb % 2][:], AF.Exp),
                    waits=[stt] + ([prevo.get("dnl")] if kb == 0 else []), inc="ss")
                prevo[("sttb", pq)] = stt
                prevo[("exp", kb % 2)] = ex
                prevo[("expk", kb)] = ex
            avt = None
            for kb in range(32):
                w0 = [prevo[("expk", kb)]] + ([prevo.get("yevp")] if kb == 0 else [])
                avt = prog.pe(lambda t, kb=kb: t.matmul(
                    yp_a[:, 0:512], v_aug[:, kb * 129: kb * 129 + 128],
                    AT[:, kb * 512:(kb + 1) * 512], start=(kb == 0), stop=(kb == 31)),
                    waits=w0, inc="ps" if kb == 31 else None)
            dn = None
            for kb in range(32):
                w0 = [prevo.get(("oev", 0)), prog.all_tok("gs")] if kb == 0 else ()
                dn = prog.pe(lambda t, kb=kb: t.matmul(
                    dp_a[0:1, 0:512], ones_col[:, 0:1],
                    AT[:, kb * 512:(kb + 1) * 512], start=(kb == 0), stop=(kb == 31)),
                    waits=w0, inc="ps" if kb == 31 else None)
            prevo["dnl"] = dn
            d0 = prog.ve(lambda v: v.reciprocal(drow_a[0:1, :], dp_a[0:1, 0:512]),
                         waits=[dn, avt], inc="vs")
            d1 = prog.ve(lambda v: v.tensor_copy(drbf_a[0:1, :], drow_a[0:1, :]),
                         inc="vs")
            rep2 = prog.pe(lambda t: t.matmul(
                npq[:, 0:512], ones_row[0:1, 0:128], drbf_a[0:1, 0:512],
                start=True, stop=True), waits=[d1, cbr_cp], inc="ps")
            rp_cp = prog.ve(lambda v: v.tensor_copy(rp_a[:], npq[:, 0:512]),
                            waits=[rep2], inc="vs")
            prevo["rp_cp"] = rp_cp
            yev = prog.ve(lambda v: v.scalar_tensor_tensor(
                yTs_a[:], yp_a[:, 0:512], 1.0, rp_a[:], ALU.mult, ALU.mult),
                waits=[rp_cp, prevo.get("wolast")], inc="vs")
            prevo["yevp"] = yev
            oev = None
            for asub in range(4):
                for nb in range(2):
                    par2 = (asub * 2 + nb) % 2
                    bank = dp_a if par2 == 0 else pp[2]
                    mo = prog.pe(lambda t, asub=asub, nb=nb, bank=bank: t.matmul(
                        bank[:, 0:512], yTs_a[:, asub * 128:(asub + 1) * 128],
                        wo_sb[:, nb * 512:(nb + 1) * 512], start=True, stop=True),
                        waits=[yev, wo_m, prevo.get(("oev", par2))] +
                              ([d0] if (asub == 0 and nb == 0) else []),
                        inc="ps")
                    oev = prog.ve(lambda v, asub=asub, nb=nb, b=b, bank=bank: v.tensor_tensor(
                        ot32[:, asub * D + nb * 512: asub * D + (nb + 1) * 512],
                        bank[:, 0:512],
                        xt32[b][:, asub * D + nb * 512: asub * D + (nb + 1) * 512],
                        ALU.add), waits=[mo, lx] +
                        ([prevo.get(("ofree", ab - 1))] if asub == 0 and nb == 0 else []),
                        inc="vs")
                    prevo[("oev", par2)] = oev
            prevo["wolast"] = oev
            prevo[("xfree", ab)] = oev
            stores.append(prog.sync(lambda sy, ab=ab: sy.dma_start(
                out=x_tile_in(x_out, ab), in_=ot32[:].rearrange("p (c d) -> p c d", c=4)),
                waits=[oev], inc=f"st{b}"))
            prevo[("ofree", ab)] = stores[-1]
        prog.sync(lambda sy: sy.nop(), waits=[stores[-1], stores[-2]])
        _fin(nc, prog, stack)
    return nc


def build_phase_l1_msa():
    nc = _mknc()
    x_in = _dram(nc, "x_in", [S, D], F32, "ExternalInput")
    x2i_o = _dram(nc, "x2i", [S, D], F32, "ExternalOutput")
    s_o = _dram(nc, "s_out", [S, D], F32, "ExternalOutput")
    wqkv = _dram(nc, "w_qkv", [D, 3 * D], F32, "ExternalInput")
    wproj = _dram(nc, "w_proj", [D, D], F32, "ExternalInput")
    ln1w = _dram(nc, "ln1_w", [1, D], F32, "ExternalInput")
    ln1b = _dram(nc, "ln1_b", [1, D], F32, "ExternalInput")
    t0_d = _dram(nc, "z_t0", [S, D], F32, "Internal")
    t1T_d = _dram(nc, "z_t1T", [8, 128, 8, 512], BF16, "Internal")
    yT_d = _dram(nc, "z_yT", [8, 128, 8, 512], BF16, "Internal")

    prog = _mkprog()
    with ExitStack() as stack:
        sb = lambda n, s, d: stack.enter_context(nc.sbuf_tensor(_nm(n), s, d))
        psal = lambda n, s, d: stack.enter_context(nc.psum_tensor(_nm(n), s, d))
        C = declare_constants(nc, stack, prog, "ld",
                              need=("id_bf", "ones_row", "ones_row32", "ones_col",
                                    "eps_ln"))
        id_bf, ones_row, ones_col = C["id_bf"], C["ones_row"], C["ones_col"]
        ones_row32 = C["ones_row32"]

        qTg = sb("qTg", [128, 2 * S], BF16)
        kTg = sb("kTg", [128, 2 * S], BF16)
        vag = sb("vag", [128, 32 * 260], BF16)
        AT = sb("AT", [128, 32 * 512], BF16)
        wqg = sb("wqg", [128, 8 * 768], BF16)
        wpj = sb("wpj", [128, 8 * D], BF16)
        slab = sb("slab", [128, 8 * 512], BF16)
        sqg = sb("sqg", [128, 4096], BF16)
        cb = sb("cb", [1, S], BF16)
        knrow = sb("knrow", [1, 512], F32)
        kmx = sb("kmx", [1, 1], F32)
        kmr = sb("kmr", [1, 8], F32)
        xt32 = sb("xt32", [128, 4 * D], F32)
        t0b = sb("t0b", [128, 4 * D], F32)
        t1b = sb("t1b", [128, 4 * D], BF16)
        g1f = sb("g1f", [128, D], F32)
        b1f = sb("b1f", [128, D], F32)
        yT_sb = sb("yT_sb", [128, 2 * 512], BF16)
        drow = sb("drow", [1, 512], F32)
        drow_bf = sb("drow_bf", [1, 512], BF16)
        x2b = sb("x2b", [128, D], F32)
        s_b = sb("s_b", [128, D], F32)
        t0c = sb("t0c", [128, D], F32)
        x4c = sb("x4c", [128, D], F32)
        bn6 = sb("bn6", [128, 48], F32)
        mv = sb("mv", [128, 8], F32)
        rs = sb("rs", [128, 4], F32)
        tmp_e = [t0b[:, 0:512], t0b[:, 512:1024]]
        cbr_sb = t0b[:, 1024:1536]
        rp64 = t0b[0:64, 1536:2048]

        pp = [psal(f"pp{j}", [128, 512], F32) for j in range(4)]
        yp = [psal(f"yp{j}", [128, 512], F32) for j in range(2)]
        tpb = psal("tpb0", [128, 1024], BF16)
        npq = psal("npq", [128, 512], F32)

        lw_t = prog.gp(lambda g: g.dma_start(out=g1f[0:1, :], in_=ln1w), inc="ld", n=16)
        lb_t = prog.gp(lambda g: g.dma_start(out=b1f[0:1, :], in_=ln1b), inc="ld", n=16)
        reptoks = []
        for (dst,) in ((b1f,), (g1f,)):
            for jj in range(2):
                w = [prog.all_tok("ld"), prog.all_tok("gs")] + ([reptoks[-1]] if reptoks else [])
                tok_mm = prog.pe(lambda t, jj=jj, dst=dst: t.matmul(
                    npq[:], ones_row32[0:1, 0:128], dst[0:1, jj * 512:(jj + 1) * 512],
                    start=True, stop=True), waits=w, inc="ps")
                reptoks.append(prog.ve(lambda v, jj=jj, dst=dst: v.tensor_copy(
                    dst[:, jj * 512:(jj + 1) * 512], npq[:]), waits=[tok_mm], inc="vs"))
        rep_done = reptoks[-1]
        ones_aug = prog.gp(lambda g: g.memset(
            vag[:].rearrange("p (c h x) -> p (c h) x", c=32, h=4)[:, :, 64:65], 1.0),
            inc="gs", n=1)
        wp_t = prog.gp(lambda g_: g_.dma_start(
            out=wpj[:].rearrange("p (k n) -> p k n", k=8),
            in_=wproj.rearrange("(k p) n -> p k n", p=128)), inc="wp", n=16)

        tp_tok = [None]
        pv = {}
        for i in range(8):
            lx = prog.sync(lambda sy, i=i: sy.dma_start(
                out=xt32[:].rearrange("p (c d) -> p c d", c=4),
                in_=x_tile_in(x_in, i)), waits=[pv.get(("xfree",))], inc="lx0")
            tok_mv, tok_rs = emit_ln_tile(prog, xt32, 4, bn6, mv, rs, C["eps_ln"],
                                          waits=[lx])
            t0_tok = None
            for c in range(4):
                t0_tok = prog.ve(lambda v, c=c: v.tensor_scalar(
                    t0b[:, c * D:(c + 1) * D], xt32[:, c * D:(c + 1) * D],
                    mv[:, 2 * c:2 * c + 1], rs[:, c:c + 1], ALU.subtract, ALU.mult),
                    waits=[tok_mv, tok_rs, pv.get(("t0free",)),
                           pv.get(("st0prev",))] if c == 0 else (),
                    inc="vs" if c == 3 else None)
            st0 = prog.sync(lambda sy, i=i: sy.dma_start(
                out=x_tile_in(t0_d, i), in_=t0b[:].rearrange("p (c d) -> p c d", c=4)),
                waits=[t0_tok], inc="st0")
            pv[("st0prev",)] = st0
            tok_mv2, tok_rs2 = emit_ln_tile(prog, t0b, 4, bn6, mv, rs, C["eps_ln"],
                                            waits=[t0_tok])
            t1_tok = None
            for c in range(4):
                a1 = prog.ve(lambda v, c=c: v.tensor_scalar(
                    xt32[:, c * D:(c + 1) * D], t0b[:, c * D:(c + 1) * D],
                    mv[:, 2 * c:2 * c + 1], rs[:, c:c + 1], ALU.subtract, ALU.mult),
                    waits=[tok_mv2, tok_rs2, rep_done] if c == 0 else (), inc="vs")
                a2 = prog.ve(lambda v, c=c: v.tensor_tensor(
                    xt32[:, c * D:(c + 1) * D], xt32[:, c * D:(c + 1) * D], g1f[:],
                    ALU.mult), waits=[a1], inc="vs")
                t1_tok = prog.ve(lambda v, c=c: v.scalar_tensor_tensor(
                    t1b[:, c * D:(c + 1) * D], xt32[:, c * D:(c + 1) * D], 0.0,
                    b1f[:], ALU.bypass, ALU.add), waits=[a2], inc="vs")
            pv[("t0free",)] = t1_tok
            pv[("xfree",)] = t1_tok
            ev_last = None
            for rnd in range(4):
                w = [t1_tok] + ([tp_tok[0]] if tp_tok[0] else [])
                for j in range(8):
                    blk = rnd * 8 + j
                    db, cc = blk // 4, blk % 4
                    tok_tr = prog.pe(lambda t, j=j, db=db, cc=cc: t.transpose(
                        tpb[:, j * 128:(j + 1) * 128],
                        t1b[:, cc * D + db * 128: cc * D + (db + 1) * 128], id_bf[:]),
                        waits=w if j == 0 else (), inc="ps" if j == 7 else None)
                for j in range(8):
                    blk = rnd * 8 + j
                    db, cc = blk // 4, blk % 4
                    ev_last = prog.ve(lambda v, j=j, db=db, cc=cc: v.tensor_copy(
                        slab[:, db * 512 + cc * 128: db * 512 + (cc + 1) * 128],
                        tpb[:, j * 128:(j + 1) * 128]),
                        waits=[tok_tr, pv.get(("slabfree",))] if j == 0 else (),
                        inc="vs" if j == 7 else None)
                tp_tok[0] = ev_last
            st1 = prog.sync(lambda sy, i=i: sy.dma_start(
                out=t1T_d[i], in_=slab[:].rearrange("p (k t) -> p k t", k=8)),
                waits=[ev_last], inc="st1")
            pv[("slabfree",)] = st1
        pre_done = [pv[("slabfree",)], prog.cur_tok("st0")]

        SCL = 0.125
        att = {}
        qkv_ev = {}
        qkv_done = []
        for g in range(4):
            prev_qkv = qkv_done if g > 0 else []
            for ti in range(3):
                wq_t = prog.gp(lambda g_, g=g, ti=ti: g_.dma_start(
                    out=wqg[:].rearrange("p (k three hh) -> p k three hh", three=3,
                                         hh=256)[:, :, ti, :],
                    in_=wqkv.rearrange("(k p) (three n) -> p k three n", p=128,
                                       three=3)[:, :, ti, g * 256:(g + 1) * 256]),
                    waits=list(prev_qkv) if ti == 0 else [], inc="wq", n=16)
            qkv_ev[("slabfree", -1)] = qkv_ev.get(("slabfree", 7))
            for t8 in range(8):
                lsl = prog.sync(lambda sy, t8=t8: sy.dma_start(
                    out=slab[:].rearrange("p (k t) -> p k t", k=8), in_=t1T_d[t8]),
                    waits=pre_done + (qkv_ev.get(("slabfree", t8 - 1)) or []), inc="lx0")
                for (which, outT, base, pqo) in (("q", qTg, 0, 0), ("k", kTg, 256, 2)):
                    for pb in range(2):
                        pq = pqo + pb
                        for kb in range(8):
                            tok_mm = prog.pe(lambda t, pq=pq, base=base, pb=pb, kb=kb: t.matmul(
                                pp[pq][:], wqg[:, kb * 768 + base + pb * 128: kb * 768 + base + (pb + 1) * 128],
                                slab[:, kb * 512:(kb + 1) * 512], start=(kb == 0), stop=(kb == 7)),
                                waits=[lsl, wq_t, qkv_ev.get((which, pb, t8 - 1)),
                                       att.get(("sttb", pq))] if kb == 0 else (),
                                inc="ps" if kb == 7 else None)
                        ev = prog.se(lambda s, outT=outT, pq=pq, pb=pb, t8=t8: s.copy(
                            outT[:, pb * S + t8 * 512:(pb * S) + (t8 + 1) * 512], pp[pq][:]),
                            waits=[tok_mm], inc="ss")
                        qkv_ev[(which, pb, t8)] = ev
                for c in range(4):
                    pq = c % 2
                    for kb in range(8):
                        tok_mm = prog.pe(lambda t, c=c, kb=kb, pq=pq: t.matmul(
                            pp[pq][:, 0:256], slab[:, kb * 512 + c * 128: kb * 512 + (c + 1) * 128],
                            wqg[:, kb * 768 + 512:(kb + 1) * 768], start=(kb == 0), stop=(kb == 7)),
                            waits=[lsl, qkv_ev.get(("q", pq, t8)),
                                   qkv_ev.get(("v", t8, c - 2),
                                               qkv_ev.get(("v", t8 - 1, 2 + (c % 2))))] if kb == 0 else (),
                            inc="ps" if kb == 7 else None)
                    vev = prog.se(lambda s, t8=t8, c=c, pq=pq: s.copy(
                        vag[:].rearrange("p (ch h x) -> p ch h x", ch=32, h=4)
                        [:, t8 * 4 + c, :, 0:64],
                        pp[pq][:, 0:256].rearrange("p (h x) -> p h x", x=64)),
                        waits=[tok_mm], inc="ss")
                    qkv_ev[("v", t8, c)] = vev
                qkv_ev[("slabfree", t8)] = [qkv_ev[("v", t8, 3)], qkv_ev[("v", t8, 2)],
                                            qkv_ev[("q", 1, t8)], qkv_ev[("k", 1, t8)],
                                            qkv_ev[("q", 0, t8)], qkv_ev[("k", 0, t8)]]
            qkv_done = [qkv_ev[("q", 1, 7)], qkv_ev[("k", 1, 7)], qkv_ev[("v", 7, 3)]]

            for j in range(4):
                hrow = (j % 2) * 64
                hcol = (j // 2) * S
                db = g * 2 + j // 2
                nprev = None
                sqk = prog.se(lambda s, hrow=hrow, hcol=hcol: s.activation(
                    sqg[64:128, :], kTg[hrow:hrow + 64, hcol:hcol + S], AF.Square),
                    waits=qkv_done + [att.get("nm_last")], inc="ss")
                for t8 in range(8):
                    nm = prog.pe(lambda t, t8=t8: t.matmul(
                        npq[0:1, 0:512], ones_col[64:128, 0:1],
                        sqg[64:128, t8 * 512:(t8 + 1) * 512], start=True, stop=True),
                        waits=[sqk, nprev, att.get("rp_cp")] if t8 == 0 else [sqk, nprev],
                        inc="ps")
                    sq_ev = prog.se(lambda s: s.activation(
                        knrow[0:1, :], npq[0:1, 0:512], AF.Sqrt), waits=[nm], inc="ss")
                    nprev = prog.ve(lambda v, t8=t8: v.reduce_max(
                        kmr[0:1, t8:t8 + 1], knrow[0:1, :], axis=AX.X),
                        waits=[sq_ev], inc="vs")
                km1 = prog.ve(lambda v: v.reduce_max(kmx[:], kmr[:], axis=AX.X),
                              waits=[nprev], inc="vs")
                sqq = prog.se(lambda s, hrow=hrow, hcol=hcol: s.activation(
                    sqg[0:64, :], qTg[hrow:hrow + 64, hcol:hcol + S], AF.Square),
                    waits=qkv_done, inc="ss")
                cb_done = None
                nprev2 = None
                nm_last = None
                for t8 in range(8):
                    nm_last = prog.pe(lambda t, t8=t8: t.matmul(
                        npq[0:1, 0:512], ones_col[0:64, 0:1],
                        sqg[0:64, t8 * 512:(t8 + 1) * 512], start=True, stop=True),
                        waits=[sqq, km1, nprev2], inc="ps")
                    sq_ev = prog.se(lambda s: s.activation(
                        knrow[0:1, :], npq[0:1, 0:512], AF.Sqrt), waits=[nm_last], inc="ss")
                    cb_done = prog.ve(lambda v, t8=t8: v.tensor_scalar(
                        cb[0:1, t8 * 512:(t8 + 1) * 512], knrow[0:1, :],
                        kmx[0:1, 0:1], -1.01 * SCL, ALU.mult, ALU.mult),
                        waits=[sq_ev], inc="vs")
                    nprev2 = cb_done
                att["nm_last"] = nm_last

                for ab in range(8):
                    par = ab % 2
                    rep_mm = prog.pe(lambda t, ab=ab: t.matmul(
                        npq[:, 0:512], ones_row[0:1, 0:128],
                        cb[0:1, ab * 512:(ab + 1) * 512], start=True, stop=True),
                        waits=[cb_done, att.get("rp_cp"), prog.all_tok("gs")],
                        inc="ps")
                    cbr_cp = prog.ve(lambda v: v.tensor_copy(cbr_sb, npq[:, 0:512]),
                                     waits=[rep_mm], inc="vs")
                    for kb in range(32):
                        pq = kb % 2
                        w0 = [att.get(("sttb", pq))] + ([cb_done] if kb == 0 else [])
                        mm = prog.pe(lambda t, pq=pq, kb=kb, ab=ab, hrow=hrow, hcol=hcol: t.matmul(
                            pp[pq][:], kTg[hrow:hrow + 64, hcol + kb * 128: hcol + (kb + 1) * 128],
                            qTg[hrow:hrow + 64, hcol + ab * 512: hcol + (ab + 1) * 512],
                            start=True, stop=True), waits=w0, inc="ps")
                        stt = prog.ve(lambda v, pq=pq, kb=kb: v.scalar_tensor_tensor(
                            tmp_e[kb % 2], pp[pq][:], SCL, cbr_sb, ALU.mult, ALU.add),
                            waits=[mm, att.get(("exp", kb % 2))] +
                                  ([cbr_cp] if kb == 0 else []), inc="vs")
                        ex = prog.se(lambda s, kb=kb: s.activation(
                            AT[:, kb * 512:(kb + 1) * 512], tmp_e[kb % 2], AF.Exp),
                            waits=[stt] + ([att.get("avdone")] if kb == 0 else []),
                            inc="ss")
                        att[("sttb", pq)] = stt
                        att[("exp", kb % 2)] = ex
                        att[("expk", kb)] = ex
                    avt = None
                    for kb in range(32):
                        w0 = [att[("expk", kb)]] + \
                            ([att.get(("yev", par)), prog.all_tok("gs")] if kb == 0 else [])
                        avt = prog.pe(lambda t, kb=kb, par=par, j=j: t.matmul(
                            yp[par][0:65, 0:512],
                            vag[:, kb * 260 + j * 65: kb * 260 + (j + 1) * 65],
                            AT[:, kb * 512:(kb + 1) * 512],
                            start=(kb == 0), stop=(kb == 31)),
                            waits=w0, inc="ps" if kb == 31 else None)
                    att["avdone"] = avt
                    d0 = prog.ve(lambda v, par=par: v.reciprocal(
                        drow[0:1, :], yp[par][64:65, 0:512]), waits=[avt], inc="vs")
                    d1 = prog.ve(lambda v: v.tensor_copy(drow_bf[0:1, :], drow[0:1, :]),
                                 inc="vs")
                    rep2 = prog.pe(lambda t: t.matmul(
                        npq[0:64, 0:512], ones_row[0:1, 0:64], drow_bf[0:1, 0:512],
                        start=True, stop=True), waits=[d1, cbr_cp], inc="ps")
                    rp_cp = prog.ve(lambda v: v.tensor_copy(rp64, npq[0:64, 0:512]),
                                    waits=[rep2], inc="vs")
                    yev = prog.ve(lambda v, par=par: v.scalar_tensor_tensor(
                        yT_sb[0:64, par * 512:(par + 1) * 512], yp[par][0:64, 0:512],
                        1.0, rp64, ALU.mult, ALU.mult),
                        waits=[rp_cp, att.get(("yst", par))], inc="vs")
                    yst = prog.gp(lambda g_, db=db, j=j, ab=ab, par=par: g_.dma_start(
                        out=yT_d[db, (j % 2) * 64:(j % 2) * 64 + 64, ab, :],
                        in_=yT_sb[0:64, par * 512:(par + 1) * 512]),
                        waits=[yev], inc="yw", n=16)
                    att[("yev", par)] = yev
                    att[("yst", par)] = yst
                    att["rp_cp"] = rp_cp

        prevc = {}
        stores = []
        for ab in range(8):
            parb = ab % 2
            ysl = slab if parb == 0 else sqg
            lyT = prog.sync(lambda sy, ab=ab, ysl=ysl: sy.dma_start(
                out=ysl[:].rearrange("p (k t) -> p k t", k=8),
                in_=yT_d[:, :, ab, :].rearrange("db p t -> p db t")),
                waits=([prog.all_tok("yw")] if ab < 2 else [prevc[("sfree", ab - 2)]]),
                inc="lx0")
            mo = None
            for ct4 in range(4):
                ch = ab * 4 + ct4
                lt0 = prog.sync(lambda sy, ch=ch: sy.dma_start(
                    out=t0c[:], in_=t0_d[ch * 128:(ch + 1) * 128, :]),
                    waits=[prevc.get(("t0free", ch - 1))], inc="lx1")
                lx4 = prog.gp(lambda g_, ch=ch: g_.dma_start(
                    out=x4c[:], in_=x_in[ch * 128:(ch + 1) * 128, :]),
                    waits=[prevc.get(("x4free", ch - 1))], inc="lx2", n=16)
                for nb in range(2):
                    pq = nb
                    for kb in range(8):
                        mo = prog.pe(lambda t, pq=pq, nb=nb, kb=kb, ct4=ct4, ysl=ysl: t.matmul(
                            pp[pq][:], ysl[:, kb * 512 + ct4 * 128: kb * 512 + ct4 * 128 + 128],
                            wpj[:, kb * D + nb * 512: kb * D + (nb + 1) * 512],
                            start=(kb == 0), stop=(kb == 7)),
                            waits=[lyT, prog.all_tok("wp"), prevc.get(("oev", pq)),
                                   att.get(("sttb", pq))] if kb == 0 else (),
                            inc="ps" if kb == 7 else None)
                    x2t = prog.ve(lambda v, pq=pq, nb=nb: v.tensor_tensor(
                        x2b[:, nb * 512:(nb + 1) * 512], pp[pq][:],
                        t0c[:, nb * 512:(nb + 1) * 512], ALU.add),
                        waits=[mo, lt0] + ([prevc.get(("x2free", ch - 1))] if nb == 0 else []),
                        inc="vs")
                    stok = prog.ve(lambda v, nb=nb: v.tensor_tensor(
                        s_b[:, nb * 512:(nb + 1) * 512], x2b[:, nb * 512:(nb + 1) * 512],
                        x4c[:, nb * 512:(nb + 1) * 512], ALU.add),
                        waits=[lx4] + ([prevc.get(("sbfree", ch - 1))] if nb == 0 else []),
                        inc="vs")
                    prevc[("oev", pq)] = stok
                st1 = prog.sync(lambda sy, ch=ch: sy.dma_start(
                    out=x2i_o[ch * 128:(ch + 1) * 128, :], in_=x2b[:]),
                    waits=[prevc[("oev", 1)]], inc=f"so{ch % 2}")
                st2 = prog.sync(lambda sy, ch=ch: sy.dma_start(
                    out=s_o[ch * 128:(ch + 1) * 128, :], in_=s_b[:]),
                    waits=[prevc[("oev", 1)]], inc=f"sq{ch % 2}")
                prevc[("t0free", ch)] = prevc[("oev", 0)]
                prevc[("x4free", ch)] = prevc[("oev", 1)]
                prevc[("x2free", ch)] = st1
                prevc[("sbfree", ch)] = st2
                stores.append(st2)
            prevc[("sfree", ab)] = mo
        prog.sync(lambda sy: sy.nop(),
                  waits=[prog.cur_tok("so0"), prog.cur_tok("so1"),
                         prog.cur_tok("sq0"), prog.cur_tok("sq1")])
        _fin(nc, prog, stack)
    return nc


def build_phase_l1_moe():
    nc = _mknc()
    x2i_i = _dram(nc, "x2i", [S, D], F32, "ExternalInput")
    s_i = _dram(nc, "s_in", [S, D], F32, "ExternalInput")
    x_out = _dram(nc, "x_out", [S, D], F32, "ExternalOutput")
    gwd = _dram(nc, "gate_w", [D, 8], F32, "ExternalInput")
    gbd = _dram(nc, "gate_b", [1, 8], F32, "ExternalInput")
    w1d = _dram(nc, "w1", [8, D, DFF], F32, "ExternalInput")
    w2d = _dram(nc, "w2", [8, DFF, D], F32, "ExternalInput")
    ln2w = _dram(nc, "ln2_w", [1, D], F32, "ExternalInput")
    ln2b = _dram(nc, "ln2_b", [1, D], F32, "ExternalInput")
    w1bf = _dram(nc, "z_w1bf", [8, D, DFF], BF16, "Internal")
    w2bf = _dram(nc, "z_w2bf", [8, DFF, D], BF16, "Internal")
    t2T_d = _dram(nc, "z_t2T", [8, 128, 8, 512], BF16, "Internal")

    prog = _mkprog()
    with ExitStack() as stack:
        sb = lambda n, s, d: stack.enter_context(nc.sbuf_tensor(_nm(n), s, d))
        psal = lambda n, s, d: stack.enter_context(nc.psum_tensor(_nm(n), s, d))
        C = declare_constants(nc, stack, prog, "ld",
                              need=("id_bf", "id32", "ones_row", "ones_row32",
                                    "eps_ln"))
        id_bf, id32 = C["id_bf"], C["id32"]
        ones_row, ones_row32 = C["ones_row"], C["ones_row32"]

        y0 = sb("y0", [128, 8 * D], F32)
        t2q = sb("t2q", [128, 8192], BF16)
        pool = sb("poolm", [128, 32768], BF16)
        w1h = pool[:, 0:16384]
        w2h = pool[:, 16384:32768]
        t2b = pool[:, 0:4096]
        slab = pool[:, 4096:8192]
        hTh = sb("hTh", [128, 16 * D], BF16)
        stg = [sb(f"stg{j}", [128, 2048], BF16) for j in range(2)]
        mask_sb = sb("mask_sb", [128, 32 * 8], F32)
        lgs = sb("lgs", [128, 32], F32)
        lgm = sb("lgm", [128, 1], F32)
        gw_f = sb("gw_f", [128, 8 * 8], F32)
        gw_hi = sb("gw_hi", [128, 8 * 8], BF16)
        gw_lo = sb("gw_lo", [128, 8 * 8], BF16)
        gw_tmp = sb("gw_tmp", [128, 8 * 8], F32)
        t2l = sb("t2l", [128, 4 * D], BF16)
        gb_sb = sb("gb_sb", [1, 8], BF16)
        gb_f = sb("gb_f", [1, 8], F32)
        gb_lo = sb("gb_lo", [1, 8], BF16)
        gb_tmp = sb("gb_tmp", [1, 8], F32)
        g2f = sb("g2f", [128, D], F32)
        b2f = sb("b2f", [128, D], F32)
        xt32 = sb("xt32", [128, 4 * D], F32)
        lgrow = sb("lgrow", [8, 512], F32)
        bn6 = sb("bn6", [128, 48], F32)
        mv = sb("mv", [128, 8], F32)
        rs = sb("rs", [128, 4], F32)

        pp = [psal(f"pp{j}", [128, 512], F32) for j in range(4)]
        tpb2 = [psal(f"tpb{j}", [128, 1024], BF16) for j in range(2)]
        npq = psal("npq", [128, 512], F32)
        gps = psal("gps", [128, 512], F32)

        gw_t0 = prog.gp(lambda g: g.dma_start(
            out=gw_f[:].rearrange("p (k n) -> p k n", k=8),
            in_=gwd.rearrange("(k p) n -> p k n", p=128)), inc="ld", n=16)
        gh1 = prog.ve(lambda v: v.tensor_copy(gw_hi[:], gw_f[:]), waits=[prog.all_tok("ld")], inc="vs")
        gh2 = prog.ve(lambda v: v.tensor_tensor(gw_tmp[:], gw_f[:], gw_hi[:],
                                                ALU.subtract), waits=[gh1], inc="vs")
        gw_t = prog.ve(lambda v: v.tensor_copy(gw_lo[:], gw_tmp[:]), waits=[gh2], inc="vs")
        gb_t = prog.gp(lambda g: g.dma_start(out=gb_f[:], in_=gbd), inc="ld", n=16)
        gb1 = prog.ve(lambda v: v.tensor_copy(gb_sb[:], gb_f[:]),
                      waits=[prog.all_tok("ld")], inc="vs")
        gb2 = prog.ve(lambda v: v.tensor_tensor(gb_tmp[:], gb_f[:], gb_sb[:],
                                                ALU.subtract), waits=[gb1], inc="vs")
        gb3 = prog.ve(lambda v: v.tensor_copy(gb_lo[:], gb_tmp[:]), waits=[gb2], inc="vs")
        l2w_t = prog.gp(lambda g: g.dma_start(out=g2f[0:1, :], in_=ln2w), inc="ld", n=16)
        l2b_t = prog.gp(lambda g: g.dma_start(out=b2f[0:1, :], in_=ln2b), inc="ld", n=16)
        reptoks = []
        for dst in (b2f, g2f):
            for jj in range(2):
                w = [prog.all_tok("ld"), prog.all_tok("gs")] + ([reptoks[-1]] if reptoks else [])
                tok_mm = prog.pe(lambda t, jj=jj, dst=dst: t.matmul(
                    npq[:], ones_row32[0:1, 0:128], dst[0:1, jj * 512:(jj + 1) * 512],
                    start=True, stop=True), waits=w, inc="ps")
                reptoks.append(prog.ve(lambda v, jj=jj, dst=dst: v.tensor_copy(
                    dst[:, jj * 512:(jj + 1) * 512], npq[:]), waits=[tok_mm], inc="vs"))
        rep_done = reptoks[-1]

        pw_last = [None, None]

        def prep(din, dout, e, rows, cols):
            ncol = max(1, cols // 2048)
            ccw = min(cols, 2048)
            for rb in range(rows // 128):
                for cb in range(ncol):
                    j = (rb * ncol + cb) % 2
                    lt = prog.gp(lambda g, rb=rb, cb=cb, j=j, ccw=ccw, e=e: g.dma_start(
                        out=stg[j][:, 0:ccw],
                        in_=din[e][rb * 128:(rb + 1) * 128, cb * ccw:(cb + 1) * ccw]),
                        waits=[pw_last[j]], inc=f"pl{j}", n=16)
                    pw_last[j] = prog.sync(lambda sy, rb=rb, cb=cb, j=j, ccw=ccw, e=e: sy.dma_start(
                        out=dout[e][rb * 128:(rb + 1) * 128, cb * ccw:(cb + 1) * ccw],
                        in_=stg[j][:, 0:ccw]), waits=[lt], inc=f"pw{j}")
            return list(pw_last)

        for e in range(8):
            prep(w1d, w1bf, e, D, DFF)
            prep_done = prep(w2d, w2bf, e, DFF, D)

        tp_tok = [None, None]
        pv = {}
        for i in range(8):
            lx = prog.sync(lambda sy, i=i: sy.dma_start(
                out=xt32[:].rearrange("p (c d) -> p c d", c=4),
                in_=x_tile_in(x2i_i, i)), waits=[pv.get(("xfree",))], inc="lx0")
            tok_mv, tok_rs = emit_ln_tile(prog, xt32, 4, bn6, mv, rs, C["eps_ln"],
                                          waits=[lx])
            t2_tok = None
            for c in range(4):
                a1 = prog.ve(lambda v, c=c: v.tensor_scalar(
                    xt32[:, c * D:(c + 1) * D], xt32[:, c * D:(c + 1) * D],
                    mv[:, 2 * c:2 * c + 1], rs[:, c:c + 1], ALU.subtract, ALU.mult),
                    waits=[tok_mv, tok_rs, rep_done] if c == 0 else (), inc="vs")
                a2 = prog.ve(lambda v, c=c: v.tensor_tensor(
                    xt32[:, c * D:(c + 1) * D], xt32[:, c * D:(c + 1) * D], g2f[:],
                    ALU.mult), waits=[a1], inc="vs")
                a3 = prog.ve(lambda v, c=c: v.scalar_tensor_tensor(
                    xt32[:, c * D:(c + 1) * D], xt32[:, c * D:(c + 1) * D], 0.0,
                    b2f[:], ALU.bypass, ALU.add), waits=[a2], inc="vs")
                a4 = prog.ve(lambda v, c=c: v.tensor_copy(
                    t2b[:, c * D:(c + 1) * D], xt32[:, c * D:(c + 1) * D]),
                    waits=[a3, pv.get(("t2bfree",))] if c == 0 else [a3], inc="vs")
                t2_tok = prog.ve(lambda v, c=c: v.tensor_tensor(
                    t2l[:, c * D:(c + 1) * D], xt32[:, c * D:(c + 1) * D],
                    t2b[:, c * D:(c + 1) * D], ALU.subtract), waits=[a4], inc="vs")
            pv[("xfree",)] = t2_tok
            ev_last = None
            for rnd in range(4):
                tb = rnd % 2
                w = [t2_tok] + ([tp_tok[tb]] if tp_tok[tb] else [])
                for j in range(8):
                    blk = rnd * 8 + j
                    db, cc = blk // 4, blk % 4
                    tok_tr = prog.pe(lambda t, tb=tb, j=j, db=db, cc=cc: t.transpose(
                        tpb2[tb][:, j * 128:(j + 1) * 128],
                        t2b[:, cc * D + db * 128: cc * D + (db + 1) * 128], id_bf[:]),
                        waits=w if j == 0 else (), inc="ps" if j == 7 else None)
                for j in range(8):
                    blk = rnd * 8 + j
                    db, cc = blk // 4, blk % 4
                    ev_last = prog.ve(lambda v, tb=tb, j=j, db=db, cc=cc: v.tensor_copy(
                        slab[:, db * 512 + cc * 128: db * 512 + (cc + 1) * 128],
                        tpb2[tb][:, j * 128:(j + 1) * 128]),
                        waits=[tok_tr, pv.get(("slabfree",))] if j == 0 else (),
                        inc="vs" if j == 7 else None)
                tp_tok[tb] = ev_last
            st1 = prog.sync(lambda sy, i=i: sy.dma_start(
                out=t2T_d[i], in_=slab.rearrange("p (k t) -> p k t", k=8)),
                waits=[ev_last], inc="st1")
            for kb in range(8):
                prog.pe(lambda t, kb=kb: t.matmul(
                    gps[0:8, 0:512], gw_hi[:, kb * 8:(kb + 1) * 8],
                    slab[:, kb * 512:(kb + 1) * 512], start=(kb == 0), stop=False),
                    waits=[ev_last, gw_t, pv.get(("gpsfree",))] if kb == 0 else ())
            for kb in range(8):
                glt_hi = prog.pe(lambda t, kb=kb: t.matmul(
                    gps[0:8, 0:512], gw_lo[:, kb * 8:(kb + 1) * 8],
                    slab[:, kb * 512:(kb + 1) * 512], start=False, stop=False),
                    inc="ps" if kb == 7 else None)
            evlo = None
            for rnd in range(4):
                tb = rnd % 2
                w = [t2_tok, st1, glt_hi] + ([tp_tok[tb]] if tp_tok[tb] else [])
                for j in range(8):
                    blk = rnd * 8 + j
                    db, cc = blk // 4, blk % 4
                    tok_tr = prog.pe(lambda t, tb=tb, j=j, db=db, cc=cc: t.transpose(
                        tpb2[tb][:, j * 128:(j + 1) * 128],
                        t2l[:, cc * D + db * 128: cc * D + (db + 1) * 128], id_bf[:]),
                        waits=w if j == 0 else (), inc="ps" if j == 7 else None)
                for j in range(8):
                    blk = rnd * 8 + j
                    db, cc = blk // 4, blk % 4
                    evlo = prog.ve(lambda v, tb=tb, j=j, db=db, cc=cc: v.tensor_copy(
                        slab[:, db * 512 + cc * 128: db * 512 + (cc + 1) * 128],
                        tpb2[tb][:, j * 128:(j + 1) * 128]),
                        waits=[tok_tr] if j == 0 else (), inc="vs" if j == 7 else None)
                tp_tok[tb] = evlo
            pv[("t2bfree",)] = evlo
            for kb in range(8):
                prog.pe(lambda t, kb=kb: t.matmul(
                    gps[0:8, 0:512], gw_hi[:, kb * 8:(kb + 1) * 8],
                    slab[:, kb * 512:(kb + 1) * 512], start=False, stop=False),
                    waits=[evlo] if kb == 0 else ())
            prog.pe(lambda t: t.matmul(
                gps[0:8, 0:512], gb_sb[0:1, :], ones_row[0:1, :],
                start=False, stop=False), waits=[gb_t if False else None])
            glt = prog.pe(lambda t: t.matmul(
                gps[0:8, 0:512], gb_lo[0:1, :], ones_row[0:1, :],
                start=False, stop=True), waits=[gb3], inc="ps")
            pv[("slabfree",)] = glt
            gle = prog.se(lambda s: s.copy(lgrow[:], gps[0:8, 0:512]),
                          waits=[glt, pv.get(("lgrowfree",))], inc="ss")
            for c in range(4):
                ttr = prog.pe(lambda t, c=c: t.transpose(
                    npq[0:128, c * 8:(c + 1) * 8], lgrow[:, c * 128:(c + 1) * 128],
                    id32[0:8, 0:8]), waits=[gle, pv.get(("lgfree",))] if c == 0 else (),
                    inc="ps" if c == 3 else None)
            lge = prog.ve(lambda v: v.tensor_copy(lgs[:], npq[0:128, 0:32]),
                          waits=[ttr], inc="vs")
            pv[("gpsfree",)] = lge
            mtoks = []
            for c in range(4):
                mm1 = prog.ve(lambda v, c=c: v.reduce_max(
                    lgm[:], lgs[:, c * 8:(c + 1) * 8], axis=AX.X),
                    waits=[lge] if c == 0 else [mtoks[-1]], inc="vs")
                mtoks.append(prog.ve(lambda v, c=c, i=i: v.tensor_scalar(
                    mask_sb[:, (i * 4 + c) * 8:(i * 4 + c + 1) * 8],
                    lgs[:, c * 8:(c + 1) * 8], lgm[:], None, ALU.is_equal),
                    waits=[mm1], inc="vs"))
            pv[("lgfree",)] = mtoks[-1]
            pv[("lgrowfree",)] = ttr
        tiles_done = [pv[("slabfree",)], pv[("lgfree",)]]

        mm_prev = {}
        ostores = []
        for q in range(4):
            l2a = prog.sync(lambda sy, q=q: sy.dma_start(
                out=t2q[:, 0:4096].rearrange("p (k t) -> p k t", k=8),
                in_=t2T_d[2 * q]), waits=tiles_done + [mm_prev.get(("t2free", q - 1))],
                inc="lx0")
            l2b_ = prog.sync(lambda sy, q=q: sy.dma_start(
                out=t2q[:, 4096:8192].rearrange("p (k t) -> p k t", k=8),
                in_=t2T_d[2 * q + 1]), inc="lx0")
            contrib = 0
            for e in range(8):
                for hh in range(2):
                    par = contrib % 2
                    lw1 = prog.gp(lambda g, e=e, hh=hh, q=q: g.dma_start(
                        out=w1h.rearrange("p (k n) -> p k n", k=8),
                        in_=w1bf[e].rearrange("(k p) n -> p k n", p=128)
                        [:, :, hh * 2048:(hh + 1) * 2048]),
                        waits=([prog.all_tok("pw0"), prog.all_tok("pw1")] + tiles_done
                               if (q == 0 and e == 0 and hh == 0) else
                               [mm_prev.get(("w1free",))]), inc="lw1", n=16)
                    lw2 = prog.gp(lambda g, e=e, hh=hh: g.dma_start(
                        out=w2h.rearrange("p (k n) -> p k n", k=16),
                        in_=w2bf[e].rearrange("(k p) n -> p k n", p=128)
                        [:, hh * 16:(hh + 1) * 16, :]),
                        waits=(tiles_done if (q == 0 and e == 0 and hh == 0) else []) +
                        [mm_prev.get(("w2free",))], inc="lw2", n=16)
                    hev = None
                    for ob in range(16):
                        for tb in range(2):
                            pq = (ob * 2 + tb) % 2
                            w0 = [lw1, l2b_] + ([mm_prev.get(("hev", pq))] if mm_prev.get(("hev", pq)) else [])
                            for kb in range(8):
                                tok_mm = prog.pe(lambda t, pq=pq, ob=ob, tb=tb, kb=kb, hh=hh: t.matmul(
                                    pp[pq][:],
                                    w1h[:, kb * 2048 + ob * 128: kb * 2048 + (ob + 1) * 128],
                                    t2q[:, kb * 512 + tb * 4096: kb * 512 + tb * 4096 + 512]
                                    if False else
                                    t2q[:, tb * 4096 + kb * 512: tb * 4096 + (kb + 1) * 512],
                                    start=(kb == 0), stop=(kb == 7)),
                                    waits=w0 if kb == 0 else (), inc="ps" if kb == 7 else None)
                            hev = prog.se(lambda s, pq=pq, ob=ob, tb=tb: s.activation(
                                hTh[:, ob * D + tb * 512: ob * D + (tb + 1) * 512],
                                pp[pq][:], AF.Gelu_apprx_tanh),
                                waits=[tok_mm], inc="ss")
                            mm_prev[("hev", pq)] = hev
                    mm_prev[("w1free",)] = hev
                    yev = None
                    for tsub in range(8):
                        for nb in range(2):
                            pq = 2 + (tsub * 2 + nb) % 2
                            w0 = [hev, lw2] + ([mm_prev.get(("yev", pq))] if mm_prev.get(("yev", pq)) else [])
                            for kb in range(16):
                                tok_mm = prog.pe(lambda t, pq=pq, tsub=tsub, nb=nb, kb=kb: t.matmul(
                                    pp[pq][:],
                                    hTh[:, kb * D + tsub * 128: kb * D + (tsub + 1) * 128],
                                    w2h[:, kb * D + nb * 512: kb * D + (nb + 1) * 512],
                                    start=(kb == 0), stop=(kb == 15)),
                                    waits=w0 if kb == 0 else (), inc="ps" if kb == 15 else None)
                            yev = prog.ve(lambda v, pq=pq, q=q, tsub=tsub, nb=nb,
                                          e=e, contrib=contrib: (
                                v.tensor_scalar(
                                    y0[:, tsub * D + nb * 512: tsub * D + (nb + 1) * 512],
                                    pp[pq][:],
                                    mask_sb[:, (q * 8 + tsub) * 8 + e:
                                            (q * 8 + tsub) * 8 + e + 1],
                                    None, ALU.mult)
                                if contrib == 0 else
                                v.scalar_tensor_tensor(
                                    y0[:, tsub * D + nb * 512: tsub * D + (nb + 1) * 512],
                                    pp[pq][:],
                                    mask_sb[:, (q * 8 + tsub) * 8 + e:
                                            (q * 8 + tsub) * 8 + e + 1],
                                    y0[:, tsub * D + nb * 512: tsub * D + (nb + 1) * 512],
                                    ALU.mult, ALU.add)), waits=[tok_mm], inc="vs")
                            mm_prev[("yev", pq)] = yev
                    mm_prev[("w2free",)] = yev
                    contrib += 1
            so_prev = None
            for half in range(2):
                lsd = prog.sync(lambda sy, q=q, half=half: sy.dma_start(
                    out=xt32[:].rearrange("p (c d) -> p c d", c=4),
                    in_=x_tile_in(s_i, 2 * q + half)),
                    waits=[mm_prev.get(("sfree", q - 1, 1)),
                           yev if half == 0 else so_prev],
                    inc=f"lx{1 + half}")
                fin = None
                for c in range(4):
                    fin = prog.ve(lambda v, half=half, c=c: v.tensor_tensor(
                        xt32[:, c * D:(c + 1) * D],
                        y0[:, (half * 4 + c) * D:(half * 4 + c + 1) * D],
                        xt32[:, c * D:(c + 1) * D], ALU.add),
                        waits=[lsd, yev] if c == 0 else [], inc="vs")
                so1 = prog.sync(lambda sy, q=q, half=half: sy.dma_start(
                    out=x_tile_in(x_out, 2 * q + half),
                    in_=xt32[:].rearrange("p (c d) -> p c d", c=4)),
                    waits=[fin], inc=f"so{half}")
                ostores.append(so1)
                so_prev = so1
                mm_prev[("sfree", q, half)] = so1
            mm_prev[("t2free", q)] = yev
        prog.sync(lambda sy: sy.nop(), waits=ostores[-2:])
        _fin(nc, prog, stack)
    return nc


def build_fused():
    nc = bass.Bass("TRN2", target_bir_lowering=False)
    prog = Prog()
    with ExitStack() as stack:
        d = {}
        d["c_ident"] = nc.dram_tensor("c_ident", [128, 128], F32, kind="ExternalInput").ap()
        d["c_mask4"] = nc.dram_tensor("c_mask4", [128, 512], F32, kind="ExternalInput").ap()
        w = {}
        w["x"] = nc.dram_tensor("L0_attn_x", [S, D], F32, kind="ExternalInput").ap()
        for nm, sh in [("Wq", [4, D, D]), ("Wk", [4, D, D]), ("Wv", [4, D, D]),
                       ("Wo", [4, D, D]), ("bq", [4, D]), ("bk", [4, D]),
                       ("bv", [4, D]), ("bo", [4, D]), ("W1", [4, D, DFF]),
                       ("b1", [4, DFF]), ("W2", [4, DFF, D]), ("b2", [4, D]),
                       ("g1", [4, D]), ("be1", [4, D]), ("g2", [4, D]),
                       ("be2", [4, D]), ("gf", [D]), ("bef", [D])]:
            w[nm] = nc.dram_tensor(f"L0_attn_{nm}", sh, F32, kind="ExternalInput").ap()
        for nm, sh in [("w1_w", [DFF, D]), ("w1_s", [DFF]), ("w2_w", [DFF, D]),
                       ("w2_s", [DFF]), ("h1_w", [DFF, DFF]), ("h1_s", [DFF]),
                       ("h2_w", [DFF, DFF]), ("h2_s", [DFF]), ("w3_w", [D, DFF]),
                       ("w3_s", [D])]:
            w[nm] = nc.dram_tensor(f"L0_ffn_{nm}", sh, F32, kind="ExternalInput").ap()
        for nm, sh, dt in [("w_q", [D, 128], F32), ("w_k", [D, 128], F32),
                           ("w_v", [D, 128], F32), ("w_o", [128, D], F32),
                           ("head_dim_mask", [128], mybir.dt.int32)]:
            w[nm] = nc.dram_tensor(f"L1_attn_{nm}", sh, dt, kind="ExternalInput").ap()
        for nm, sh in [("ln1_w", [D]), ("ln1_b", [D]), ("w_qkv", [D, 3 * D]),
                       ("w_proj", [D, D]), ("ln2_w", [D]), ("ln2_b", [D]),
                       ("gate_w", [D, 8]), ("gate_b", [8]),
                       ("w1", [8, D, DFF]), ("w2", [8, DFF, D])]:
            w[nm] = nc.dram_tensor(f"L1_ffn_{nm}", sh, F32, kind="ExternalInput").ap()
        out_ap = nc.dram_tensor("out", [S, D], F32, kind="ExternalOutput").ap()
        xa = nc.dram_tensor("zz_xa", [S, D], F32, kind="Internal").ap()
        xb_ = nc.dram_tensor("zz_xb", [S, D], F32, kind="Internal").ap()
        x1_ = nc.dram_tensor("zz_x1", [S, D], F32, kind="Internal").ap()
        x2_ = nc.dram_tensor("zz_x2", [S, D], F32, kind="Internal").ap()
        x3_ = nc.dram_tensor("zz_x3", [S, D], F32, kind="Internal").ap()
        x2i_ = nc.dram_tensor("zz_x2i", [S, D], F32, kind="Internal").ap()
        s_ = nc.dram_tensor("zz_s", [S, D], F32, kind="Internal").ap()

        _FUSED.update(on=True, nc=nc, prog=prog, ns="c_", dram=d, C=None)
        C = declare_constants(nc, stack, prog, "ld")
        _FUSED["C"] = C

        def call(ns, fn, dram):
            _FUSED["ns"] = ns
            _FUSED["dram"] = dram
            fn()

        call("p0_", build_phase_init, {"x_in": w["x"], "x_out": xa})
        cur, nxt = xa, xb_
        for l in range(4):
            dm = {"x_in": cur, "x_out": nxt,
                  "Wq": w["Wq"][l], "Wk": w["Wk"][l], "Wv": w["Wv"][l],
                  "Wo": w["Wo"][l],
                  "bq": w["bq"][l:l + 1, :], "bk": w["bk"][l:l + 1, :],
                  "bv": w["bv"][l:l + 1, :], "bo": w["bo"][l:l + 1, :]}
            call(f"a{l}_", lambda l=l: build_phase_l0_att(l), dm)
            cur, nxt = nxt, cur
            dm = {"x_in": cur, "x_out": nxt,
                  "W1": w["W1"][l], "W2": w["W2"][l],
                  "b1": w["b1"][l:l + 1, :], "b2": w["b2"][l:l + 1, :],
                  "g1": w["g1"][l:l + 1, :], "be1": w["be1"][l:l + 1, :],
                  "g2": w["g2"][l:l + 1, :], "be2": w["be2"][l:l + 1, :]}
            call(f"f{l}_", lambda l=l: build_phase_l0_ffn(l), dm)
            cur, nxt = nxt, cur
        call("r_", build_phase_res_ln,
             {"x_in": cur, "res_in": w["x"], "x_out": x1_,
              "gf": w["gf"].unsqueeze(0), "bef": w["bef"].unsqueeze(0)})
        dm = {"x_in": x1_, "x_out": x2_}
        for nm in ("w1_w", "w1_s", "w2_w", "w2_s", "h1_w", "h1_s", "h2_w",
                   "h2_s", "w3_w", "w3_s"):
            dm[nm] = w[nm]
        call("g_", build_phase_l0ffn_big, dm)
        call("t_", build_phase_l1_att,
             {"x_in": x2_, "x_out": x3_, "w_q": w["w_q"], "w_k": w["w_k"],
              "w_v": w["w_v"], "w_o": w["w_o"],
              "head_dim_mask": w["head_dim_mask"].unsqueeze(1)})
        call("m_", build_phase_l1_msa,
             {"x_in": x3_, "x2i": x2i_, "s_out": s_, "w_qkv": w["w_qkv"],
              "w_proj": w["w_proj"], "ln1_w": w["ln1_w"].unsqueeze(0),
              "ln1_b": w["ln1_b"].unsqueeze(0)})
        call("e_", build_phase_l1_moe,
             {"x2i": x2i_, "s_in": s_, "x_out": out_ap,
              "gate_w": w["gate_w"], "gate_b": w["gate_b"].unsqueeze(0),
              "w1": w["w1"], "w2": w["w2"],
              "ln2_w": w["ln2_w"].unsqueeze(0), "ln2_b": w["ln2_b"].unsqueeze(0)})
        _FUSED.update(on=False, nc=None, prog=None, dram={}, C=None)
        realize(nc, prog, stack)
    return nc


_CONSTS = host_constants()
_CACHE = {}


def _get(name, builder):
    if name not in _CACHE:
        _CACHE[name] = builder()
    return _CACHE[name]


def _run(nc, shared, per_core):
    maps = [dict(shared, **pc) for pc in per_core]
    res = run_bass_kernel_spmd(nc, maps, core_ids=list(range(8)))
    return res.results


def kernel(**inputs):
    inp = {k: np.ascontiguousarray(v) for k, v in inputs.items()}
    x = inp["L0_attn_x"]
    nc = _get("fused", build_fused)
    shared = dict(_CONSTS)
    for k, v in inp.items():
        if k != "L0_attn_x":
            shared[k] = v
    maps = [dict(shared, L0_attn_x=x[b]) for b in range(8)]
    res = run_bass_kernel_spmd(nc, maps, core_ids=list(range(8)))
    out = np.stack([res.results[b]["out"] for b in range(8)], axis=0)
    return out.astype(np.float32)


# version 05 (final)
# speedup vs baseline: 1.0151x; 1.0151x over previous
import numpy as np
from contextlib import ExitStack
import concourse.bass as bass
import concourse.mybir as mybir
from concourse.bass_utils import run_bass_kernel_spmd

F32 = mybir.dt.float32
BF16 = mybir.dt.bfloat16
AF = mybir.ActivationFunctionType
ALU = mybir.AluOpType
AX = mybir.AxisListType

S = 4096
D = 1024
NH = 16
HD = 64
DFF = 4096
LN_EPS = 1e-5
ATT_EPS = 1e-6

ENGINES = ("sync", "tensor", "vector", "scalar", "gpsimd")

_FUSED = {"on": False, "nc": None, "prog": None, "ns": "", "dram": {}, "C": None}


def _mknc():
    if _FUSED["on"]:
        return _FUSED["nc"]
    return bass.Bass("TRN2", target_bir_lowering=False)


def _mkprog():
    if _FUSED["on"]:
        return _FUSED["prog"]
    return Prog()


def _dram(nc, name, shape, dt, kind):
    if _FUSED["on"]:
        if name in _FUSED["dram"]:
            return _FUSED["dram"][name]
        return nc.dram_tensor(_FUSED["ns"] + name, shape, dt, kind="Internal").ap()
    return nc.dram_tensor(name, shape, dt, kind=kind).ap()


def _nm(name):
    return _FUSED["ns"] + name if _FUSED["on"] else name


def _fin(nc, prog, stack):
    if _FUSED["on"]:
        prog.barrier()
    else:
        realize(nc, prog, stack)


class Prog:
    def __init__(self):
        self.steps = []
        self.counts = {}

    def step(self, eng, fn, waits=(), inc=None, n=1):
        tok = None
        if inc is not None:
            self.counts[inc] = self.counts.get(inc, 0) + n
            tok = (inc, self.counts[inc])
        self.steps.append((eng, fn, [w for w in waits if w is not None], inc, n))
        return tok

    def sync(self, fn, waits=(), inc=None, n=16):
        return self.step("sync", fn, waits, inc, n)

    def all_tok(self, name):
        return (name, self.counts.get(name, 0))

    def cur_tok(self, name):
        return (name, self.counts.get(name, 0))

    def barrier(self):
        waits = [(n, c) for (n, c) in self.counts.items()]
        for eng in ENGINES:
            self.step(eng, lambda e: e.nop(), waits=list(waits))

    def gp(self, fn, waits=(), inc=None, n=1):
        return self.step("gpsimd", fn, waits, inc, n)

    def pe(self, fn, waits=(), inc=None, n=1):
        return self.step("tensor", fn, waits, inc, n)

    def ve(self, fn, waits=(), inc=None, n=1):
        return self.step("vector", fn, waits, inc, n)

    def se(self, fn, waits=(), inc=None, n=1):
        return self.step("scalar", fn, waits, inc, n)


def realize(nc, prog, stack):
    def tail(sy):
        return sy.nop()
    prog.step("sync", tail, waits=[(n, prog.counts[n]) for n in list(prog.counts)])
    sems = {name: stack.enter_context(nc.semaphore(name)) for name in prog.counts}
    block = stack.enter_context(nc.Block())
    for eng_name in ENGINES:
        steps = [s for s in prog.steps if s[0] == eng_name]
        if not steps:
            continue

        def make_body(steps):
            def body(eng):
                seen = {}
                for (_, fn, waits, inc, n) in steps:
                    for (sname, val) in waits:
                        if val == "ALL":
                            val = prog.counts[sname]
                        if seen.get(sname, 0) < val:
                            eng.wait_ge(sems[sname], val)
                            seen[sname] = val
                    instr = fn(eng)
                    if inc is not None:
                        instr.then_inc(sems[inc], n)
            return body

        getattr(block, eng_name)(make_body(steps))


def host_constants():
    ident = np.eye(128, dtype=np.float32)
    m = np.triu(np.ones((128, 128), np.float32))
    mask4 = np.tile(m, (1, 4))
    return {"c_ident": ident, "c_mask4": mask4}


def declare_constants(nc, stack, prog, ld, need=("id_bf", "id32", "mask4",
                                                  "ones_row", "ones_row32",
                                                  "ones_col", "eps_ln")):
    if _FUSED["on"] and _FUSED["C"] is not None:
        return _FUSED["C"]
    out = {"const_toks": []}
    if "id_bf" in need or "id32" in need:
        ident_d = _dram(nc, "c_ident", [128, 128], F32, "ExternalInput")
    if "id_bf" in need:
        id_bf = stack.enter_context(nc.sbuf_tensor("id_bf", [128, 128], BF16))
        out["const_toks"].append(prog.gp(lambda g: g.dma_start(out=id_bf[:], in_=ident_d), inc=ld, n=16))
        out["id_bf"] = id_bf
    if "id32" in need:
        id32 = stack.enter_context(nc.sbuf_tensor("id32", [128, 128], F32))
        out["const_toks"].append(prog.gp(lambda g: g.dma_start(out=id32[:], in_=ident_d), inc=ld, n=16))
        out["id32"] = id32
    if "mask4" in need:
        mask4_d = _dram(nc, "c_mask4", [128, 512], F32, "ExternalInput")
        mask4 = stack.enter_context(nc.sbuf_tensor("mask4", [128, 512], BF16))
        out["const_toks"].append(prog.gp(lambda g: g.dma_start(out=mask4[:], in_=mask4_d), inc=ld, n=16))
        out["mask4"] = mask4
    if "ones_row" in need:
        ones_row = stack.enter_context(nc.sbuf_tensor("ones_row", [1, 512], BF16))
        out["const_toks"].append(prog.gp(lambda g: g.memset(ones_row[:], 1.0), inc="gs", n=1))
        out["ones_row"] = ones_row
    if "ones_row32" in need:
        ones_row32 = stack.enter_context(nc.sbuf_tensor("ones_row32", [1, 128], F32))
        out["const_toks"].append(prog.gp(lambda g: g.memset(ones_row32[:], 1.0), inc="gs", n=1))
        out["ones_row32"] = ones_row32
    if "ones_col" in need:
        ones_col = stack.enter_context(nc.sbuf_tensor("ones_col", [128, 1], BF16))
        out["const_toks"].append(prog.gp(lambda g: g.memset(ones_col[:], 1.0), inc="gs", n=1))
        out["ones_col"] = ones_col
    if "eps_ln" in need:
        eps_ln = stack.enter_context(nc.sbuf_tensor("eps_ln", [128, 1], F32))
        out["const_toks"].append(prog.gp(lambda g: g.memset(eps_ln[:], LN_EPS), inc="gs", n=1))
        out["eps_ln"] = eps_ln
    return out


def emit_ln_tile(prog, x_tile, nchunks, bn6, mv, rs, eps_col, waits):
    tok_bn = None
    for c in range(nchunks):
        for i in range(2):
            last = (c == nchunks - 1 and i == 1)
            tok_bn = prog.ve(lambda v, c=c, i=i: v.bn_stats(
                bn6[:, c * 12 + i * 6: c * 12 + (i + 1) * 6],
                x_tile[:, c * 1024 + i * 512: c * 1024 + (i + 1) * 512]),
                waits=waits if (c == 0 and i == 0) else (),
                inc="vs" if last else None)
    tok_mv = None
    for c in range(nchunks):
        tok_mv = prog.ve(lambda v, c=c: v.bn_aggr(
            mv[:, c * 2:(c + 1) * 2], bn6[:, c * 12:(c + 1) * 12]),
            waits=[tok_bn] if c == 0 else (),
            inc="vs" if c == nchunks - 1 else None)
    tok_l = prog.se(lambda s: s.activation(
        rs[:], mv[:].rearrange("p (c two) -> p c two", two=2)[:, :, 1:2],
        AF.Ln, bias=eps_col[:]), waits=[tok_mv], inc="ss")
    tok_rs = prog.se(lambda s: s.activation(rs[:], rs[:], AF.Exp, scale=-0.5),
                     waits=[tok_l], inc="ss")
    return tok_mv, tok_rs


def load_weight_T(prog, dram_ap, sb_view, kdim, ndim, maxld=None):
    kb = kdim // 128
    nsplit = 1 if ndim <= 2048 else (ndim + 2047) // 2048
    step = ndim // nsplit
    tok = None
    for j in range(nsplit):
        tok = prog.gp(lambda g, j=j: g.dma_start(
            out=sb_view.rearrange("p (k n) -> p k n", k=kb)[:, :, j * step:(j + 1) * step],
            in_=dram_ap.rearrange("(k p) n -> p k n", p=128)[:, :, j * step:(j + 1) * step]),
            inc="ld", n=16)
    return tok


def x_tile_in(ap, i, nchunk=4):
    return ap.rearrange("(t c p) d -> t p c d", p=128, c=nchunk)[i]


def build_phase_init():
    nc = _mknc()
    x_in = _dram(nc, "x_in", [S, D], F32, "ExternalInput")
    x_out = _dram(nc, "x_out", [S, D], F32, "ExternalOutput")
    prog = _mkprog()
    with ExitStack() as stack:
        sb = lambda n, s, d: stack.enter_context(nc.sbuf_tensor(_nm(n), s, d))
        xt = [sb(f"xt{i}", [128, 4 * D], F32) for i in range(2)]
        ot = [sb(f"ot{i}", [128, 4 * D], F32) for i in range(2)]
        bn6 = sb("bn6", [128, 48], F32)
        mv = sb("mv", [128, 8], F32)
        rs = sb("rs", [128, 4], F32)
        eps_ln = sb("eps_ln", [128, 1], F32)
        eps_tok = prog.gp(lambda g: g.memset(eps_ln[:], LN_EPS), inc="gs", n=1)
        comp, stores, loads = [], [], []
        for i in range(8):
            b = i % 2
            loads.append(prog.sync(
                lambda sy, i=i, b=b: sy.dma_start(
                    out=xt[b][:].rearrange("p (c d) -> p c d", c=4),
                    in_=x_tile_in(x_in, i)),
                waits=[comp[i - 2]] if i >= 2 else [], inc=f"lx{b}"))
            tok_mv, tok_rs = emit_ln_tile(prog, xt[b], 4, bn6, mv, rs, eps_ln,
                                          waits=[loads[i], eps_tok])
            ctok = None
            for c in range(4):
                ctok = prog.ve(
                    lambda v, b=b, c=c: v.tensor_scalar(
                        ot[b][:, c * D:(c + 1) * D], xt[b][:, c * D:(c + 1) * D],
                        mv[:, 2 * c:2 * c + 1], rs[:, c:c + 1],
                        ALU.subtract, ALU.mult),
                    waits=[tok_mv, tok_rs] + ([stores[i - 2]] if i >= 2 else [])
                    if c == 0 else (), inc="vs" if c == 3 else None)
            comp.append(ctok)
            stores.append(prog.sync(
                lambda sy, i=i, b=b: sy.dma_start(
                    out=x_tile_in(x_out, i),
                    in_=ot[b][:].rearrange("p (c d) -> p c d", c=4)),
                waits=[comp[i]], inc=f"st{b}"))
        prog.sync(lambda sy: sy.nop(), waits=[stores[-1]])
        _fin(nc, prog, stack)
    return nc


def build_phase_l0_att(layer, NT=None, debug_stage=None):
    TT = 512
    if NT is None:
        NT = S // TT
    nc = _mknc()
    x_in = _dram(nc, "x_in", [S, D], F32, "ExternalInput")
    x_out = _dram(nc, "x_out", [S, D], F32, "ExternalOutput")
    Wq = _dram(nc, "Wq", [D, D], F32, "ExternalInput")
    Wk = _dram(nc, "Wk", [D, D], F32, "ExternalInput")
    Wv = _dram(nc, "Wv", [D, D], F32, "ExternalInput")
    Wo = _dram(nc, "Wo", [D, D], F32, "ExternalInput")
    bqd = _dram(nc, "bq", [1, D], F32, "ExternalInput")
    bkd = _dram(nc, "bk", [1, D], F32, "ExternalInput")
    bvd = _dram(nc, "bv", [1, D], F32, "ExternalInput")
    bod = _dram(nc, "bo", [1, D], F32, "ExternalInput")

    prog = _mkprog()
    with ExitStack() as stack:
        sb = lambda n, s, d: stack.enter_context(nc.sbuf_tensor(_nm(n), s, d))
        psal = lambda n, s, d: stack.enter_context(nc.psum_tensor(_nm(n), s, d))
        C = declare_constants(nc, stack, prog, "ld",
                              need=("id_bf", "mask4", "ones_row"))
        id_bf, mask4 = C["id_bf"], C["mask4"]
        ones_row = C["ones_row"]

        Wq_sb = sb("Wq_sb", [128, 8 * D], BF16)
        Wk_sb = sb("Wk_sb", [128, 8 * D], BF16)
        Wv_sb = sb("Wv_sb", [128, 8 * D], BF16)
        Wo_sb = sb("Wo_sb", [128, 8 * D], BF16)
        bq_sb = sb("bq_sb", [1, D], BF16)
        bk_sb = sb("bk_sb", [1, D], BF16)
        bv_sb = sb("bv_sb", [1, D], BF16)
        bo_sb = sb("bo_sb", [1, D], BF16)

        wq_t = load_weight_T(prog, Wq, Wq_sb[:], D, D)
        wk_t = load_weight_T(prog, Wk, Wk_sb[:], D, D)
        wv_t = load_weight_T(prog, Wv, Wv_sb[:], D, D)
        wo_t = load_weight_T(prog, Wo, Wo_sb[:], D, D)
        bq_t = prog.gp(lambda g: g.dma_start(out=bq_sb[:], in_=bqd), inc="ld", n=16)
        bk_t = prog.gp(lambda g: g.dma_start(out=bk_sb[:], in_=bkd), inc="ld", n=16)
        bv_t = prog.gp(lambda g: g.dma_start(out=bv_sb[:], in_=bvd), inc="ld", n=16)
        bo_t = prog.gp(lambda g: g.dma_start(out=bo_sb[:], in_=bod), inc="ld", n=16)

        xt32 = [sb(f"xt32_{i}", [128, 4 * D], F32) for i in range(2)]
        xo32 = [sb("xo32_0", [128, 4 * D], F32)] * 2
        xb = sb("xb", [128, 4 * D], BF16)
        xT = sb("xT", [128, 8 * TT], BF16)
        qT = sb("qT", [128, 8 * TT], BF16)
        kT = sb("kT", [128, 8 * TT], BF16)
        kphi = sb("kphi", [128, 4 * D], BF16)
        v_aug = sb("v_aug", [128, 4 * 1040], BF16)
        st_sb = sb("st_sb", [128, 520], BF16)
        st32 = [sb(f"st32_{j}", [128, 520], F32) for j in range(2)]
        sT_bf = sb("sT_bf", [128, 2048], BF16)
        att_sb = sb("att_sb", [128, 1024], BF16)
        attT_sb = sb("attT_sb", [128, 1024], BF16)
        dd = sb("dd", [128, 16], F32)
        dr = sb("dr", [128, 16], F32)
        phi1 = [sb(f"phi1_{j}", [128, 512], F32) for j in range(2)]
        phi2 = [sb(f"phi2_{j}", [128, 512], BF16) for j in range(2)]

        sp = [psal(f"sp{j}", [128, 512], F32) for j in range(2)]
        pp = [psal(f"pp{j}", [128, 512], F32) for j in range(2)]
        npp = [psal(f"np{j}", [128, 512], F32) for j in range(2)]
        tp = [psal(f"tp{j}", [128, 1024], BF16) for j in range(2)]

        ones_aug = prog.gp(lambda g: g.memset(
            v_aug[:].rearrange("p (c h x) -> p (c h) x", c=4, h=16)[:, :, 64:65], 1.0),
            inc="gs", n=1)
        st0_tok = prog.gp(lambda g: g.memset(st32[1][:], 0.0), inc="gs", n=1)

        consts = C["const_toks"]
        tp_tok = [None, None]
        tp_rr = [0]

        def transpose_block(src_view_fn, dst_view_fn, nblocks, waits, evac_fn=None):
            last = None
            for base in range(0, nblocks, 8):
                cnt = min(8, nblocks - base)
                tb = tp_rr[0] % 2
                tp_rr[0] += 1
                w = list(waits) + ([tp_tok[tb]] if tp_tok[tb] else [])
                for j in range(cnt):
                    tok_tr = prog.pe(lambda t, tb=tb, j=j, base=base: t.transpose(
                        tp[tb][:, j * 128:(j + 1) * 128], src_view_fn(base + j), id_bf[:]),
                        waits=w if j == 0 else (), inc="ps" if j == cnt - 1 else None)
                for j in range(cnt):
                    last = prog.se(lambda s, tb=tb, j=j, base=base: s.copy(
                        dst_view_fn(base + j), tp[tb][:, j * 128:(j + 1) * 128]),
                        waits=[tok_tr] if j == 0 else (), inc="ss" if j == cnt - 1 else None)
                tp_tok[tb] = last
                waits = ()
            return last

        stores = []
        xt_free = [None, None]
        xo_free = [None, None]
        last_m4 = None
        last_sadd = None
        attT_free = None
        prev_pp_user = None
        for i in range(NT):
            b = i % 2
            tok_x = prog.sync(
                lambda sy, i=i, b=b: sy.dma_start(
                    out=xt32[b][:].rearrange("p (c d) -> p c d", c=4),
                    in_=x_tile_in(x_in, i)),
                waits=[xt_free[b]], inc=f"lx{b}")
            tok_xb = prog.ve(lambda v, b=b: v.tensor_copy(xb[:], xt32[b][:]),
                             waits=[tok_x], inc="vs")
            if debug_stage == "load":
                dtok = prog.sync(lambda sy, i=i, b=b: sy.dma_start(
                    out=x_tile_in(x_out, i),
                    in_=xt32[b][:].rearrange("p (c d) -> p c d", c=4)),
                    waits=[tok_xb], inc=f"st{b}")
                stores.append(dtok)
                xt_free[b] = dtok
                xo_free[b] = dtok
                continue
            xT_done = transpose_block(
                lambda j: xb[:, (j % 4) * D + (j // 4) * 128:(j % 4) * D + (j // 4) * 128 + 128],
                lambda j: xT[:, (j // 4) * TT + (j % 4) * 128:(j // 4) * TT + (j % 4) * 128 + 128],
                32, [tok_xb, prog.all_tok('ld')])

            def emit_proj_T(W_sb, b_sb, out_T, wtoks, prev_done=None):
                outs = []
                for db in range(8):
                    pb = db % 2
                    w0 = [xT_done] + wtoks + ([outs[db - 2][0]] if db >= 2 else [prev_done])
                    for kb in range(8):
                        prog.pe(lambda t, pb=pb, db=db, kb=kb: t.matmul(
                            pp[pb][:], W_sb[:, kb * D + db * 128: kb * D + (db + 1) * 128],
                            xT[:, kb * TT:(kb + 1) * TT], start=(kb == 0), stop=False),
                            waits=w0 if kb == 0 else ())
                    tok_mm = prog.pe(lambda t, pb=pb, db=db: t.matmul(
                        pp[pb][:], b_sb[0:1, db * 128:(db + 1) * 128], ones_row[0:1, :],
                        start=False, stop=True), inc="ps")
                    fb = db % 2
                    w1 = [tok_mm] + ([outs[db - 2][1]] if db >= 2 else [])
                    tok_v1 = prog.ve(lambda v, pb=pb, fb=fb: v.tensor_scalar(
                        phi1[fb][:], pp[pb][:], 0.0, None, ALU.min), waits=w1, inc="vs")
                    tok_s1 = prog.se(lambda s, fb=fb: s.activation(
                        phi2[fb][:], phi1[fb][:], AF.Exp), waits=[tok_v1], inc="ss")
                    tok_v2 = prog.ve(lambda v, pb=pb, fb=fb, db=db: v.scalar_tensor_tensor(
                        out_T[:, db * TT:(db + 1) * TT], pp[pb][:], 0.0, phi2[fb][:],
                        ALU.max, ALU.add), waits=[tok_s1], inc="vs")
                    outs.append((tok_v2, tok_s1))
                return outs[-1][0], outs[-2][0]

            qT_done, _ = emit_proj_T(Wq_sb, bq_sb, qT, [prog.all_tok('ld')], prev_pp_user)
            kT_done, _ = emit_proj_T(Wk_sb, bk_sb, kT, [prog.all_tok('ld')], qT_done)

            v_last = []
            for c in range(4):
                for nb in range(2):
                    pb = (c * 2 + nb) % 2
                    w0 = [xT_done, prog.all_tok('ld'), kT_done] + ([v_last[-2]] if len(v_last) >= 2 else [])
                    for kb in range(8):
                        prog.pe(lambda t, pb=pb, c=c, kb=kb, nb=nb: t.matmul(
                            pp[pb][:], xT[:, kb * TT + c * 128: kb * TT + (c + 1) * 128],
                            Wv_sb[:, kb * D + nb * 512: kb * D + (nb + 1) * 512],
                            start=(kb == 0), stop=False), waits=w0 if kb == 0 else ())
                    tok_mm = prog.pe(lambda t, pb=pb, nb=nb: t.matmul(
                        pp[pb][:], ones_row[0:1, 0:128], bv_sb[0:1, nb * 512:(nb + 1) * 512],
                        start=False, stop=True), inc="ps")
                    v_last.append(prog.se(lambda s, pb=pb, c=c, nb=nb: s.copy(
                        v_aug[:].rearrange("p (c h x) -> p c h x", c=4, h=16)
                        [:, c, nb * 8:(nb + 1) * 8, 0:64],
                        pp[pb][:].rearrange("p (h x) -> p h x", x=64)),
                        waits=[tok_mm], inc="ss"))
            v_done = v_last[-1]

            if debug_stage == "proj":
                dtok = prog.gp(lambda g, i=i: g.dma_start(
                    out=x_tile_in(x_out, i),
                    in_=qT[:].rearrange("p (c x) -> p c x", c=4)),
                    waits=[kT_done], inc=f"st{b}", n=16)
                stores.append(dtok)
                xt_free[b] = dtok
                xo_free[b] = dtok
                continue

            kphi_done = transpose_block(
                lambda j: kT[:, (j // 4) * TT + (j % 4) * 128:(j // 4) * TT + (j % 4) * 128 + 128],
                lambda j: kphi[:, (j % 4) * D + (j // 4) * 128:(j % 4) * D + (j // 4) * 128 + 128],
                32, [kT_done, v_done])

            if debug_stage == "kphi":
                dtok = prog.gp(lambda g, i=i: g.dma_start(
                    out=x_tile_in(x_out, i),
                    in_=kphi[:].rearrange("p (c x) -> p c x", c=4)),
                    waits=[kphi_done], inc=f"st{b}", n=16)
                stores.append(dtok)
                xt_free[b] = dtok
                xo_free[b] = dtok
                continue

            for c in range(4):
                g = i * 4 + c
                pv, cur = (g + 1) % 2, g % 2
                e1 = prog.ve(lambda v, pv=pv: v.tensor_copy(st_sb[:], st32[pv][:]),
                             waits=[st0_tok, ones_aug, last_sadd], inc="vs")
                st_ready = e1
                def score_slot(h):
                    bank = (h % 2) + 2 * (h // 8)
                    lane = (h // 2) % 4
                    return bank, lane
                score_banks = [pp[0], pp[1], npp[0], npp[1]]
                bank_tok = [None] * 4
                for h in range(NH):
                    db, half = h // 2, (h % 2) * 64
                    bank, lane = score_slot(h)
                    tgt_b = score_banks[bank]
                    w0 = [st_ready, qT_done, kphi_done] if h == 0 else ()
                    bank_tok[bank] = prog.pe(lambda t, tgt_b=tgt_b, lane=lane, db=db,
                                             half=half, c=c: t.matmul(
                        tgt_b[:, lane * 128:(lane + 1) * 128],
                        kT[half:half + 64, db * TT + c * 128: db * TT + (c + 1) * 128],
                        qT[half:half + 64, db * TT + c * 128: db * TT + (c + 1) * 128],
                        start=True, stop=True),
                        waits=w0, inc="ps" if h in (6, 7, 14, 15) else None)
                mask_toks = []
                for bank in range(4):
                    mask_toks.append(prog.ve(lambda v, bank=bank: v.tensor_tensor(
                        sT_bf[:, bank * 512:(bank + 1) * 512], score_banks[bank][:],
                        mask4[:], ALU.mult),
                        waits=[bank_tok[bank]], inc="vs"))
                sT_done = mask_toks[-1]
                if debug_stage == "scores":
                    if c == 0:
                        dtok = prog.sync(lambda sy, i=i, b=b: sy.dma_start(
                            out=x_tile_in(x_out, i),
                            in_=xt32[b][:].rearrange("p (c d) -> p c d", c=4)),
                            waits=[sT_done], inc=f"st{b}", n=16)
                        stores.append(dtok)
                        xt_free[b] = dtok
                        xo_free[b] = dtok
                        last_sadd = sT_done
                    continue

                def npv(h):
                    if h < 7:
                        return npp[0], h * 65
                    if h < 14:
                        return npp[1], (h - 7) * 65
                    return pp[0], (h - 14) * 65

                for h in range(NH):
                    db, half = h // 2, (h % 2) * 64
                    jp = h // 8
                    scol = ((h % 8) // 2) * 65
                    prow = (h % 2) * 64
                    tgt, col = npv(h)
                    bank_s, lane_s = score_slot(h)
                    slot = bank_s * 4 + lane_s
                    w0 = [sT_done] if h == 0 else ()
                    prog.pe(lambda t, h=h, c=c, tgt=tgt, col=col, slot=slot: t.matmul(
                        tgt[:, col:col + 65],
                        sT_bf[:, slot * 128:(slot + 1) * 128],
                        v_aug[:, c * 1040 + h * 65: c * 1040 + (h + 1) * 65],
                        start=True, stop=False), waits=w0)
                    prog.pe(lambda t, h=h, c=c, db=db, half=half, jp=jp, prow=prow,
                            scol=scol, tgt=tgt, col=col: t.matmul(
                        tgt[:, col:col + 65],
                        qT[half:half + 64, db * TT + c * 128: db * TT + (c + 1) * 128],
                        st_sb[prow:prow + 64, jp * 260 + scol: jp * 260 + scol + 65],
                        start=False, stop=True))
                for h in range(NH):
                    jp = h // 8
                    scol = ((h % 8) // 2) * 65
                    prow = (h % 2) * 64
                    tok_m4 = prog.pe(lambda t, h=h, c=c, jp=jp, prow=prow, scol=scol: t.matmul(
                        sp[jp][prow:prow + 64, scol:scol + 65],
                        kphi[:, c * D + h * 64: c * D + (h + 1) * 64],
                        v_aug[:, c * 1040 + h * 65: c * 1040 + (h + 1) * 65],
                        start=True, stop=True), inc="ps" if h == NH - 1 else None)
                last_m4 = tok_m4
                sa1 = prog.ve(lambda v, pv=pv, cur=cur: v.tensor_tensor(
                    st32[cur][:, 0:260], sp[0][:, 0:260], st32[pv][:, 0:260], ALU.add),
                    waits=[tok_m4], inc="vs")
                last_sadd = prog.ve(lambda v, pv=pv, cur=cur: v.tensor_tensor(
                    st32[cur][:, 260:520], sp[1][:, 0:260], st32[pv][:, 260:520], ALU.add),
                    inc="vs")

                if debug_stage == "state":
                    if c == 3:
                        dtok = prog.gp(lambda g, i=i, cur=cur: g.dma_start(
                            out=x_tile_in(x_out, i)[:, 0:1, 0:520],
                            in_=st32[cur][:].rearrange("p x -> p 1 x")),
                            waits=[last_sadd], inc=f"st{b}", n=16)
                        stores.append(dtok)
                        xt_free[b] = dtok
                        xo_free[b] = dtok
                    continue
                d1a = prog.ve(lambda v: v.tensor_scalar(
                    dd[:, 0:7].unsqueeze(2),
                    npp[0][:, 0:455].rearrange("p (h x) -> p h x", x=65)[:, :, 64:65],
                    ATT_EPS, None, ALU.add), waits=[tok_m4], inc="vs")
                d1b = prog.ve(lambda v: v.tensor_scalar(
                    dd[:, 7:14].unsqueeze(2),
                    npp[1][:, 0:455].rearrange("p (h x) -> p h x", x=65)[:, :, 64:65],
                    ATT_EPS, None, ALU.add), inc="vs")
                d1c = prog.ve(lambda v: v.tensor_scalar(
                    dd[:, 14:16].unsqueeze(2),
                    pp[0][:, 0:130].rearrange("p (h x) -> p h x", x=65)[:, :, 64:65],
                    ATT_EPS, None, ALU.add), inc="vs")
                d2 = prog.ve(lambda v: v.reciprocal(dr[:], dd[:]), waits=[d1c], inc="vs")
                div_last = None
                for h in range(NH):
                    tgt, col = npv(h)
                    div_last = prog.ve(lambda v, h=h, tgt=tgt, col=col: v.tensor_scalar(
                        att_sb[:, h * 64:(h + 1) * 64], tgt[:, col:col + 64],
                        dr[:, h:h + 1], None, ALU.mult),
                        waits=[d2] if h == 0 else (), inc="vs" if h == NH - 1 else None)

                if debug_stage == "chunks":
                    dtok = prog.gp(lambda g, i=i, c=c: g.dma_start(
                        out=x_tile_in(x_out, i)[:, c, :],
                        in_=att_sb[:]),
                        waits=[div_last], inc=f"st{b}", n=16)
                    if c == 3:
                        stores.append(dtok)
                        xt_free[b] = dtok
                        xo_free[b] = dtok
                        tok_v3 = dtok
                        prev_pp_user = sT_done
                    continue
                attT_done = transpose_block(
                    lambda j: att_sb[:, j * 128:(j + 1) * 128],
                    lambda j: attT_sb[:, j * 128:(j + 1) * 128],
                    8, [div_last, attT_free])
                attT_free = None
                for nb in range(2):
                    pb = nb
                    for pr in range(8):
                        prog.pe(lambda t, pb=pb, pr=pr, nb=nb: t.matmul(
                            pp[pb][:], attT_sb[:, pr * 128:(pr + 1) * 128],
                            Wo_sb[:, pr * D + nb * 512: pr * D + (nb + 1) * 512],
                            start=(pr == 0), stop=False),
                            waits=[attT_done, prog.all_tok('ld')] if pr == 0 else ())
                    tok_o = prog.pe(lambda t, pb=pb, nb=nb: t.matmul(
                        pp[pb][:], ones_row[0:1, 0:128], bo_sb[0:1, nb * 512:(nb + 1) * 512],
                        start=False, stop=True), inc="ps")
                    tok_v3 = prog.ve(lambda v, pb=pb, c=c, nb=nb, b=b: v.tensor_tensor(
                        xo32[b][:, c * D + nb * 512: c * D + (nb + 1) * 512],
                        pp[pb][:], xt32[b][:, c * D + nb * 512: c * D + (nb + 1) * 512],
                        ALU.add),
                        waits=[tok_o] + ([xo_free[b]] if (c == 0 and nb == 0) else []),
                        inc="vs")
                    if nb == 1:
                        attT_free = tok_v3
            if debug_stage in ("chunks", "scores", "state"):
                continue
            xt_free[b] = tok_v3
            prev_pp_user = tok_v3
            stores.append(prog.sync(
                lambda sy, i=i, b=b: sy.dma_start(
                    out=x_tile_in(x_out, i),
                    in_=xo32[b][:].rearrange("p (c d) -> p c d", c=4)),
                waits=[tok_v3], inc=f"st{b}"))
            xo_free[b] = stores[-1]
        prog.sync(lambda sy: sy.nop(), waits=[stores[-1]])
        _fin(nc, prog, stack)
    return nc


def build_phase_l0_ffn(layer):
    TT = 256
    NT = S // TT
    NC = TT // 128
    nc = _mknc()
    x_in = _dram(nc, "x_in", [S, D], F32, "ExternalInput")
    x_out = _dram(nc, "x_out", [S, D], F32, "ExternalOutput")
    W1 = _dram(nc, "W1", [D, DFF], F32, "ExternalInput")
    b1d = _dram(nc, "b1", [1, DFF], F32, "ExternalInput")
    W2 = _dram(nc, "W2", [DFF, D], F32, "ExternalInput")
    b2d = _dram(nc, "b2", [1, D], F32, "ExternalInput")
    g1d = _dram(nc, "g1", [1, D], F32, "ExternalInput")
    be1d = _dram(nc, "be1", [1, D], F32, "ExternalInput")
    g2d = _dram(nc, "g2", [1, D], F32, "ExternalInput")
    be2d = _dram(nc, "be2", [1, D], F32, "ExternalInput")

    prog = _mkprog()
    with ExitStack() as stack:
        sb = lambda n, s, d: stack.enter_context(nc.sbuf_tensor(_nm(n), s, d))
        psal = lambda n, s, d: stack.enter_context(nc.psum_tensor(_nm(n), s, d))
        C = declare_constants(nc, stack, prog, "ld",
                              need=("id32", "ones_row", "ones_row32", "eps_ln"))
        id32, ones_row = C["id32"], C["ones_row"]
        consts = C["const_toks"]

        W1_sb = sb("W1_sb", [128, 8 * DFF], BF16)
        W2_sb = sb("W2_sb", [128, 32 * D], BF16)
        b1_sb = sb("b1_sb", [1, DFF], BF16)
        b2p_sb = sb("b2p_sb", [1, D], BF16)
        g1c = sb("g1c", [128, 8], F32)
        be1c = sb("be1c", [128, 8], F32)
        g1f = sb("g1f", [128, D], F32)
        g2f = sb("g2f", [128, D], F32)
        be2f = sb("be2f", [128, D], F32)

        w1_t = load_weight_T(prog, W1, W1_sb[:], D, DFF)
        w2_t = load_weight_T(prog, W2, W2_sb[:], DFF, D)
        b1_t = prog.gp(lambda g: g.dma_start(out=b1_sb[:], in_=b1d), inc="ld", n=16)
        g1c_t = prog.gp(lambda g: g.dma_start(
            out=g1c[:].unsqueeze(2),
            in_=g1d.rearrange("one (k p) -> p k one", p=128),
            allow_slow_non_contiguous=True), inc="ld", n=16)
        be1c_t = prog.gp(lambda g: g.dma_start(
            out=be1c[:].unsqueeze(2),
            in_=be1d.rearrange("one (k p) -> p k one", p=128),
            allow_slow_non_contiguous=True), inc="ld", n=16)
        b2_t = prog.gp(lambda g: g.dma_start(out=g1f[0:1, :], in_=b2d), inc="ld", n=16)
        be1_t = prog.gp(lambda g: g.dma_start(out=be2f[0:1, :], in_=be1d), inc="ld", n=16)
        g1r_t = prog.gp(lambda g: g.dma_start(out=g2f[0:1, :], in_=g1d), inc="ld", n=16)
        g2r_t = prog.gp(lambda g: g.dma_start(out=u_acc[0:1, 0:D], in_=g2d), inc="ld", n=16)
        be2r_t = prog.gp(lambda g: g.dma_start(out=u_acc[0:1, D:2 * D], in_=be2d), inc="ld", n=16)
        b2p_tok = prog.ve(lambda v: v.tensor_tensor(b2p_sb[:], g1f[0:1, :], be2f[0:1, :], ALU.add),
                          waits=[prog.all_tok('ld')], inc="vs")

        xt32 = [sb("xt32_s", [128, NC * D], F32)] * 2
        xc32 = [sb(f"xc32_{i}", [128, NC * D], F32) for i in range(2)]
        xl1T = sb("xl1T", [128, 8 * TT], BF16)
        hT = sb("hT", [128, 32 * TT], BF16)
        u_acc = sb("u_acc", [128, NC * D], F32)
        bn6 = sb("bn6", [128, 48], F32)
        mv = sb("mv", [128, 2 * NC], F32)
        rs = sb("rs", [128, NC], F32)

        pp = [psal(f"pp{j}", [128, 512], F32) for j in range(2)]
        npps = [psal(f"np{j}", [128, 512], F32) for j in range(2)]
        tpf = [psal(f"tp{j}", [128, 512], F32) for j in range(2)]
        gpp = psal("gpp", [128, 512], F32)

        ones_row32 = C["ones_row32"]

        def replicate(dst, src_row, wtok, prev):
            prev = prev if isinstance(prev, (list, tuple)) else ([prev] if prev else [])
            toks = []
            for jj in range(2):
                w = [wtok, prog.all_tok('gs')] + list(prev) + ([toks[-1]] if toks else [])
                tok_mm = prog.pe(lambda t, jj=jj: t.matmul(
                    gpp[:], ones_row32[0:1, 0:128], src_row[:, jj * 512:(jj + 1) * 512],
                    start=True, stop=True), waits=w, inc="ps")
                toks.append(prog.ve(lambda v, jj=jj: v.tensor_copy(
                    dst[:, jj * 512:(jj + 1) * 512], gpp[:]), waits=[tok_mm], inc="vs"))
            return toks[-1]

        rep0 = replicate(be2f, u_acc[0:1, D:2 * D], be2r_t, [b2p_tok])
        rep1 = replicate(g1f, g2f[0:1, :], g1r_t, [b2p_tok, rep0])
        rep_done = replicate(g2f, u_acc[0:1, 0:D], g2r_t, [rep1])

        tp_tok = [None, None]
        stores = []
        xt_free = [None, None]
        xc_free = [None, None]
        u_free = rep_done
        for i in range(NT):
            b = i % 2
            tok_x = prog.sync(
                lambda sy, i=i, b=b: sy.dma_start(
                    out=xt32[b][:].rearrange("p (c d) -> p c d", c=NC),
                    in_=x_tile_in(x_in, i, nchunk=NC)),
                waits=[xt_free[b]], inc=f"lx{b}")
            tok_mv, tok_rs = emit_ln_tile(prog, xt32[b], NC, bn6, mv, rs,
                                          C["eps_ln"], waits=[tok_x])
            tok_xc = None
            for cc in range(NC):
                tok_xc = prog.ve(lambda v, b=b, cc=cc: v.tensor_scalar(
                    xc32[b][:, cc * D:(cc + 1) * D], xt32[b][:, cc * D:(cc + 1) * D],
                    mv[:, 2 * cc:2 * cc + 1], rs[:, cc:cc + 1],
                    ALU.subtract, ALU.mult),
                    waits=[tok_mv, tok_rs, xc_free[b]] if cc == 0 else (),
                    inc="vs" if cc == NC - 1 else None)
            xt_free[b] = tok_xc
            u0_toks = []
            for cc in range(NC):
                u0_toks.append(prog.ve(lambda v, b=b, cc=cc: v.tensor_tensor(
                    u_acc[:, cc * D:(cc + 1) * D], xc32[b][:, cc * D:(cc + 1) * D],
                    g1f[:], ALU.mult),
                    waits=[tok_xc, rep_done, u_free] if cc == 0 else (),
                    inc="vs" if cc == NC - 1 else None))
            tok_u0 = u0_toks[-1]
            ev_last = None
            for rnd in range(4):
                tb = rnd % 2
                w = [tok_xc] + ([tp_tok[tb]] if tp_tok[tb] else [])
                for j in range(4):
                    blk = rnd * 4 + j
                    db, cc = blk // NC, blk % NC
                    tok_tr = prog.pe(lambda t, tb=tb, j=j, db=db, cc=cc, b=b: t.transpose(
                        tpf[tb][:, j * 128:(j + 1) * 128],
                        xc32[b][:, cc * D + db * 128: cc * D + (db + 1) * 128], id32[:]),
                        waits=w if j == 0 else (), inc="ps" if j == 3 else None)
                for j in range(4):
                    blk = rnd * 4 + j
                    db, cc = blk // NC, blk % NC
                    ev_last = prog.ve(lambda v, tb=tb, j=j, db=db, cc=cc: v.tensor_scalar(
                        xl1T[:, db * TT + cc * 128: db * TT + (cc + 1) * 128],
                        tpf[tb][:, j * 128:(j + 1) * 128],
                        g1c[:, db:db + 1], be1c[:, db:db + 1], ALU.mult, ALU.add),
                        waits=[tok_tr, prog.all_tok('ld')] if j == 0 else (),
                        inc="vs" if j == 3 else None)
                tp_tok[tb] = ev_last
            xl1T_done = ev_last

            h_evs = []
            for ob in range(32):
                pb = ob % 2
                w0 = [xl1T_done, prog.all_tok('ld')] + ([h_evs[ob - 2]] if ob >= 2 else [])
                for kb in range(8):
                    prog.pe(lambda t, pb=pb, ob=ob, kb=kb: t.matmul(
                        pp[pb][:, 0:TT], W1_sb[:, kb * DFF + ob * 128: kb * DFF + (ob + 1) * 128],
                        xl1T[:, kb * TT:(kb + 1) * TT], start=(kb == 0), stop=False),
                        waits=w0 if kb == 0 else ())
                tok_mm = prog.pe(lambda t, pb=pb, ob=ob: t.matmul(
                    pp[pb][:, 0:TT], b1_sb[0:1, ob * 128:(ob + 1) * 128],
                    ones_row[0:1, 0:TT], start=False, stop=True), inc="ps")
                h_evs.append(prog.se(lambda s, pb=pb, ob=ob: s.activation(
                    hT[:, ob * TT:(ob + 1) * TT], pp[pb][:, 0:TT], AF.Gelu),
                    waits=[tok_mm], inc="ss"))
            hT_done = h_evs[-1]

            u_toks = []
            for cc in range(NC):
                for nb in range(2):
                    qb = (cc * 2 + nb) % 2
                    w0 = [hT_done, prog.all_tok('ld'), b2p_tok] + ([u_toks[-2]] if len(u_toks) >= 2 else [])
                    for ob in range(32):
                        prog.pe(lambda t, qb=qb, cc=cc, nb=nb, ob=ob: t.matmul(
                            npps[qb][:], hT[:, ob * TT + cc * 128: ob * TT + (cc + 1) * 128],
                            W2_sb[:, ob * D + nb * 512: ob * D + (nb + 1) * 512],
                            start=(ob == 0), stop=False), waits=w0 if ob == 0 else ())
                    tok_mm = prog.pe(lambda t, qb=qb, nb=nb: t.matmul(
                        npps[qb][:], ones_row[0:1, 0:128], b2p_sb[0:1, nb * 512:(nb + 1) * 512],
                        start=False, stop=True), inc="ps")
                    u_toks.append(prog.ve(lambda v, qb=qb, cc=cc, nb=nb, b=b: v.tensor_tensor(
                        xc32[b][:, cc * D + nb * 512: cc * D + (nb + 1) * 512],
                        npps[qb][:],
                        u_acc[:, cc * D + nb * 512: cc * D + (nb + 1) * 512], ALU.add),
                        waits=[tok_mm, tok_u0], inc="vs"))
            u_done = u_toks[-1]

            tok_mv2, tok_rs2 = emit_ln_tile(prog, xc32[b], NC, bn6, mv, rs,
                                            C["eps_ln"], waits=[u_done])
            a1 = None
            for cc in range(NC):
                a1 = prog.ve(lambda v, b=b, cc=cc: v.tensor_scalar(
                    u_acc[:, cc * D:(cc + 1) * D], xc32[b][:, cc * D:(cc + 1) * D],
                    mv[:, 2 * cc:2 * cc + 1], rs[:, cc:cc + 1],
                    ALU.subtract, ALU.mult),
                    waits=[tok_mv2, tok_rs2] if cc == 0 else (),
                    inc="vs" if cc == NC - 1 else None)
            a2s = []
            for cc in range(NC):
                a2s.append(prog.ve(lambda v, cc=cc, b=b: v.tensor_tensor(
                    xc32[b][:, cc * D:(cc + 1) * D], u_acc[:, cc * D:(cc + 1) * D],
                    g2f[:], ALU.mult), waits=[a1] if cc == 0 else (),
                    inc="vs" if cc == NC - 1 else None))
            a3s = []
            for cc in range(NC):
                a3s.append(prog.ve(lambda v, cc=cc, b=b: v.tensor_tensor(
                    u_acc[:, cc * D:(cc + 1) * D], xc32[b][:, cc * D:(cc + 1) * D],
                    be2f[:], ALU.add), waits=[a2s[-1]] if cc == 0 else (),
                    inc="vs" if cc == NC - 1 else None))
            a3 = a3s[-1]
            xc_free[b] = a3
            stores.append(prog.sync(
                lambda sy, i=i: sy.dma_start(
                    out=x_tile_in(x_out, i, nchunk=NC),
                    in_=u_acc[:].rearrange("p (c d) -> p c d", c=NC)),
                waits=[a3], inc="st"))
            u_free = stores[-1]
        prog.sync(lambda sy: sy.nop(), waits=[stores[-1]])
        _fin(nc, prog, stack)
    return nc


def build_phase_res_ln():
    nc = _mknc()
    x_in = _dram(nc, "x_in", [S, D], F32, "ExternalInput")
    res_in = _dram(nc, "res_in", [S, D], F32, "ExternalInput")
    x_out = _dram(nc, "x_out", [S, D], F32, "ExternalOutput")
    gfd = _dram(nc, "gf", [1, D], F32, "ExternalInput")
    befd = _dram(nc, "bef", [1, D], F32, "ExternalInput")
    prog = _mkprog()
    with ExitStack() as stack:
        sb = lambda n, s, d: stack.enter_context(nc.sbuf_tensor(_nm(n), s, d))
        psal = lambda n, s, d: stack.enter_context(nc.psum_tensor(_nm(n), s, d))
        C = declare_constants(nc, stack, prog, "ld", need=("ones_row32", "eps_ln"))
        ones_row32 = C["ones_row32"]
        xt = [sb(f"xt{i}", [128, 4 * D], F32) for i in range(2)]
        rt = [sb(f"rt{i}", [128, 4 * D], F32) for i in range(2)]
        ot = [sb(f"ot{i}", [128, 4 * D], F32) for i in range(2)]
        gff = sb("gff", [128, D], F32)
        beff = sb("beff", [128, D], F32)
        bn6 = sb("bn6", [128, 48], F32)
        mv = sb("mv", [128, 8], F32)
        rs = sb("rs", [128, 4], F32)
        gpp = psal("gpp", [128, 512], F32)
        g_t = prog.gp(lambda g: g.dma_start(out=gff[0:1, :], in_=gfd), inc="ld", n=16)
        b_t = prog.gp(lambda g: g.dma_start(out=beff[0:1, :], in_=befd), inc="ld", n=16)
        rtoks = []
        for (dst, srcdst) in ((beff, beff), (gff, gff)):
            for jj in range(2):
                w = [prog.all_tok("ld"), prog.all_tok("gs")] + ([rtoks[-1]] if rtoks else [])
                tok_mm = prog.pe(lambda t, jj=jj, srcdst=srcdst: t.matmul(
                    gpp[:], ones_row32[0:1, 0:128], srcdst[0:1, jj * 512:(jj + 1) * 512],
                    start=True, stop=True), waits=w, inc="ps")
                rtoks.append(prog.ve(lambda v, jj=jj, dst=dst: v.tensor_copy(
                    dst[:, jj * 512:(jj + 1) * 512], gpp[:]), waits=[tok_mm], inc="vs"))
        rep_done = rtoks[-1]
        comp, stores = [], []
        for i in range(8):
            b = i % 2
            lx = prog.sync(lambda sy, i=i, b=b: sy.dma_start(
                out=xt[b][:].rearrange("p (c d) -> p c d", c=4), in_=x_tile_in(x_in, i)),
                waits=[comp[i - 2]] if i >= 2 else [], inc=f"lx{b}")
            lr = prog.sync(lambda sy, i=i, b=b: sy.dma_start(
                out=rt[b][:].rearrange("p (c d) -> p c d", c=4), in_=x_tile_in(res_in, i)),
                inc=f"lx{b}")
            tok_mv, tok_rs = emit_ln_tile(prog, xt[b], 4, bn6, mv, rs, C["eps_ln"],
                                          waits=[lx, lr])
            ctok = None
            for c in range(4):
                t1 = prog.ve(lambda v, b=b, c=c: v.tensor_scalar(
                    ot[b][:, c * D:(c + 1) * D], xt[b][:, c * D:(c + 1) * D],
                    mv[:, 2 * c:2 * c + 1], rs[:, c:c + 1], ALU.subtract, ALU.mult),
                    waits=([tok_mv, tok_rs, rep_done] +
                           ([stores[i - 2]] if i >= 2 else [])) if c == 0 else ())
                t2 = prog.ve(lambda v, b=b, c=c: v.tensor_tensor(
                    ot[b][:, c * D:(c + 1) * D], ot[b][:, c * D:(c + 1) * D], gff[:],
                    ALU.mult))
                t3 = prog.ve(lambda v, b=b, c=c: v.tensor_tensor(
                    ot[b][:, c * D:(c + 1) * D], ot[b][:, c * D:(c + 1) * D], beff[:],
                    ALU.add))
                ctok = prog.ve(lambda v, b=b, c=c: v.tensor_tensor(
                    ot[b][:, c * D:(c + 1) * D], ot[b][:, c * D:(c + 1) * D],
                    rt[b][:, c * D:(c + 1) * D], ALU.add), inc="vs" if c == 3 else None)
            comp.append(ctok)
            stores.append(prog.sync(lambda sy, i=i, b=b: sy.dma_start(
                out=x_tile_in(x_out, i), in_=ot[b][:].rearrange("p (c d) -> p c d", c=4)),
                waits=[ctok], inc=f"st{b}"))
        prog.sync(lambda sy: sy.nop(), waits=[stores[-1]])
        _fin(nc, prog, stack)
    return nc


def build_phase_l0ffn_big():
    nc = _mknc()
    x_in = _dram(nc, "x_in", [S, D], F32, "ExternalInput")
    x_out = _dram(nc, "x_out", [S, D], F32, "ExternalOutput")
    w1d = _dram(nc, "w1_w", [DFF, D], F32, "ExternalInput")
    w2d = _dram(nc, "w2_w", [DFF, D], F32, "ExternalInput")
    h1d = _dram(nc, "h1_w", [DFF, DFF], F32, "ExternalInput")
    h2d = _dram(nc, "h2_w", [DFF, DFF], F32, "ExternalInput")
    w3d = _dram(nc, "w3_w", [D, DFF], F32, "ExternalInput")
    s1d = _dram(nc, "w1_s", [DFF], F32, "ExternalInput")
    s2d = _dram(nc, "w2_s", [DFF], F32, "ExternalInput")
    h1sd = _dram(nc, "h1_s", [DFF], F32, "ExternalInput")
    h2sd = _dram(nc, "h2_s", [DFF], F32, "ExternalInput")
    s3d = _dram(nc, "w3_s", [D], F32, "ExternalInput")
    w1bf = _dram(nc, "z_w1bf", [DFF, D], BF16, "Internal")
    w2bf = _dram(nc, "z_w2bf", [DFF, D], BF16, "Internal")
    h1bf = _dram(nc, "z_h1bf", [DFF, DFF], BF16, "Internal")
    h2bf = _dram(nc, "z_h2bf", [DFF, DFF], BF16, "Internal")
    w3bf = _dram(nc, "z_w3bf", [D, DFF], BF16, "Internal")
    gT_d = _dram(nc, "z_gT", [8, 128, 32, 512], BF16, "Internal")
    uT_d = _dram(nc, "z_uT", [8, 128, 32, 512], BF16, "Internal")
    u2T_d = _dram(nc, "z_u2T", [8, 128, 32, 512], BF16, "Internal")

    prog = _mkprog()
    with ExitStack() as stack:
        sb = lambda n, s, d: stack.enter_context(nc.sbuf_tensor(_nm(n), s, d))
        psal = lambda n, s, d: stack.enter_context(nc.psum_tensor(_nm(n), s, d))
        C = declare_constants(nc, stack, prog, "ld", need=("id_bf", "id32", "eps_ln"))
        id32, id_bf = C["id32"], C["id_bf"]

        pool = sb("pool", [128, 32768], BF16)
        stg = [sb(f"stg{j}", [128, 2048], BF16) for j in range(2)]
        ev1 = [sb(f"ev1_{j}", [128, 512], BF16) for j in range(2)]
        ev2 = [sb(f"ev2_{j}", [128, 512], BF16) for j in range(2)]
        xt32 = sb("xt32", [128, 4 * D], F32)
        ot32 = sb("ot32", [128, 4 * D], F32)
        xc = sb("xc", [128, 4 * D], BF16)
        xl0T = sb("xl0T", [128, 8 * 512], BF16)
        gslab = sb("gslab", [128, 32 * 512], BF16)
        scol = sb("scol", [128, 136], F32)
        sstg = sb("sstg", [32, 128], F32)
        bn6 = sb("bn6", [128, 48], F32)
        mv = sb("mv", [128, 8], F32)
        rs = sb("rs", [128, 4], F32)

        pp = [psal(f"pp{j}", [128, 512], F32) for j in range(4)]
        tpb2 = [psal(f"tpb{j}", [128, 1024], BF16) for j in range(2)]
        gpp = psal("gpp", [128, 512], F32)

        prev_ev = None
        for (sd, cnt, coff) in [(s1d, 32, 0), (s2d, 32, 32), (h1sd, 32, 64),
                                (h2sd, 32, 96), (s3d, 8, 128)]:
            lt = prog.sync(lambda sy, sd=sd, cnt=cnt: sy.dma_start(
                out=sstg[0:cnt, :], in_=sd.rearrange("(k p) -> k p", p=128)),
                waits=[prev_ev], inc="sx")
            tt = prog.pe(lambda t, cnt=cnt: t.transpose(
                gpp[:, 0:cnt], sstg[0:cnt, :], id32[0:cnt, 0:cnt]),
                waits=[lt, prog.all_tok("ld")], inc="ps")
            prev_ev = prog.ve(lambda v, cnt=cnt, coff=coff: v.tensor_copy(
                scol[:, coff:coff + cnt], gpp[:, 0:cnt]), waits=[tt], inc="vs")
        scol_done = prev_ev

        pw_last = [None, None]

        def prep(dram_in, dram_out, rows, cols, scale_coff=None):
            ncol = max(1, cols // 2048)
            ccw = min(cols, 2048)
            for rb in range(rows // 128):
                for cb in range(ncol):
                    j = (rb * ncol + cb) % 2
                    lt = prog.gp(lambda g, rb=rb, cb=cb, j=j, ccw=ccw: g.dma_start(
                        out=stg[j][:, 0:ccw],
                        in_=dram_in[rb * 128:(rb + 1) * 128, cb * ccw:(cb + 1) * ccw]),
                        waits=[pw_last[j]], inc=f"pl{j}", n=16)
                    if scale_coff is not None:
                        lt = prog.ve(lambda v, rb=rb, j=j, ccw=ccw: v.tensor_scalar(
                            stg[j][:, 0:ccw], stg[j][:, 0:ccw],
                            scol[:, scale_coff + rb:scale_coff + rb + 1], None,
                            ALU.mult), waits=[lt, scol_done], inc="vs")
                    pw_last[j] = prog.sync(lambda sy, rb=rb, cb=cb, j=j, ccw=ccw: sy.dma_start(
                        out=dram_out[rb * 128:(rb + 1) * 128, cb * ccw:(cb + 1) * ccw],
                        in_=stg[j][:, 0:ccw]), waits=[lt], inc=f"pw{j}")
            return pw_last[0], pw_last[1]

        w1p = prep(w1d, w1bf, DFF, D)
        w2p = prep(w2d, w2bf, DFF, D)
        h1p = prep(h1d, h1bf, DFF, DFF)
        h2p = prep(h2d, h2bf, DFF, DFF)
        w3p = prep(w3d, w3bf, D, DFF, scale_coff=128)
        prep_done = [w3p[0], w3p[1]]

        lastg = {}
        c1_done = None
        for oh in range(2):
            for (bfd, base) in [(w1bf, 0), (w2bf, 16384)]:
                for ib in range(8):
                    wv = []
                    if oh == 0 and base == 0 and ib == 0:
                        wv = [w2p[0], w2p[1]]
                    if oh == 1 and base == 0 and ib == 0:
                        wv = [lastg.get(("gst", 0)), lastg.get(("gst", 1))]
                    prog.sync(lambda sy, bfd=bfd, base=base, ib=ib, oh=oh: sy.dma_start(
                        out=pool[:, base + ib * 2048: base + (ib + 1) * 2048],
                        in_=bfd[oh * 2048:(oh + 1) * 2048, ib * 128:(ib + 1) * 128],
                        transpose=True), waits=wv, inc="tx")
            w12T_done = prog.cur_tok("tx")
            for i in range(8):
                lx = prog.sync(lambda sy, i=i: sy.dma_start(
                    out=xt32[:].rearrange("p (c d) -> p c d", c=4),
                    in_=x_tile_in(x_in, i)), waits=[lastg.get(("xfree",))], inc="lx0")
                tok_mv, tok_rs = emit_ln_tile(prog, xt32, 4, bn6, mv, rs, C["eps_ln"],
                                              waits=[lx])
                xc_tok = None
                for c in range(4):
                    xc_tok = prog.ve(lambda v, c=c: v.tensor_scalar(
                        xc[:, c * D:(c + 1) * D], xt32[:, c * D:(c + 1) * D],
                        mv[:, 2 * c:2 * c + 1], rs[:, c:c + 1], ALU.subtract, ALU.mult),
                        waits=[tok_mv, tok_rs, lastg.get(("xcfree",))] if c == 0 else (),
                        inc="vs" if c == 3 else None)
                lastg[("xfree",)] = xc_tok
                ev_last = None
                for rnd in range(4):
                    w = [xc_tok] + ([lastg.get(("tp", rnd % 2))] if lastg.get(("tp", rnd % 2)) else [])
                    tb = rnd % 2
                    for j in range(8):
                        blk = rnd * 8 + j
                        db, cc = blk // 4, blk % 4
                        tok_tr = prog.pe(lambda t, tb=tb, j=j, db=db, cc=cc: t.transpose(
                            tpb2[tb][:, (j % 8) * 128:(j % 8 + 1) * 128],
                            xc[:, cc * D + db * 128: cc * D + (db + 1) * 128], id_bf[:]),
                            waits=w if j == 0 else (), inc="ps" if j == 7 else None)
                    for j in range(8):
                        blk = rnd * 8 + j
                        db, cc = blk // 4, blk % 4
                        ev_last = prog.se(lambda s, tb=tb, j=j, db=db, cc=cc: s.copy(
                            xl0T[:, db * 512 + cc * 128: db * 512 + (cc + 1) * 128],
                            tpb2[tb][:, (j % 8) * 128:(j % 8 + 1) * 128]),
                            waits=[tok_tr] if j == 0 else (), inc="ss" if j == 7 else None)
                    lastg[("tp", rnd % 2)] = ev_last
                xl0T_done = ev_last
                lastg[("xcfree",)] = xl0T_done

                for ogl in range(16):
                    og = oh * 16 + ogl
                    pa, pb_ = (0, 1) if ogl % 2 == 0 else (2, 3)
                    e = ogl % 2
                    w0 = [xl0T_done, w12T_done] + \
                         ([lastg.get(("gev", pa))] if lastg.get(("gev", pa)) else [])
                    for ib in range(8):
                        tok_a = prog.pe(lambda t, pa=pa, ogl=ogl, ib=ib: t.matmul(
                            pp[pa][:], pool[:, ib * 2048 + ogl * 128: ib * 2048 + (ogl + 1) * 128],
                            xl0T[:, ib * 512:(ib + 1) * 512], start=(ib == 0), stop=(ib == 7)),
                            waits=w0 if ib == 0 else (), inc="ps" if ib == 7 else None)
                    for ib in range(8):
                        tok_b = prog.pe(lambda t, pb_=pb_, ogl=ogl, ib=ib: t.matmul(
                            pp[pb_][:], pool[:, 16384 + ib * 2048 + ogl * 128: 16384 + ib * 2048 + (ogl + 1) * 128],
                            xl0T[:, ib * 512:(ib + 1) * 512], start=(ib == 0), stop=(ib == 7)),
                            inc="ps" if ib == 7 else None)
                    sil = prog.se(lambda s, pa=pa, og=og, e=e: s.activation(
                        ev1[e][:], pp[pa][:], AF.Silu, scale=scol[:, og:og + 1]),
                        waits=[tok_a, scol_done], inc="ss")
                    gev = prog.ve(lambda v, pb_=pb_, og=og, e=e: v.scalar_tensor_tensor(
                        ev2[e][:], pp[pb_][:], scol[:, 32 + og:32 + og + 1],
                        ev1[e][:], ALU.mult, ALU.mult),
                        waits=[tok_b, sil, lastg.get(("gst", e))], inc="vs")
                    gst = prog.gp(lambda g, og=og, i=i, e=e: g.dma_start(
                        out=gT_d[i, :, og, :], in_=ev2[e][:]),
                        waits=[gev], inc=f"gw{e}", n=16)
                    lastg[("gev", pa)] = gst
                    lastg[("gst", e)] = gst
        c1_pair = [lastg[("gst", 0)], lastg[("gst", 1)]]
        c1_done = c1_pair

        def hlayer(src_d, dst_d, bfd, sc_off, wait_w, wait_src, pool_free):
            done = None
            out_last = {}
            for oq in range(4):
                stoks = []
                for hb in range(32):
                    wv = ([wait_w[0], wait_w[1]] + list(pool_free)) if (oq == 0 and hb == 0) else (done if (hb == 0 and done) else [])
                    stoks.append(prog.sync(lambda sy, bfd=bfd, oq=oq, hb=hb: sy.dma_start(
                        out=pool[:, hb * 1024:(hb + 1) * 1024],
                        in_=bfd[oq * 1024:(oq + 1) * 1024, hb * 128:(hb + 1) * 128],
                        transpose=True), waits=wv, inc="tx"))
                strips_done = prog.cur_tok("tx")
                for t8 in range(8):
                    lslab = prog.sync(lambda sy, t8=t8: sy.dma_start(
                        out=gslab[:].rearrange("p (h t) -> p h t", h=32),
                        in_=src_d[t8]),
                        waits=list(wait_src) + [out_last.get(("slabfree", t8 - 1, oq))], inc="lx0")
                    for og in range(8):
                        o = oq * 8 + og
                        pq = og % 4
                        w0 = [lslab, strips_done] +                              ([out_last.get((t8, og - 4))] if og >= 4 else
                              [out_last.get((t8 - 1, og + 4))])
                        for kb in range(32):
                            tok_mm = prog.pe(lambda t, pq=pq, og=og, kb=kb: t.matmul(
                                pp[pq][:], pool[:, kb * 1024 + og * 128: kb * 1024 + (og + 1) * 128],
                                gslab[:, kb * 512:(kb + 1) * 512],
                                start=(kb == 0), stop=(kb == 31)),
                                waits=w0 if kb == 0 else (), inc="ps" if kb == 31 else None)
                        e = og % 2
                        uev = prog.se(lambda s, pq=pq, o=o, e=e: s.activation(
                            ev1[e][:], pp[pq][:], AF.Gelu, scale=scol[:, sc_off + o:sc_off + o + 1]),
                            waits=[tok_mm, out_last.get(("ust", e, t8, og - 2)) if og >= 2
                                   else out_last.get(("ust", e, t8 - 1, 6 + e))], inc="ss")
                        ust = prog.gp(lambda g, o=o, t8=t8, e=e: g.dma_start(
                            out=dst_d[t8, :, o, :], in_=ev1[e][:]),
                            waits=[uev], inc=f"uw{e}", n=16)
                        out_last[(t8, og)] = ust
                        out_last[("ust", e, t8, og)] = ust
                    out_last[("slabfree", t8, oq)] = out_last[(t8, 7)]
                done = [out_last[("ust", 0, 7, 6)], out_last[("ust", 1, 7, 7)]]
            return done

        c2_done = hlayer(gT_d, uT_d, h1bf, 64, h1p, c1_pair, c1_pair)
        c3_done = hlayer(uT_d, u2T_d, h2bf, 96, h2p, c2_done, c2_done)

        w3T_toks = []
        for hb in range(32):
            w = [w3p[0], w3p[1]] + list(c3_done) if hb == 0 else []
            w3T_toks.append(prog.sync(lambda sy, hb=hb: sy.dma_start(
                out=pool[:, hb * 1024:(hb + 1) * 1024],
                in_=w3bf[:, hb * 128:(hb + 1) * 128], transpose=True),
                waits=w, inc="tx"))
        w3T_done = prog.cur_tok("tx")
        ostores = []
        prev = {}
        for i in range(8):
            lslab = prog.sync(lambda sy, i=i: sy.dma_start(
                out=gslab[:].rearrange("p (h t) -> p h t", h=32),
                in_=u2T_d[i]),
                waits=list(c3_done) + [prev.get(("sfree", i - 1))], inc="lx0")
            lx = prog.sync(lambda sy, i=i: sy.dma_start(
                out=xt32[:].rearrange("p (c d) -> p c d", c=4),
                in_=x_tile_in(x_in, i)), waits=[prev.get(("xfree", i - 1))] + c1_pair,
                inc="lx1")
            otok = None
            for cc in range(4):
                for nb in range(2):
                    pq = (cc * 2 + nb) % 4
                    w0 = [lslab, lx, w3T_done] +                          ([prev.get(("ev", cc * 2 + nb - 4))] if cc * 2 + nb >= 4 else
                          [prev.get(("pev", cc * 2 + nb + 4))])
                    for kb in range(32):
                        tok_mm = prog.pe(lambda t, pq=pq, cc=cc, nb=nb, kb=kb: t.matmul(
                            pp[pq][:], gslab[:, kb * 512 + cc * 128: kb * 512 + (cc + 1) * 128],
                            pool[:, kb * 1024 + nb * 512: kb * 1024 + (nb + 1) * 512],
                            start=(kb == 0), stop=(kb == 31)),
                            waits=w0 if kb == 0 else (), inc="ps" if kb == 31 else None)
                    otok = prog.ve(lambda v, cc=cc, nb=nb, pq=pq: v.tensor_tensor(
                        ot32[:, cc * D + nb * 512: cc * D + (nb + 1) * 512], pp[pq][:],
                        xt32[:, cc * D + nb * 512: cc * D + (nb + 1) * 512], ALU.add),
                        waits=[tok_mm] + ([prev.get(("ofree", i - 1))] if cc == 0 and nb == 0 else []),
                        inc="vs")
                    prev[("ev", cc * 2 + nb)] = otok
            for k in range(8):
                prev[("pev", k)] = prev.get(("ev", k))
            prev[("xfree", i)] = otok
            prev[("sfree", i)] = otok
            ostores.append(prog.sync(lambda sy, i=i: sy.dma_start(
                out=x_tile_in(x_out, i), in_=ot32[:].rearrange("p (c d) -> p c d", c=4)),
                waits=[otok], inc="st0"))
            prev[("ofree", i)] = ostores[-1]
        prog.sync(lambda sy: sy.nop(), waits=[ostores[-1]])
        _fin(nc, prog, stack)
    return nc


def build_phase_l1_att():
    nc = _mknc()
    x_in = _dram(nc, "x_in", [S, D], F32, "ExternalInput")
    x_out = _dram(nc, "x_out", [S, D], F32, "ExternalOutput")
    wqd = _dram(nc, "w_q", [D, 128], F32, "ExternalInput")
    wkd = _dram(nc, "w_k", [D, 128], F32, "ExternalInput")
    wvd = _dram(nc, "w_v", [D, 128], F32, "ExternalInput")
    wod = _dram(nc, "w_o", [128, D], F32, "ExternalInput")
    hmd = _dram(nc, "head_dim_mask", [128, 1], mybir.dt.int32, "ExternalInput")

    prog = _mkprog()
    with ExitStack() as stack:
        sb = lambda n, s, d: stack.enter_context(nc.sbuf_tensor(_nm(n), s, d))
        psal = lambda n, s, d: stack.enter_context(nc.psum_tensor(_nm(n), s, d))
        C = declare_constants(nc, stack, prog, "ld",
                              need=("id_bf", "ones_row", "ones_col", "eps_ln"))
        id_bf, ones_row, ones_col = C["id_bf"], C["ones_row"], C["ones_col"]

        wq_sb = sb("wq_sb", [128, 8 * 128], BF16)
        wk_sb = sb("wk_sb", [128, 8 * 128], BF16)
        wv_sb = sb("wv_sb", [128, 8 * 128], BF16)
        wo_sb = sb("wo_sb", [128, D], BF16)
        hm_sb = sb("hm_sb", [128, 1], F32)
        qT = sb("qT", [128, S], BF16)
        kT = sb("kT", [128, S], BF16)
        v_aug = sb("v_aug", [128, 32 * 129], BF16)
        AT = sb("AT", [128, 32 * 512], BF16)
        xt32 = [sb(f"xt32_{j}", [128, 4 * D], F32) for j in range(2)]
        ot32 = sb("ot32", [128, 4 * D], F32)
        xc = sb("xc", [128, 4 * D], BF16)
        xT = sb("xT", [128, 8 * 512], BF16)
        sq = sb("sq", [128, 512], BF16)
        qn = sb("qn", [1, S], F32)
        kn = sb("kn", [1, S], F32)
        kmx = sb("kmx", [1, 1], F32)
        rp_a = sb("rp_a", [128, 512], F32)
        yTs_a = sb("yTs_a", [128, 512], BF16)
        drbf_a = sb("drbf_a", [1, 512], BF16)
        qmxa = sb("qmxa", [1, 1], F32)
        cva = sb("cva", [1, 1], F32)
        cvba = sb("cvba", [1, 1], BF16)
        c_cola = sb("c_cola", [128, 1], F32)
        bn6 = sb("bn6", [128, 48], F32)
        mv = sb("mv", [128, 8], F32)
        rs = sb("rs", [128, 4], F32)

        pp = [psal(f"pp{j}", [128, 512], F32) for j in range(3)]
        tpb2 = [psal(f"tpb{j}", [128, 1024], BF16) for j in range(2)]
        npq = psal("npq", [128, 512], F32)
        yp_a = psal("ypa", [128, 512], F32)
        dp_a = psal("dpa", [128, 512], F32)

        for (dd_, sbuf) in ((wqd, wq_sb), (wkd, wk_sb), (wvd, wv_sb)):
            prog.gp(lambda g, dd_=dd_, sbuf=sbuf: g.dma_start(
                out=sbuf[:].rearrange("p (k n) -> p k n", k=8),
                in_=dd_.rearrange("(k p) n -> p k n", p=128)), inc="ld", n=16)
        prog.gp(lambda g: g.dma_start(out=wo_sb[:], in_=wod), inc="ld", n=16)
        prog.gp(lambda g: g.dma_start(out=hm_sb[:], in_=hmd), inc="ld", n=16)
        wo_m = prog.ve(lambda v: v.tensor_scalar(wo_sb[:], wo_sb[:], hm_sb[:], None,
                                                 ALU.mult),
                       waits=[prog.all_tok("ld")], inc="vs")

        tp_tok = [None, None]
        lastq = {}
        nprev = None
        for i in range(8):
            b = i % 2
            lx = prog.sync(lambda sy, i=i, b=b: sy.dma_start(
                out=xt32[b][:].rearrange("p (c d) -> p c d", c=4),
                in_=x_tile_in(x_in, i)), waits=[lastq.get(("xfree", i - 2))],
                inc=f"lx{b}")
            tok_mv, tok_rs = emit_ln_tile(prog, xt32[b], 4, bn6, mv, rs, C["eps_ln"],
                                          waits=[lx])
            xc_tok = None
            for c in range(4):
                xc_tok = prog.ve(lambda v, b=b, c=c: v.tensor_scalar(
                    xc[:, c * D:(c + 1) * D], xt32[b][:, c * D:(c + 1) * D],
                    mv[:, 2 * c:2 * c + 1], rs[:, c:c + 1], ALU.subtract, ALU.mult),
                    waits=[tok_mv, tok_rs, lastq.get(("xcfree", i - 1))] if c == 0 else (),
                    inc="vs" if c == 3 else None)
            lastq[("xfree", i)] = xc_tok
            ev_last = None
            for rnd in range(4):
                tb = rnd % 2
                w = [xc_tok] + ([tp_tok[tb]] if tp_tok[tb] else [])
                for j in range(8):
                    blk = rnd * 8 + j
                    db, cc = blk // 4, blk % 4
                    tok_tr = prog.pe(lambda t, tb=tb, j=j, db=db, cc=cc: t.transpose(
                        tpb2[tb][:, j * 128:(j + 1) * 128],
                        xc[:, cc * D + db * 128: cc * D + (db + 1) * 128], id_bf[:]),
                        waits=w if j == 0 else (), inc="ps" if j == 7 else None)
                for j in range(8):
                    blk = rnd * 8 + j
                    db, cc = blk // 4, blk % 4
                    ev_last = prog.se(lambda s, tb=tb, j=j, db=db, cc=cc: s.copy(
                        xT[:, db * 512 + cc * 128: db * 512 + (cc + 1) * 128],
                        tpb2[tb][:, j * 128:(j + 1) * 128]),
                        waits=[tok_tr] if j == 0 else (), inc="ss" if j == 7 else None)
                tp_tok[tb] = ev_last
            xT_done = ev_last
            lastq[("xcfree", i)] = xT_done
            for (wsb, outT, nrow, pq) in ((wq_sb, qT, qn, 0), (wk_sb, kT, kn, 1)):
                for kb in range(8):
                    tok_mm = prog.pe(lambda t, wsb=wsb, pq=pq, kb=kb: t.matmul(
                        pp[pq][:], wsb[:, kb * 128:(kb + 1) * 128],
                        xT[:, kb * 512:(kb + 1) * 512], start=(kb == 0), stop=(kb == 7)),
                        waits=[xT_done, prog.all_tok("ld"),
                               lastq.get(("pev", pq, i - 1))] if kb == 0 else (),
                        inc="ps" if kb == 7 else None)
                ev = prog.se(lambda s, outT=outT, pq=pq, i=i: s.copy(
                    outT[:, i * 512:(i + 1) * 512], pp[pq][:]), waits=[tok_mm], inc="ss")
                lastq[("pev", pq, i)] = ev
                sq_t = prog.se(lambda s, outT=outT, i=i: s.activation(
                    sq[:], outT[:, i * 512:(i + 1) * 512], AF.Square),
                    waits=[ev], inc="ss")
                nm = prog.pe(lambda t: t.matmul(
                    npq[0:1, 0:512], ones_col[:], sq[:], start=True, stop=True),
                    waits=[sq_t, nprev], inc="ps")
                nprev = prog.se(lambda s, nrow=nrow, i=i: s.activation(
                    nrow[0:1, i * 512:(i + 1) * 512], npq[0:1, 0:512], AF.Sqrt),
                    waits=[nm], inc="ss")
            for c in range(4):
                for kb in range(8):
                    tok_mm = prog.pe(lambda t, c=c, kb=kb: t.matmul(
                        pp[2][:, 0:128], xT[:, kb * 512 + c * 128: kb * 512 + (c + 1) * 128],
                        wv_sb[:, kb * 128:(kb + 1) * 128], start=(kb == 0), stop=(kb == 7)),
                        waits=[xT_done, lastq.get(("vev", i, c - 1)),
                               lastq.get(("vev", i - 1, 3))] if kb == 0 else (),
                        inc="ps" if kb == 7 else None)
                vev = prog.se(lambda s, i=i, c=c: s.copy(
                    v_aug[:, (i * 4 + c) * 129:(i * 4 + c) * 129 + 128], pp[2][:, 0:128]),
                    waits=[tok_mm], inc="ss")
                lastq[("vev", i, c)] = vev
        v_all = lastq[("vev", 7, 3)]
        SCL = 0.08838834764831845
        km1 = prog.ve(lambda v: v.reduce_max(kmx[:], kn[:], axis=AX.X),
                      waits=[nprev], inc="vs")
        qm1 = prog.ve(lambda v: v.reduce_max(qmxa[:], qn[:], axis=AX.X),
                      waits=[km1], inc="vs")
        cv_t = prog.ve(lambda v: v.tensor_scalar(
            cva[:], kmx[:], qmxa[0:1, 0:1], -1.01 * SCL, ALU.mult, ALU.mult),
            waits=[qm1], inc="vs")
        cvb_t = prog.ve(lambda v: v.tensor_copy(cvba[:], cva[:]), inc="vs")
        rep_c = prog.pe(lambda t: t.matmul(
            npq[0:128, 0:1], ones_row[0:1, 0:128], cvba[0:1, 0:1],
            start=True, stop=True),
            waits=[cvb_t, v_all, prog.all_tok("gs")], inc="ps")
        c_cpa = prog.ve(lambda v: v.tensor_copy(c_cola[:], npq[0:128, 0:1]),
                        waits=[rep_c], inc="vs")

        prevo = {}
        stores = []
        for ab in range(8):
            b = ab % 2
            lx = prog.sync(lambda sy, ab=ab, b=b: sy.dma_start(
                out=xt32[b][:].rearrange("p (c d) -> p c d", c=4),
                in_=x_tile_in(x_in, ab)),
                waits=[prevo.get(("xfree", ab - 2), lastq.get(("xfree", 6 + (ab % 2)))),
                       prog.cur_tok("vs")], inc=f"lx{b}")
            for kb in range(32):
                pq = kb % 2
                w0 = [prevo.get(("expb", pq))] + ([c_cpa] if (ab == 0 and kb == 0) else [])
                mm1 = prog.pe(lambda t, pq=pq, kb=kb, ab=ab: t.matmul(
                    pp[pq][:], kT[:, kb * 128:(kb + 1) * 128],
                    qT[:, ab * 512:(ab + 1) * 512], start=True, stop=True),
                    waits=w0, inc="ps")
                ex = prog.se(lambda s, pq=pq, kb=kb: s.activation(
                    AT[:, kb * 512:(kb + 1) * 512], pp[pq][:], AF.Exp,
                    bias=c_cola[:], scale=SCL),
                    waits=[mm1] + ([c_cpa, prevo.get("dnl")] if kb == 0 else []),
                    inc="ss")
                prevo[("expb", pq)] = ex
                prevo[("expk", kb)] = ex
            avt = None
            for kb in range(32):
                w0 = [prevo[("expk", kb)]] + ([prevo.get("yevp")] if kb == 0 else [])
                avt = prog.pe(lambda t, kb=kb: t.matmul(
                    yp_a[:, 0:512], v_aug[:, kb * 129: kb * 129 + 128],
                    AT[:, kb * 512:(kb + 1) * 512], start=(kb == 0), stop=(kb == 31)),
                    waits=w0, inc="ps" if kb == 31 else None)
            dn = None
            for kb in range(32):
                w0 = [prevo.get(("oev", 0)), prog.all_tok("gs")] if kb == 0 else ()
                dn = prog.pe(lambda t, kb=kb: t.matmul(
                    dp_a[0:1, 0:512], ones_col[:, 0:1],
                    AT[:, kb * 512:(kb + 1) * 512], start=(kb == 0), stop=(kb == 31)),
                    waits=w0, inc="ps" if kb == 31 else None)
            prevo["dnl"] = dn
            d1 = prog.ve(lambda v: v.tensor_scalar(
                drbf_a[0:1, :], dp_a[0:1, 0:512], 1e-30, None, ALU.add),
                waits=[dn, avt], inc="vs")
            rep2 = prog.pe(lambda t: t.matmul(
                npq[:, 0:512], ones_row[0:1, 0:128], drbf_a[0:1, 0:512],
                start=True, stop=True), waits=[d1, c_cpa], inc="ps")
            rp_cp = prog.ve(lambda v: v.reciprocal(rp_a[:], npq[:, 0:512]),
                            waits=[rep2], inc="vs")
            prevo["rp_cp"] = rp_cp
            yev = prog.ve(lambda v: v.scalar_tensor_tensor(
                yTs_a[:], yp_a[:, 0:512], 1.0, rp_a[:], ALU.mult, ALU.mult),
                waits=[rp_cp, prevo.get("wolast")], inc="vs")
            prevo["yevp"] = yev
            oev = None
            for asub in range(4):
                for nb in range(2):
                    par2 = (asub * 2 + nb) % 2
                    bank = dp_a if par2 == 0 else pp[2]
                    mo = prog.pe(lambda t, asub=asub, nb=nb, bank=bank: t.matmul(
                        bank[:, 0:512], yTs_a[:, asub * 128:(asub + 1) * 128],
                        wo_sb[:, nb * 512:(nb + 1) * 512], start=True, stop=True),
                        waits=[yev, wo_m, prevo.get(("oev", par2))] +
                              ([d1] if (asub == 0 and nb == 0) else []),
                        inc="ps")
                    oev = prog.ve(lambda v, asub=asub, nb=nb, b=b, bank=bank: v.tensor_tensor(
                        ot32[:, asub * D + nb * 512: asub * D + (nb + 1) * 512],
                        bank[:, 0:512],
                        xt32[b][:, asub * D + nb * 512: asub * D + (nb + 1) * 512],
                        ALU.add), waits=[mo, lx] +
                        ([prevo.get(("ofree", ab - 1))] if asub == 0 and nb == 0 else []),
                        inc="vs")
                    prevo[("oev", par2)] = oev
            prevo["wolast"] = oev
            prevo[("xfree", ab)] = oev
            stores.append(prog.sync(lambda sy, ab=ab: sy.dma_start(
                out=x_tile_in(x_out, ab), in_=ot32[:].rearrange("p (c d) -> p c d", c=4)),
                waits=[oev], inc=f"st{b}"))
            prevo[("ofree", ab)] = stores[-1]
        prog.sync(lambda sy: sy.nop(), waits=[stores[-1], stores[-2]])
        _fin(nc, prog, stack)
    return nc


def build_phase_l1_msa():
    nc = _mknc()
    x_in = _dram(nc, "x_in", [S, D], F32, "ExternalInput")
    x2i_o = _dram(nc, "x2i", [S, D], F32, "ExternalOutput")
    s_o = _dram(nc, "s_out", [S, D], F32, "ExternalOutput")
    wqkv = _dram(nc, "w_qkv", [D, 3 * D], F32, "ExternalInput")
    wproj = _dram(nc, "w_proj", [D, D], F32, "ExternalInput")
    ln1w = _dram(nc, "ln1_w", [1, D], F32, "ExternalInput")
    ln1b = _dram(nc, "ln1_b", [1, D], F32, "ExternalInput")
    t0_d = _dram(nc, "z_t0", [S, D], F32, "Internal")
    t1T_d = _dram(nc, "z_t1T", [8, 128, 8, 512], BF16, "Internal")
    yT_d = _dram(nc, "z_yT", [8, 128, 8, 512], BF16, "Internal")

    prog = _mkprog()
    with ExitStack() as stack:
        sb = lambda n, s, d: stack.enter_context(nc.sbuf_tensor(_nm(n), s, d))
        psal = lambda n, s, d: stack.enter_context(nc.psum_tensor(_nm(n), s, d))
        C = declare_constants(nc, stack, prog, "ld",
                              need=("id_bf", "ones_row", "ones_row32", "ones_col",
                                    "eps_ln"))
        id_bf, ones_row, ones_col = C["id_bf"], C["ones_row"], C["ones_col"]
        ones_row32 = C["ones_row32"]

        qTg = sb("qTg", [128, 2 * S], BF16)
        kTg = sb("kTg", [128, 2 * S], BF16)
        vag = sb("vag", [128, 32 * 260], BF16)
        AT = sb("AT", [128, 32 * 512], BF16)
        wqg = sb("wqg", [128, 8 * 768], BF16)
        wpj = sb("wpj", [128, 8 * D], BF16)
        slab = sb("slab", [128, 8 * 512], BF16)
        sqg = sb("sqg", [128, 4096], BF16)
        knrow = sb("knrow", [1, 512], F32)
        kmx = sb("kmx", [1, 1], F32)
        kmr = sb("kmr", [1, 8], F32)
        xt32 = sb("xt32", [128, 4 * D], F32)
        t0b = sb("t0b", [128, 4 * D], F32)
        t1b = sb("t1b", [128, 4 * D], BF16)
        g1f = sb("g1f", [128, D], F32)
        b1f = sb("b1f", [128, D], F32)
        yT_sb = sb("yT_sb", [128, 2 * 512], BF16)
        drow_bf = sb("drow_bf", [1, 512], BF16)
        qmr = sb("qmr", [1, 8], F32)
        qmx = sb("qmx", [1, 1], F32)
        cv = sb("cv", [1, 1], F32)
        cvb = sb("cvb", [1, 1], BF16)
        c_col = sb("c_col", [128, 1], F32)
        x2b = sb("x2b", [128, D], F32)
        s_b = sb("s_b", [128, D], F32)
        t0c = sb("t0c", [128, D], F32)
        x4c = sb("x4c", [128, D], F32)
        bn6 = sb("bn6", [128, 48], F32)
        mv = sb("mv", [128, 8], F32)
        rs = sb("rs", [128, 4], F32)
        rp64 = t0b[0:64, 1536:2048]

        pp = [psal(f"pp{j}", [128, 512], F32) for j in range(4)]
        yp = [psal(f"yp{j}", [128, 512], F32) for j in range(2)]
        tpb = psal("tpb0", [128, 1024], BF16)
        npq = psal("npq", [128, 512], F32)

        lw_t = prog.gp(lambda g: g.dma_start(out=g1f[0:1, :], in_=ln1w), inc="ld", n=16)
        lb_t = prog.gp(lambda g: g.dma_start(out=b1f[0:1, :], in_=ln1b), inc="ld", n=16)
        reptoks = []
        for (dst,) in ((b1f,), (g1f,)):
            for jj in range(2):
                w = [prog.all_tok("ld"), prog.all_tok("gs")] + ([reptoks[-1]] if reptoks else [])
                tok_mm = prog.pe(lambda t, jj=jj, dst=dst: t.matmul(
                    npq[:], ones_row32[0:1, 0:128], dst[0:1, jj * 512:(jj + 1) * 512],
                    start=True, stop=True), waits=w, inc="ps")
                reptoks.append(prog.ve(lambda v, jj=jj, dst=dst: v.tensor_copy(
                    dst[:, jj * 512:(jj + 1) * 512], npq[:]), waits=[tok_mm], inc="vs"))
        rep_done = reptoks[-1]
        ones_aug = prog.gp(lambda g: g.memset(
            vag[:].rearrange("p (c h x) -> p (c h) x", c=32, h=4)[:, :, 64:65], 1.0),
            inc="gs", n=1)
        wp_t = prog.gp(lambda g_: g_.dma_start(
            out=wpj[:].rearrange("p (k n) -> p k n", k=8),
            in_=wproj.rearrange("(k p) n -> p k n", p=128)), inc="wp", n=16)

        tp_tok = [None]
        pv = {}
        for i in range(8):
            lx = prog.sync(lambda sy, i=i: sy.dma_start(
                out=xt32[:].rearrange("p (c d) -> p c d", c=4),
                in_=x_tile_in(x_in, i)), waits=[pv.get(("xfree",))], inc="lx0")
            tok_mv, tok_rs = emit_ln_tile(prog, xt32, 4, bn6, mv, rs, C["eps_ln"],
                                          waits=[lx])
            t0_tok = None
            for c in range(4):
                t0_tok = prog.ve(lambda v, c=c: v.tensor_scalar(
                    t0b[:, c * D:(c + 1) * D], xt32[:, c * D:(c + 1) * D],
                    mv[:, 2 * c:2 * c + 1], rs[:, c:c + 1], ALU.subtract, ALU.mult),
                    waits=[tok_mv, tok_rs, pv.get(("t0free",)),
                           pv.get(("st0prev",))] if c == 0 else (),
                    inc="vs" if c == 3 else None)
            st0 = prog.sync(lambda sy, i=i: sy.dma_start(
                out=x_tile_in(t0_d, i), in_=t0b[:].rearrange("p (c d) -> p c d", c=4)),
                waits=[t0_tok], inc="st0")
            pv[("st0prev",)] = st0
            tok_mv2, tok_rs2 = emit_ln_tile(prog, t0b, 4, bn6, mv, rs, C["eps_ln"],
                                            waits=[t0_tok])
            t1_tok = None
            for c in range(4):
                a1 = prog.ve(lambda v, c=c: v.tensor_scalar(
                    xt32[:, c * D:(c + 1) * D], t0b[:, c * D:(c + 1) * D],
                    mv[:, 2 * c:2 * c + 1], rs[:, c:c + 1], ALU.subtract, ALU.mult),
                    waits=[tok_mv2, tok_rs2, rep_done] if c == 0 else (), inc="vs")
                a2 = prog.ve(lambda v, c=c: v.tensor_tensor(
                    xt32[:, c * D:(c + 1) * D], xt32[:, c * D:(c + 1) * D], g1f[:],
                    ALU.mult), waits=[a1], inc="vs")
                t1_tok = prog.ve(lambda v, c=c: v.scalar_tensor_tensor(
                    t1b[:, c * D:(c + 1) * D], xt32[:, c * D:(c + 1) * D], 0.0,
                    b1f[:], ALU.bypass, ALU.add), waits=[a2], inc="vs")
            pv[("t0free",)] = t1_tok
            pv[("xfree",)] = t1_tok
            ev_last = None
            for rnd in range(4):
                w = [t1_tok] + ([tp_tok[0]] if tp_tok[0] else [])
                for j in range(8):
                    blk = rnd * 8 + j
                    db, cc = blk // 4, blk % 4
                    tok_tr = prog.pe(lambda t, j=j, db=db, cc=cc: t.transpose(
                        tpb[:, j * 128:(j + 1) * 128],
                        t1b[:, cc * D + db * 128: cc * D + (db + 1) * 128], id_bf[:]),
                        waits=w if j == 0 else (), inc="ps" if j == 7 else None)
                for j in range(8):
                    blk = rnd * 8 + j
                    db, cc = blk // 4, blk % 4
                    ev_last = prog.ve(lambda v, j=j, db=db, cc=cc: v.tensor_copy(
                        slab[:, db * 512 + cc * 128: db * 512 + (cc + 1) * 128],
                        tpb[:, j * 128:(j + 1) * 128]),
                        waits=[tok_tr, pv.get(("slabfree",))] if j == 0 else (),
                        inc="vs" if j == 7 else None)
                tp_tok[0] = ev_last
            st1 = prog.sync(lambda sy, i=i: sy.dma_start(
                out=t1T_d[i], in_=slab[:].rearrange("p (k t) -> p k t", k=8)),
                waits=[ev_last], inc="st1")
            pv[("slabfree",)] = st1
        pre_done = [pv[("slabfree",)], prog.cur_tok("st0")]

        SCL = 0.125
        att = {}
        qkv_ev = {}
        qkv_done = []
        for g in range(4):
            prev_qkv = qkv_done if g > 0 else []
            for ti in range(3):
                wq_t = prog.gp(lambda g_, g=g, ti=ti: g_.dma_start(
                    out=wqg[:].rearrange("p (k three hh) -> p k three hh", three=3,
                                         hh=256)[:, :, ti, :],
                    in_=wqkv.rearrange("(k p) (three n) -> p k three n", p=128,
                                       three=3)[:, :, ti, g * 256:(g + 1) * 256]),
                    waits=list(prev_qkv) if ti == 0 else [], inc="wq", n=16)
            qkv_ev[("slabfree", -1)] = qkv_ev.get(("slabfree", 7))
            for t8 in range(8):
                lsl = prog.sync(lambda sy, t8=t8: sy.dma_start(
                    out=slab[:].rearrange("p (k t) -> p k t", k=8), in_=t1T_d[t8]),
                    waits=pre_done + (qkv_ev.get(("slabfree", t8 - 1)) or []), inc="lx0")
                for (which, outT, base, pqo) in (("q", qTg, 0, 0), ("k", kTg, 256, 2)):
                    for pb in range(2):
                        pq = pqo + pb
                        for kb in range(8):
                            tok_mm = prog.pe(lambda t, pq=pq, base=base, pb=pb, kb=kb: t.matmul(
                                pp[pq][:], wqg[:, kb * 768 + base + pb * 128: kb * 768 + base + (pb + 1) * 128],
                                slab[:, kb * 512:(kb + 1) * 512], start=(kb == 0), stop=(kb == 7)),
                                waits=[lsl, wq_t, qkv_ev.get((which, pb, t8 - 1)),
                                       att.get(("expb", pq))] if kb == 0 else (),
                                inc="ps" if kb == 7 else None)
                        ev = prog.se(lambda s, outT=outT, pq=pq, pb=pb, t8=t8: s.copy(
                            outT[:, pb * S + t8 * 512:(pb * S) + (t8 + 1) * 512], pp[pq][:]),
                            waits=[tok_mm], inc="ss")
                        qkv_ev[(which, pb, t8)] = ev
                for c in range(4):
                    pq = c % 2
                    for kb in range(8):
                        tok_mm = prog.pe(lambda t, c=c, kb=kb, pq=pq: t.matmul(
                            pp[pq][:, 0:256], slab[:, kb * 512 + c * 128: kb * 512 + (c + 1) * 128],
                            wqg[:, kb * 768 + 512:(kb + 1) * 768], start=(kb == 0), stop=(kb == 7)),
                            waits=[lsl, qkv_ev.get(("q", pq, t8)),
                                   qkv_ev.get(("v", t8, c - 2),
                                               qkv_ev.get(("v", t8 - 1, 2 + (c % 2))))] if kb == 0 else (),
                            inc="ps" if kb == 7 else None)
                    vev = prog.se(lambda s, t8=t8, c=c, pq=pq: s.copy(
                        vag[:].rearrange("p (ch h x) -> p ch h x", ch=32, h=4)
                        [:, t8 * 4 + c, :, 0:64],
                        pp[pq][:, 0:256].rearrange("p (h x) -> p h x", x=64)),
                        waits=[tok_mm], inc="ss")
                    qkv_ev[("v", t8, c)] = vev
                qkv_ev[("slabfree", t8)] = [qkv_ev[("v", t8, 3)], qkv_ev[("v", t8, 2)],
                                            qkv_ev[("q", 1, t8)], qkv_ev[("k", 1, t8)],
                                            qkv_ev[("q", 0, t8)], qkv_ev[("k", 0, t8)]]
            qkv_done = [qkv_ev[("q", 1, 7)], qkv_ev[("k", 1, 7)], qkv_ev[("v", 7, 3)]]

            for j in range(4):
                hrow = (j % 2) * 64
                hcol = (j // 2) * S
                db = g * 2 + j // 2
                nprev = None
                sqk = prog.se(lambda s, hrow=hrow, hcol=hcol: s.activation(
                    sqg[64:128, :], kTg[hrow:hrow + 64, hcol:hcol + S], AF.Square),
                    waits=qkv_done + [att.get("nm_last")], inc="ss")
                for t8 in range(8):
                    nm = prog.pe(lambda t, t8=t8: t.matmul(
                        npq[0:1, 0:512], ones_col[64:128, 0:1],
                        sqg[64:128, t8 * 512:(t8 + 1) * 512], start=True, stop=True),
                        waits=[sqk, nprev, att.get("rp_cp")] if t8 == 0 else [sqk, nprev],
                        inc="ps")
                    sq_ev = prog.se(lambda s: s.activation(
                        knrow[0:1, :], npq[0:1, 0:512], AF.Sqrt), waits=[nm], inc="ss")
                    nprev = prog.ve(lambda v, t8=t8: v.reduce_max(
                        kmr[0:1, t8:t8 + 1], knrow[0:1, :], axis=AX.X),
                        waits=[sq_ev], inc="vs")
                km1 = prog.ve(lambda v: v.reduce_max(kmx[:], kmr[:], axis=AX.X),
                              waits=[nprev], inc="vs")
                sqq = prog.se(lambda s, hrow=hrow, hcol=hcol: s.activation(
                    sqg[0:64, :], qTg[hrow:hrow + 64, hcol:hcol + S], AF.Square),
                    waits=qkv_done, inc="ss")
                nprev2 = None
                nm_last = None
                for t8 in range(8):
                    nm_last = prog.pe(lambda t, t8=t8: t.matmul(
                        npq[0:1, 0:512], ones_col[0:64, 0:1],
                        sqg[0:64, t8 * 512:(t8 + 1) * 512], start=True, stop=True),
                        waits=[sqq, km1, nprev2], inc="ps")
                    sq_ev = prog.se(lambda s: s.activation(
                        knrow[0:1, :], npq[0:1, 0:512], AF.Sqrt), waits=[nm_last], inc="ss")
                    nprev2 = prog.ve(lambda v, t8=t8: v.reduce_max(
                        qmr[0:1, t8:t8 + 1], knrow[0:1, :], axis=AX.X),
                        waits=[sq_ev], inc="vs")
                att["nm_last"] = nm_last
                qm1 = prog.ve(lambda v: v.reduce_max(qmx[:], qmr[:], axis=AX.X),
                              waits=[nprev2], inc="vs")
                cv_t = prog.ve(lambda v: v.tensor_scalar(
                    cv[:], kmx[:], qmx[0:1, 0:1], -1.01 * SCL, ALU.mult, ALU.mult),
                    waits=[qm1], inc="vs")
                cvb_t = prog.ve(lambda v: v.tensor_copy(cvb[:], cv[:]), inc="vs")
                rep_c = prog.pe(lambda t: t.matmul(
                    npq[0:128, 0:1], ones_row[0:1, 0:128], cvb[0:1, 0:1],
                    start=True, stop=True),
                    waits=[cvb_t, att.get("rp_cp"), prog.all_tok("gs")], inc="ps")
                c_cp = prog.ve(lambda v: v.tensor_copy(c_col[:], npq[0:128, 0:1]),
                               waits=[rep_c], inc="vs")

                for ab in range(8):
                    par = ab % 2
                    for kb in range(32):
                        pq = kb % 2
                        w0 = [att.get(("expb", pq))]
                        mm = prog.pe(lambda t, pq=pq, kb=kb, ab=ab, hrow=hrow, hcol=hcol: t.matmul(
                            pp[pq][:], kTg[hrow:hrow + 64, hcol + kb * 128: hcol + (kb + 1) * 128],
                            qTg[hrow:hrow + 64, hcol + ab * 512: hcol + (ab + 1) * 512],
                            start=True, stop=True), waits=w0, inc="ps")
                        ex = prog.se(lambda s, pq=pq, kb=kb: s.activation(
                            AT[:, kb * 512:(kb + 1) * 512], pp[pq][:], AF.Exp,
                            bias=c_col[:], scale=SCL),
                            waits=[mm] + ([c_cp, att.get("avdone")] if kb == 0 else []),
                            inc="ss")
                        att[("expb", pq)] = ex
                        att[("expk", kb)] = ex
                    avt = None
                    for kb in range(32):
                        w0 = [att[("expk", kb)]] + \
                            ([att.get(("yev", par)), prog.all_tok("gs")] if kb == 0 else [])
                        avt = prog.pe(lambda t, kb=kb, par=par, j=j: t.matmul(
                            yp[par][0:65, 0:512],
                            vag[:, kb * 260 + j * 65: kb * 260 + (j + 1) * 65],
                            AT[:, kb * 512:(kb + 1) * 512],
                            start=(kb == 0), stop=(kb == 31)),
                            waits=w0, inc="ps" if kb == 31 else None)
                    att["avdone"] = avt
                    d1 = prog.ve(lambda v, par=par: v.tensor_scalar(
                        drow_bf[0:1, :], yp[par][64:65, 0:512], 1e-30, None, ALU.add),
                        waits=[avt], inc="vs")
                    rep2 = prog.pe(lambda t: t.matmul(
                        npq[0:64, 0:512], ones_row[0:1, 0:64], drow_bf[0:1, 0:512],
                        start=True, stop=True), waits=[d1, c_cp], inc="ps")
                    rp_cp = prog.ve(lambda v: v.reciprocal(rp64, npq[0:64, 0:512]),
                                    waits=[rep2], inc="vs")
                    yev = prog.ve(lambda v, par=par: v.scalar_tensor_tensor(
                        yT_sb[0:64, par * 512:(par + 1) * 512], yp[par][0:64, 0:512],
                        1.0, rp64, ALU.mult, ALU.mult),
                        waits=[rp_cp, att.get(("yst", par))], inc="vs")
                    yst = prog.gp(lambda g_, db=db, j=j, ab=ab, par=par: g_.dma_start(
                        out=yT_d[db, (j % 2) * 64:(j % 2) * 64 + 64, ab, :],
                        in_=yT_sb[0:64, par * 512:(par + 1) * 512]),
                        waits=[yev], inc="yw", n=16)
                    att[("yev", par)] = yev
                    att[("yst", par)] = yst
                    att["rp_cp"] = rp_cp

        prevc = {}
        stores = []
        for ab in range(8):
            parb = ab % 2
            ysl = slab if parb == 0 else sqg
            lyT = prog.sync(lambda sy, ab=ab, ysl=ysl: sy.dma_start(
                out=ysl[:].rearrange("p (k t) -> p k t", k=8),
                in_=yT_d[:, :, ab, :].rearrange("db p t -> p db t")),
                waits=([prog.all_tok("yw")] if ab < 2 else [prevc[("sfree", ab - 2)]]),
                inc="lx0")
            mo = None
            for ct4 in range(4):
                ch = ab * 4 + ct4
                lt0 = prog.sync(lambda sy, ch=ch: sy.dma_start(
                    out=t0c[:], in_=t0_d[ch * 128:(ch + 1) * 128, :]),
                    waits=[prevc.get(("t0free", ch - 1))], inc="lx1")
                lx4 = prog.gp(lambda g_, ch=ch: g_.dma_start(
                    out=x4c[:], in_=x_in[ch * 128:(ch + 1) * 128, :]),
                    waits=[prevc.get(("x4free", ch - 1))], inc="lx2", n=16)
                for nb in range(2):
                    pq = nb
                    for kb in range(8):
                        mo = prog.pe(lambda t, pq=pq, nb=nb, kb=kb, ct4=ct4, ysl=ysl: t.matmul(
                            pp[pq][:], ysl[:, kb * 512 + ct4 * 128: kb * 512 + ct4 * 128 + 128],
                            wpj[:, kb * D + nb * 512: kb * D + (nb + 1) * 512],
                            start=(kb == 0), stop=(kb == 7)),
                            waits=[lyT, prog.all_tok("wp"), prevc.get(("oev", pq)),
                                   att.get(("expb", pq))] if kb == 0 else (),
                            inc="ps" if kb == 7 else None)
                    x2t = prog.ve(lambda v, pq=pq, nb=nb: v.tensor_tensor(
                        x2b[:, nb * 512:(nb + 1) * 512], pp[pq][:],
                        t0c[:, nb * 512:(nb + 1) * 512], ALU.add),
                        waits=[mo, lt0] + ([prevc.get(("x2free", ch - 1))] if nb == 0 else []),
                        inc="vs")
                    stok = prog.ve(lambda v, nb=nb: v.tensor_tensor(
                        s_b[:, nb * 512:(nb + 1) * 512], x2b[:, nb * 512:(nb + 1) * 512],
                        x4c[:, nb * 512:(nb + 1) * 512], ALU.add),
                        waits=[lx4] + ([prevc.get(("sbfree", ch - 1))] if nb == 0 else []),
                        inc="vs")
                    prevc[("oev", pq)] = stok
                st1 = prog.sync(lambda sy, ch=ch: sy.dma_start(
                    out=x2i_o[ch * 128:(ch + 1) * 128, :], in_=x2b[:]),
                    waits=[prevc[("oev", 1)]], inc=f"so{ch % 2}")
                st2 = prog.sync(lambda sy, ch=ch: sy.dma_start(
                    out=s_o[ch * 128:(ch + 1) * 128, :], in_=s_b[:]),
                    waits=[prevc[("oev", 1)]], inc=f"sq{ch % 2}")
                prevc[("t0free", ch)] = prevc[("oev", 0)]
                prevc[("x4free", ch)] = prevc[("oev", 1)]
                prevc[("x2free", ch)] = st1
                prevc[("sbfree", ch)] = st2
                stores.append(st2)
            prevc[("sfree", ab)] = mo
        prog.sync(lambda sy: sy.nop(),
                  waits=[prog.cur_tok("so0"), prog.cur_tok("so1"),
                         prog.cur_tok("sq0"), prog.cur_tok("sq1")])
        _fin(nc, prog, stack)
    return nc


def build_phase_l1_moe():
    nc = _mknc()
    x2i_i = _dram(nc, "x2i", [S, D], F32, "ExternalInput")
    s_i = _dram(nc, "s_in", [S, D], F32, "ExternalInput")
    x_out = _dram(nc, "x_out", [S, D], F32, "ExternalOutput")
    gwd = _dram(nc, "gate_w", [D, 8], F32, "ExternalInput")
    gbd = _dram(nc, "gate_b", [1, 8], F32, "ExternalInput")
    w1d = _dram(nc, "w1", [8, D, DFF], F32, "ExternalInput")
    w2d = _dram(nc, "w2", [8, DFF, D], F32, "ExternalInput")
    ln2w = _dram(nc, "ln2_w", [1, D], F32, "ExternalInput")
    ln2b = _dram(nc, "ln2_b", [1, D], F32, "ExternalInput")
    w1bf = _dram(nc, "z_w1bf", [8, D, DFF], BF16, "Internal")
    w2bf = _dram(nc, "z_w2bf", [8, DFF, D], BF16, "Internal")
    t2T_d = _dram(nc, "z_t2T", [8, 128, 8, 512], BF16, "Internal")

    prog = _mkprog()
    with ExitStack() as stack:
        sb = lambda n, s, d: stack.enter_context(nc.sbuf_tensor(_nm(n), s, d))
        psal = lambda n, s, d: stack.enter_context(nc.psum_tensor(_nm(n), s, d))
        C = declare_constants(nc, stack, prog, "ld",
                              need=("id_bf", "id32", "ones_row", "ones_row32",
                                    "eps_ln"))
        id_bf, id32 = C["id_bf"], C["id32"]
        ones_row, ones_row32 = C["ones_row"], C["ones_row32"]

        y0 = sb("y0", [128, 8 * D], F32)
        t2q = sb("t2q", [128, 8192], BF16)
        pool = sb("poolm", [128, 32768], BF16)
        w1h = pool[:, 0:16384]
        w2h = pool[:, 16384:32768]
        t2b = pool[:, 0:4096]
        slab = pool[:, 4096:8192]
        hTh = sb("hTh", [128, 16 * D], BF16)
        stg = [sb(f"stg{j}", [128, 2048], BF16) for j in range(2)]
        mask_sb = sb("mask_sb", [128, 32 * 8], F32)
        lgs = sb("lgs", [128, 32], F32)
        lgm = sb("lgm", [128, 1], F32)
        gw_f = sb("gw_f", [128, 8 * 8], F32)
        gw_hi = sb("gw_hi", [128, 8 * 8], BF16)
        gw_lo = sb("gw_lo", [128, 8 * 8], BF16)
        gw_tmp = sb("gw_tmp", [128, 8 * 8], F32)
        t2l = sb("t2l", [128, 4 * D], BF16)
        gb_sb = sb("gb_sb", [1, 8], BF16)
        gb_f = sb("gb_f", [1, 8], F32)
        gb_lo = sb("gb_lo", [1, 8], BF16)
        gb_tmp = sb("gb_tmp", [1, 8], F32)
        g2f = sb("g2f", [128, D], F32)
        b2f = sb("b2f", [128, D], F32)
        xt32 = sb("xt32", [128, 4 * D], F32)
        lgrow = sb("lgrow", [8, 512], F32)
        bn6 = sb("bn6", [128, 48], F32)
        mv = sb("mv", [128, 8], F32)
        rs = sb("rs", [128, 4], F32)

        pp = [psal(f"pp{j}", [128, 512], F32) for j in range(4)]
        tpb2 = [psal(f"tpb{j}", [128, 1024], BF16) for j in range(2)]
        npq = psal("npq", [128, 512], F32)
        gps = psal("gps", [128, 512], F32)

        gw_t0 = prog.gp(lambda g: g.dma_start(
            out=gw_f[:].rearrange("p (k n) -> p k n", k=8),
            in_=gwd.rearrange("(k p) n -> p k n", p=128)), inc="ld", n=16)
        gh1 = prog.ve(lambda v: v.tensor_copy(gw_hi[:], gw_f[:]), waits=[prog.all_tok("ld")], inc="vs")
        gh2 = prog.ve(lambda v: v.tensor_tensor(gw_tmp[:], gw_f[:], gw_hi[:],
                                                ALU.subtract), waits=[gh1], inc="vs")
        gw_t = prog.ve(lambda v: v.tensor_copy(gw_lo[:], gw_tmp[:]), waits=[gh2], inc="vs")
        gb_t = prog.gp(lambda g: g.dma_start(out=gb_f[:], in_=gbd), inc="ld", n=16)
        gb1 = prog.ve(lambda v: v.tensor_copy(gb_sb[:], gb_f[:]),
                      waits=[prog.all_tok("ld")], inc="vs")
        gb2 = prog.ve(lambda v: v.tensor_tensor(gb_tmp[:], gb_f[:], gb_sb[:],
                                                ALU.subtract), waits=[gb1], inc="vs")
        gb3 = prog.ve(lambda v: v.tensor_copy(gb_lo[:], gb_tmp[:]), waits=[gb2], inc="vs")
        l2w_t = prog.gp(lambda g: g.dma_start(out=g2f[0:1, :], in_=ln2w), inc="ld", n=16)
        l2b_t = prog.gp(lambda g: g.dma_start(out=b2f[0:1, :], in_=ln2b), inc="ld", n=16)
        reptoks = []
        for dst in (b2f, g2f):
            for jj in range(2):
                w = [prog.all_tok("ld"), prog.all_tok("gs")] + ([reptoks[-1]] if reptoks else [])
                tok_mm = prog.pe(lambda t, jj=jj, dst=dst: t.matmul(
                    npq[:], ones_row32[0:1, 0:128], dst[0:1, jj * 512:(jj + 1) * 512],
                    start=True, stop=True), waits=w, inc="ps")
                reptoks.append(prog.ve(lambda v, jj=jj, dst=dst: v.tensor_copy(
                    dst[:, jj * 512:(jj + 1) * 512], npq[:]), waits=[tok_mm], inc="vs"))
        rep_done = reptoks[-1]

        pw_last = [None, None]

        def prep(din, dout, e, rows, cols):
            ncol = max(1, cols // 2048)
            ccw = min(cols, 2048)
            for rb in range(rows // 128):
                for cb in range(ncol):
                    j = (rb * ncol + cb) % 2
                    lt = prog.gp(lambda g, rb=rb, cb=cb, j=j, ccw=ccw, e=e: g.dma_start(
                        out=stg[j][:, 0:ccw],
                        in_=din[e][rb * 128:(rb + 1) * 128, cb * ccw:(cb + 1) * ccw]),
                        waits=[pw_last[j]], inc=f"pl{j}", n=16)
                    pw_last[j] = prog.sync(lambda sy, rb=rb, cb=cb, j=j, ccw=ccw, e=e: sy.dma_start(
                        out=dout[e][rb * 128:(rb + 1) * 128, cb * ccw:(cb + 1) * ccw],
                        in_=stg[j][:, 0:ccw]), waits=[lt], inc=f"pw{j}")
            return list(pw_last)

        for e in range(8):
            prep(w1d, w1bf, e, D, DFF)
            prep_done = prep(w2d, w2bf, e, DFF, D)

        tp_tok = [None, None]
        pv = {}
        for i in range(8):
            lx = prog.sync(lambda sy, i=i: sy.dma_start(
                out=xt32[:].rearrange("p (c d) -> p c d", c=4),
                in_=x_tile_in(x2i_i, i)), waits=[pv.get(("xfree",))], inc="lx0")
            tok_mv, tok_rs = emit_ln_tile(prog, xt32, 4, bn6, mv, rs, C["eps_ln"],
                                          waits=[lx])
            t2_tok = None
            for c in range(4):
                a1 = prog.ve(lambda v, c=c: v.tensor_scalar(
                    xt32[:, c * D:(c + 1) * D], xt32[:, c * D:(c + 1) * D],
                    mv[:, 2 * c:2 * c + 1], rs[:, c:c + 1], ALU.subtract, ALU.mult),
                    waits=[tok_mv, tok_rs, rep_done] if c == 0 else (), inc="vs")
                a2 = prog.ve(lambda v, c=c: v.tensor_tensor(
                    xt32[:, c * D:(c + 1) * D], xt32[:, c * D:(c + 1) * D], g2f[:],
                    ALU.mult), waits=[a1], inc="vs")
                a3 = prog.ve(lambda v, c=c: v.scalar_tensor_tensor(
                    xt32[:, c * D:(c + 1) * D], xt32[:, c * D:(c + 1) * D], 0.0,
                    b2f[:], ALU.bypass, ALU.add), waits=[a2], inc="vs")
                a4 = prog.ve(lambda v, c=c: v.tensor_copy(
                    t2b[:, c * D:(c + 1) * D], xt32[:, c * D:(c + 1) * D]),
                    waits=[a3, pv.get(("t2bfree",))] if c == 0 else [a3], inc="vs")
                t2_tok = prog.ve(lambda v, c=c: v.tensor_tensor(
                    t2l[:, c * D:(c + 1) * D], xt32[:, c * D:(c + 1) * D],
                    t2b[:, c * D:(c + 1) * D], ALU.subtract), waits=[a4], inc="vs")
            pv[("xfree",)] = t2_tok
            ev_last = None
            for rnd in range(4):
                tb = rnd % 2
                w = [t2_tok] + ([tp_tok[tb]] if tp_tok[tb] else [])
                for j in range(8):
                    blk = rnd * 8 + j
                    db, cc = blk // 4, blk % 4
                    tok_tr = prog.pe(lambda t, tb=tb, j=j, db=db, cc=cc: t.transpose(
                        tpb2[tb][:, j * 128:(j + 1) * 128],
                        t2b[:, cc * D + db * 128: cc * D + (db + 1) * 128], id_bf[:]),
                        waits=w if j == 0 else (), inc="ps" if j == 7 else None)
                for j in range(8):
                    blk = rnd * 8 + j
                    db, cc = blk // 4, blk % 4
                    ev_last = prog.ve(lambda v, tb=tb, j=j, db=db, cc=cc: v.tensor_copy(
                        slab[:, db * 512 + cc * 128: db * 512 + (cc + 1) * 128],
                        tpb2[tb][:, j * 128:(j + 1) * 128]),
                        waits=[tok_tr, pv.get(("slabfree",))] if j == 0 else (),
                        inc="vs" if j == 7 else None)
                tp_tok[tb] = ev_last
            st1 = prog.sync(lambda sy, i=i: sy.dma_start(
                out=t2T_d[i], in_=slab.rearrange("p (k t) -> p k t", k=8)),
                waits=[ev_last], inc="st1")
            for kb in range(8):
                prog.pe(lambda t, kb=kb: t.matmul(
                    gps[0:8, 0:512], gw_hi[:, kb * 8:(kb + 1) * 8],
                    slab[:, kb * 512:(kb + 1) * 512], start=(kb == 0), stop=False),
                    waits=[ev_last, gw_t, pv.get(("gpsfree",))] if kb == 0 else ())
            for kb in range(8):
                glt_hi = prog.pe(lambda t, kb=kb: t.matmul(
                    gps[0:8, 0:512], gw_lo[:, kb * 8:(kb + 1) * 8],
                    slab[:, kb * 512:(kb + 1) * 512], start=False, stop=False),
                    inc="ps" if kb == 7 else None)
            evlo = None
            for rnd in range(4):
                tb = rnd % 2
                w = [t2_tok, st1, glt_hi] + ([tp_tok[tb]] if tp_tok[tb] else [])
                for j in range(8):
                    blk = rnd * 8 + j
                    db, cc = blk // 4, blk % 4
                    tok_tr = prog.pe(lambda t, tb=tb, j=j, db=db, cc=cc: t.transpose(
                        tpb2[tb][:, j * 128:(j + 1) * 128],
                        t2l[:, cc * D + db * 128: cc * D + (db + 1) * 128], id_bf[:]),
                        waits=w if j == 0 else (), inc="ps" if j == 7 else None)
                for j in range(8):
                    blk = rnd * 8 + j
                    db, cc = blk // 4, blk % 4
                    evlo = prog.ve(lambda v, tb=tb, j=j, db=db, cc=cc: v.tensor_copy(
                        slab[:, db * 512 + cc * 128: db * 512 + (cc + 1) * 128],
                        tpb2[tb][:, j * 128:(j + 1) * 128]),
                        waits=[tok_tr] if j == 0 else (), inc="vs" if j == 7 else None)
                tp_tok[tb] = evlo
            pv[("t2bfree",)] = evlo
            for kb in range(8):
                prog.pe(lambda t, kb=kb: t.matmul(
                    gps[0:8, 0:512], gw_hi[:, kb * 8:(kb + 1) * 8],
                    slab[:, kb * 512:(kb + 1) * 512], start=False, stop=False),
                    waits=[evlo] if kb == 0 else ())
            prog.pe(lambda t: t.matmul(
                gps[0:8, 0:512], gb_sb[0:1, :], ones_row[0:1, :],
                start=False, stop=False), waits=[gb_t if False else None])
            glt = prog.pe(lambda t: t.matmul(
                gps[0:8, 0:512], gb_lo[0:1, :], ones_row[0:1, :],
                start=False, stop=True), waits=[gb3], inc="ps")
            pv[("slabfree",)] = glt
            gle = prog.se(lambda s: s.copy(lgrow[:], gps[0:8, 0:512]),
                          waits=[glt, pv.get(("lgrowfree",))], inc="ss")
            for c in range(4):
                ttr = prog.pe(lambda t, c=c: t.transpose(
                    npq[0:128, c * 8:(c + 1) * 8], lgrow[:, c * 128:(c + 1) * 128],
                    id32[0:8, 0:8]), waits=[gle, pv.get(("lgfree",))] if c == 0 else (),
                    inc="ps" if c == 3 else None)
            lge = prog.ve(lambda v: v.tensor_copy(lgs[:], npq[0:128, 0:32]),
                          waits=[ttr], inc="vs")
            pv[("gpsfree",)] = lge
            mtoks = []
            for c in range(4):
                mm1 = prog.ve(lambda v, c=c: v.reduce_max(
                    lgm[:], lgs[:, c * 8:(c + 1) * 8], axis=AX.X),
                    waits=[lge] if c == 0 else [mtoks[-1]], inc="vs")
                mtoks.append(prog.ve(lambda v, c=c, i=i: v.tensor_scalar(
                    mask_sb[:, (i * 4 + c) * 8:(i * 4 + c + 1) * 8],
                    lgs[:, c * 8:(c + 1) * 8], lgm[:], None, ALU.is_equal),
                    waits=[mm1], inc="vs"))
            pv[("lgfree",)] = mtoks[-1]
            pv[("lgrowfree",)] = ttr
        tiles_done = [pv[("slabfree",)], pv[("lgfree",)]]

        mm_prev = {}
        ostores = []
        for q in range(4):
            l2a = prog.sync(lambda sy, q=q: sy.dma_start(
                out=t2q[:, 0:4096].rearrange("p (k t) -> p k t", k=8),
                in_=t2T_d[2 * q]), waits=tiles_done + [mm_prev.get(("t2free", q - 1))],
                inc="lx0")
            l2b_ = prog.sync(lambda sy, q=q: sy.dma_start(
                out=t2q[:, 4096:8192].rearrange("p (k t) -> p k t", k=8),
                in_=t2T_d[2 * q + 1]), inc="lx0")
            contrib = 0
            for e in range(8):
                for hh in range(2):
                    par = contrib % 2
                    lw1 = prog.gp(lambda g, e=e, hh=hh, q=q: g.dma_start(
                        out=w1h.rearrange("p (k n) -> p k n", k=8),
                        in_=w1bf[e].rearrange("(k p) n -> p k n", p=128)
                        [:, :, hh * 2048:(hh + 1) * 2048]),
                        waits=([prog.all_tok("pw0"), prog.all_tok("pw1")] + tiles_done
                               if (q == 0 and e == 0 and hh == 0) else
                               [mm_prev.get(("w1free",))]), inc="lw1", n=16)
                    lw2 = prog.gp(lambda g, e=e, hh=hh: g.dma_start(
                        out=w2h.rearrange("p (k n) -> p k n", k=16),
                        in_=w2bf[e].rearrange("(k p) n -> p k n", p=128)
                        [:, hh * 16:(hh + 1) * 16, :]),
                        waits=(tiles_done if (q == 0 and e == 0 and hh == 0) else []) +
                        [mm_prev.get(("w2free",))], inc="lw2", n=16)
                    hev = None
                    for ob in range(16):
                        for tb in range(2):
                            pq = (ob * 2 + tb) % 2
                            w0 = [lw1, l2b_] + ([mm_prev.get(("hev", pq))] if mm_prev.get(("hev", pq)) else [])
                            for kb in range(8):
                                tok_mm = prog.pe(lambda t, pq=pq, ob=ob, tb=tb, kb=kb, hh=hh: t.matmul(
                                    pp[pq][:],
                                    w1h[:, kb * 2048 + ob * 128: kb * 2048 + (ob + 1) * 128],
                                    t2q[:, kb * 512 + tb * 4096: kb * 512 + tb * 4096 + 512]
                                    if False else
                                    t2q[:, tb * 4096 + kb * 512: tb * 4096 + (kb + 1) * 512],
                                    start=(kb == 0), stop=(kb == 7)),
                                    waits=w0 if kb == 0 else (), inc="ps" if kb == 7 else None)
                            hev = prog.se(lambda s, pq=pq, ob=ob, tb=tb: s.activation(
                                hTh[:, ob * D + tb * 512: ob * D + (tb + 1) * 512],
                                pp[pq][:], AF.Gelu_apprx_tanh),
                                waits=[tok_mm], inc="ss")
                            mm_prev[("hev", pq)] = hev
                    mm_prev[("w1free",)] = hev
                    yev = None
                    for tsub in range(8):
                        for nb in range(2):
                            pq = 2 + (tsub * 2 + nb) % 2
                            w0 = [hev, lw2] + ([mm_prev.get(("yev", pq))] if mm_prev.get(("yev", pq)) else [])
                            for kb in range(16):
                                tok_mm = prog.pe(lambda t, pq=pq, tsub=tsub, nb=nb, kb=kb: t.matmul(
                                    pp[pq][:],
                                    hTh[:, kb * D + tsub * 128: kb * D + (tsub + 1) * 128],
                                    w2h[:, kb * D + nb * 512: kb * D + (nb + 1) * 512],
                                    start=(kb == 0), stop=(kb == 15)),
                                    waits=w0 if kb == 0 else (), inc="ps" if kb == 15 else None)
                            yev = prog.ve(lambda v, pq=pq, q=q, tsub=tsub, nb=nb,
                                          e=e, contrib=contrib: (
                                v.tensor_scalar(
                                    y0[:, tsub * D + nb * 512: tsub * D + (nb + 1) * 512],
                                    pp[pq][:],
                                    mask_sb[:, (q * 8 + tsub) * 8 + e:
                                            (q * 8 + tsub) * 8 + e + 1],
                                    None, ALU.mult)
                                if contrib == 0 else
                                v.scalar_tensor_tensor(
                                    y0[:, tsub * D + nb * 512: tsub * D + (nb + 1) * 512],
                                    pp[pq][:],
                                    mask_sb[:, (q * 8 + tsub) * 8 + e:
                                            (q * 8 + tsub) * 8 + e + 1],
                                    y0[:, tsub * D + nb * 512: tsub * D + (nb + 1) * 512],
                                    ALU.mult, ALU.add)), waits=[tok_mm], inc="vs")
                            mm_prev[("yev", pq)] = yev
                    mm_prev[("w2free",)] = yev
                    contrib += 1
            so_prev = None
            for half in range(2):
                lsd = prog.sync(lambda sy, q=q, half=half: sy.dma_start(
                    out=xt32[:].rearrange("p (c d) -> p c d", c=4),
                    in_=x_tile_in(s_i, 2 * q + half)),
                    waits=[mm_prev.get(("sfree", q - 1, 1)),
                           yev if half == 0 else so_prev],
                    inc=f"lx{1 + half}")
                fin = None
                for c in range(4):
                    fin = prog.ve(lambda v, half=half, c=c: v.tensor_tensor(
                        xt32[:, c * D:(c + 1) * D],
                        y0[:, (half * 4 + c) * D:(half * 4 + c + 1) * D],
                        xt32[:, c * D:(c + 1) * D], ALU.add),
                        waits=[lsd, yev] if c == 0 else [], inc="vs")
                so1 = prog.sync(lambda sy, q=q, half=half: sy.dma_start(
                    out=x_tile_in(x_out, 2 * q + half),
                    in_=xt32[:].rearrange("p (c d) -> p c d", c=4)),
                    waits=[fin], inc=f"so{half}")
                ostores.append(so1)
                so_prev = so1
                mm_prev[("sfree", q, half)] = so1
            mm_prev[("t2free", q)] = yev
        prog.sync(lambda sy: sy.nop(), waits=ostores[-2:])
        _fin(nc, prog, stack)
    return nc


def build_fused():
    nc = bass.Bass("TRN2", target_bir_lowering=False)
    prog = Prog()
    with ExitStack() as stack:
        d = {}
        d["c_ident"] = nc.dram_tensor("c_ident", [128, 128], F32, kind="ExternalInput").ap()
        d["c_mask4"] = nc.dram_tensor("c_mask4", [128, 512], F32, kind="ExternalInput").ap()
        w = {}
        w["x"] = nc.dram_tensor("L0_attn_x", [S, D], F32, kind="ExternalInput").ap()
        for nm, sh in [("Wq", [4, D, D]), ("Wk", [4, D, D]), ("Wv", [4, D, D]),
                       ("Wo", [4, D, D]), ("bq", [4, D]), ("bk", [4, D]),
                       ("bv", [4, D]), ("bo", [4, D]), ("W1", [4, D, DFF]),
                       ("b1", [4, DFF]), ("W2", [4, DFF, D]), ("b2", [4, D]),
                       ("g1", [4, D]), ("be1", [4, D]), ("g2", [4, D]),
                       ("be2", [4, D]), ("gf", [D]), ("bef", [D])]:
            w[nm] = nc.dram_tensor(f"L0_attn_{nm}", sh, F32, kind="ExternalInput").ap()
        for nm, sh in [("w1_w", [DFF, D]), ("w1_s", [DFF]), ("w2_w", [DFF, D]),
                       ("w2_s", [DFF]), ("h1_w", [DFF, DFF]), ("h1_s", [DFF]),
                       ("h2_w", [DFF, DFF]), ("h2_s", [DFF]), ("w3_w", [D, DFF]),
                       ("w3_s", [D])]:
            w[nm] = nc.dram_tensor(f"L0_ffn_{nm}", sh, F32, kind="ExternalInput").ap()
        for nm, sh, dt in [("w_q", [D, 128], F32), ("w_k", [D, 128], F32),
                           ("w_v", [D, 128], F32), ("w_o", [128, D], F32),
                           ("head_dim_mask", [128], mybir.dt.int32)]:
            w[nm] = nc.dram_tensor(f"L1_attn_{nm}", sh, dt, kind="ExternalInput").ap()
        for nm, sh in [("ln1_w", [D]), ("ln1_b", [D]), ("w_qkv", [D, 3 * D]),
                       ("w_proj", [D, D]), ("ln2_w", [D]), ("ln2_b", [D]),
                       ("gate_w", [D, 8]), ("gate_b", [8]),
                       ("w1", [8, D, DFF]), ("w2", [8, DFF, D])]:
            w[nm] = nc.dram_tensor(f"L1_ffn_{nm}", sh, F32, kind="ExternalInput").ap()
        out_ap = nc.dram_tensor("out", [S, D], F32, kind="ExternalOutput").ap()
        xa = nc.dram_tensor("zz_xa", [S, D], F32, kind="Internal").ap()
        xb_ = nc.dram_tensor("zz_xb", [S, D], F32, kind="Internal").ap()
        x1_ = nc.dram_tensor("zz_x1", [S, D], F32, kind="Internal").ap()
        x2_ = nc.dram_tensor("zz_x2", [S, D], F32, kind="Internal").ap()
        x3_ = nc.dram_tensor("zz_x3", [S, D], F32, kind="Internal").ap()
        x2i_ = nc.dram_tensor("zz_x2i", [S, D], F32, kind="Internal").ap()
        s_ = nc.dram_tensor("zz_s", [S, D], F32, kind="Internal").ap()

        _FUSED.update(on=True, nc=nc, prog=prog, ns="c_", dram=d, C=None)
        C = declare_constants(nc, stack, prog, "ld")
        _FUSED["C"] = C

        def call(ns, fn, dram):
            _FUSED["ns"] = ns
            _FUSED["dram"] = dram
            fn()

        call("p0_", build_phase_init, {"x_in": w["x"], "x_out": xa})
        cur, nxt = xa, xb_
        for l in range(4):
            dm = {"x_in": cur, "x_out": nxt,
                  "Wq": w["Wq"][l], "Wk": w["Wk"][l], "Wv": w["Wv"][l],
                  "Wo": w["Wo"][l],
                  "bq": w["bq"][l:l + 1, :], "bk": w["bk"][l:l + 1, :],
                  "bv": w["bv"][l:l + 1, :], "bo": w["bo"][l:l + 1, :]}
            call(f"a{l}_", lambda l=l: build_phase_l0_att(l), dm)
            cur, nxt = nxt, cur
            dm = {"x_in": cur, "x_out": nxt,
                  "W1": w["W1"][l], "W2": w["W2"][l],
                  "b1": w["b1"][l:l + 1, :], "b2": w["b2"][l:l + 1, :],
                  "g1": w["g1"][l:l + 1, :], "be1": w["be1"][l:l + 1, :],
                  "g2": w["g2"][l:l + 1, :], "be2": w["be2"][l:l + 1, :]}
            call(f"f{l}_", lambda l=l: build_phase_l0_ffn(l), dm)
            cur, nxt = nxt, cur
        call("r_", build_phase_res_ln,
             {"x_in": cur, "res_in": w["x"], "x_out": x1_,
              "gf": w["gf"].unsqueeze(0), "bef": w["bef"].unsqueeze(0)})
        dm = {"x_in": x1_, "x_out": x2_}
        for nm in ("w1_w", "w1_s", "w2_w", "w2_s", "h1_w", "h1_s", "h2_w",
                   "h2_s", "w3_w", "w3_s"):
            dm[nm] = w[nm]
        call("g_", build_phase_l0ffn_big, dm)
        call("t_", build_phase_l1_att,
             {"x_in": x2_, "x_out": x3_, "w_q": w["w_q"], "w_k": w["w_k"],
              "w_v": w["w_v"], "w_o": w["w_o"],
              "head_dim_mask": w["head_dim_mask"].unsqueeze(1)})
        call("m_", build_phase_l1_msa,
             {"x_in": x3_, "x2i": x2i_, "s_out": s_, "w_qkv": w["w_qkv"],
              "w_proj": w["w_proj"], "ln1_w": w["ln1_w"].unsqueeze(0),
              "ln1_b": w["ln1_b"].unsqueeze(0)})
        call("e_", build_phase_l1_moe,
             {"x2i": x2i_, "s_in": s_, "x_out": out_ap,
              "gate_w": w["gate_w"], "gate_b": w["gate_b"].unsqueeze(0),
              "w1": w["w1"], "w2": w["w2"],
              "ln2_w": w["ln2_w"].unsqueeze(0), "ln2_b": w["ln2_b"].unsqueeze(0)})
        _FUSED.update(on=False, nc=None, prog=None, dram={}, C=None)
        realize(nc, prog, stack)
    return nc


_CONSTS = host_constants()
_CACHE = {}


def _get(name, builder):
    if name not in _CACHE:
        _CACHE[name] = builder()
    return _CACHE[name]


def _run(nc, shared, per_core):
    maps = [dict(shared, **pc) for pc in per_core]
    res = run_bass_kernel_spmd(nc, maps, core_ids=list(range(8)))
    return res.results


def kernel(**inputs):
    inp = {k: np.ascontiguousarray(v) for k, v in inputs.items()}
    x = inp["L0_attn_x"]
    nc = _get("fused", build_fused)
    shared = dict(_CONSTS)
    for k, v in inp.items():
        if k != "L0_attn_x":
            shared[k] = v
    maps = [dict(shared, L0_attn_x=x[b]) for b in range(8)]
    res = run_bass_kernel_spmd(nc, maps, core_ids=list(range(8)))
    out = np.stack([res.results[b]["out"] for b in range(8)], axis=0)
    return out.astype(np.float32)
